# version 03
# baseline (speedup 1.0000x reference)
import math
import numpy as np
import ml_dtypes
import concourse.bass as bass
import concourse.mybir as mybir
from concourse.bass_utils import run_bass_kernel_spmd

F32 = mybir.dt.float32
BF16 = mybir.dt.bfloat16
AF = mybir.ActivationFunctionType
ALU = mybir.AluOpType
AX = mybir.AxisListType

ENGINES = ("tensor", "vector", "scalar", "gpsimd", "sync")
DMA_QUEUES = ("sync", "gpsimd", "scalar")

T = 4096
D = 1024
NT = T // 128
ND = D // 128
NB = T // 512
LN_EPS = 1e-5
DBG = False
DBG2 = False


class Buf:
    __slots__ = ("name", "writers", "readers", "const")

    def __init__(self, name="", const=False):
        self.name = name
        self.writers = []
        self.readers = []
        self.const = const


class _Op:
    __slots__ = ("idx", "eng", "fn", "deps", "dma", "needs_inc", "sem", "val", "tag",
                 "small")

    def __init__(self, idx, eng, fn, dma):
        self.tag = ""
        self.small = False
        self.idx = idx
        self.eng = eng
        self.fn = fn
        self.dma = dma
        self.deps = {}
        self.needs_inc = False
        self.sem = None
        self.val = None


class Prog:
    N_DMA_SEMS = 20
    N_EXEC_SEMS = 4

    def __init__(self, nc):
        self.nc = nc
        self.ops = []
        self.eng_ops = {e: [] for e in ENGINES}
        from collections import deque
        self.recent_dma = {e: deque(maxlen=self.N_DMA_SEMS) for e in ENGINES}

    def op(self, eng, fn, reads=(), scalar_reads=(), writes=(), dma=False):
        o = _Op(len(self.ops), eng, fn, dma)
        for b in reads:
            for w in b.writers:
                o.deps.setdefault(w, False)
        for b in scalar_reads:
            for w in b.writers:
                o.deps[w] = True
        for b in writes:
            for w in b.writers:
                o.deps.setdefault(w, False)
            for r in b.readers:
                o.deps.setdefault(r, False)
        o.deps.pop(o.idx, None)
        self.ops.append(o)
        self.eng_ops[eng].append(o)
        if dma:
            self.recent_dma[eng].append(o.idx)
        for b in reads:
            if not b.const:
                b.readers.append(o.idx)
        for b in scalar_reads:
            if not b.const:
                b.readers.append(o.idx)
        for b in writes:
            b.readers = []
            b.writers.append(o.idx)
        return o.idx

    def dma(self, queue, out_ap, in_ap, reads=(), writes=(), transpose=False, tag=""):
        if transpose:
            fn = lambda e, o=out_ap, i=in_ap: e.dma_start_transpose(o, i)
        else:
            fn = lambda e, o=out_ap, i=in_ap: e.dma_start(out=o, in_=i)
        i = self.op(queue, fn, reads=reads, writes=writes, dma=True)
        self.ops[i].tag = tag
        try:
            self.ops[i].small = out_ap.nbytes() <= 32 * 1024
        except Exception:
            self.ops[i].small = True
        return i

    def barrier(self):
        lasts = [self.eng_ops[e][-1].idx for e in ENGINES if self.eng_ops[e]]
        for e in ENGINES:
            lasts.extend(self.recent_dma[e])
        lasts = sorted(set(lasts))
        for e in ENGINES:
            o = _Op(len(self.ops), e, None, False)
            for dd in lasts:
                o.deps[dd] = False
            o.deps.pop(o.idx, None)
            self.ops.append(o)
            self.eng_ops[e].append(o)

    def finalize_and_emit(self, stack):
        nc = self.nc
        for o in self.ops:
            if o.dma:
                o.needs_inc = True
            for dd in o.deps:
                self.ops[dd].needs_inc = True
        exec_sems = {e: [stack.enter_context(nc.semaphore(f"x_{e}_{i}"))
                         for i in range(self.N_EXEC_SEMS)] for e in ENGINES}
        dma_sems = {q: [stack.enter_context(nc.semaphore(f"d_{q}_{i}"))
                        for i in range(self.N_DMA_SEMS)] for q in DMA_QUEUES}
        for e in ENGINES:
            nexec = 0
            ndma = 0
            for o in self.eng_ops[e]:
                if not o.needs_inc:
                    continue
                if o.dma:
                    pool = dma_sems[e]
                    o.sem = pool[ndma % len(pool)]
                    o.val = 16 * (ndma // len(pool) + 1)
                    ndma += 1
                else:
                    pool = exec_sems[e]
                    o.sem = pool[nexec % len(pool)]
                    o.val = nexec // len(pool) + 1
                    nexec += 1
        waited = {e: {} for e in ENGINES}
        MAX_DMA_INFLIGHT = 20

        def run_engine(eng_obj, ename):
            w = waited[ename]
            my_dmas = [o for o in self.eng_ops[ename] if o.dma]
            dma_idx = {o.idx: i for i, o in enumerate(my_dmas)}
            for o in self.eng_ops[ename]:
                tgt = {}
                for dd, force in o.deps.items():
                    p = self.ops[dd]
                    if p.sem is None:
                        continue
                    k = p.sem.name
                    if tgt.get(k, (None, -1))[1] < p.val:
                        tgt[k] = (p.sem, p.val)
                if o.dma:
                    i = dma_idx[o.idx]
                    gate = MAX_DMA_INFLIGHT
                    if i >= gate:
                        p = my_dmas[i - gate]
                        k = p.sem.name
                        if tgt.get(k, (None, -1))[1] < p.val:
                            tgt[k] = (p.sem, p.val)
                for k in sorted(tgt):
                    sem, val = tgt[k]
                    if w.get(k, 0) < val:
                        eng_obj.wait_ge(sem, val)
                        w[k] = val
                inst = o.fn(eng_obj) if o.fn is not None else eng_obj.nop()
                if o.needs_inc:
                    inst.then_inc(o.sem, 16 if o.dma else 1)

        with nc.Block() as block:
            block.tensor(lambda t: run_engine(t, "tensor"))
            block.vector(lambda v: run_engine(v, "vector"))
            block.scalar(lambda s: run_engine(s, "scalar"))
            block.gpsimd(lambda g: run_engine(g, "gpsimd"))
            block.sync(lambda y: run_engine(y, "sync"))


class Arena:
    def __init__(self, nc, stack, cols=88 * 1024):
        self.t = stack.enter_context(nc.sbuf_tensor("arena", [128, cols], BF16))
        self.cols = cols
        self.off = 0
        self.marks = []
        self.peak = 0

    def bf16(self, cols, parts=128):
        alloc = (cols + 15) & ~15
        assert self.off + alloc <= self.cols, f"arena overflow {self.off}+{alloc}"
        ap = self.t[0:parts, self.off:self.off + cols]
        self.off += alloc
        self.peak = max(self.peak, self.off)
        return ap

    def f32(self, cols, parts=128):
        return self.bf16(2 * cols, parts).bitcast(F32)

    def mark(self):
        self.marks.append(self.off)

    def release(self):
        self.off = self.marks.pop()


def _bank(ctx, i, n=1):
    return ctx["psum"][:, 512 * i:512 * (i + n)]


def emit_ln_cols(ctx, cols, sl, cb):
    P = ctx["P"]
    s_c = cols[:, 5 * sl + 0:5 * sl + 1]
    ss_c = cols[:, 5 * sl + 1:5 * sl + 2]
    t1_c = cols[:, 5 * sl + 2:5 * sl + 3]
    rstd_c = cols[:, 5 * sl + 3:5 * sl + 4]
    nmr_c = cols[:, 5 * sl + 4:5 * sl + 5]
    P.op("scalar", lambda s, i_=s_c, o=t1_c:
         s.activation(out=o, in_=i_, func=AF.Square, scale=float(D) ** -0.5),
         reads=[cb], writes=[cb])
    P.op("vector", lambda v, i_=ss_c, o=t1_c, s1=t1_c:
         v.tensor_scalar(out=o, in0=i_, scalar1=s1, scalar2=1.0 / D,
                         op0=ALU.subtract, op1=ALU.mult),
         reads=[cb], scalar_reads=[cb], writes=[cb])
    P.op("scalar", lambda s, i_=t1_c, o=rstd_c, e=ctx["epscol"]:
         s.activation(out=o, in_=i_, func=AF.Ln, bias=e),
         reads=[cb], scalar_reads=[ctx["epscolb"]], writes=[cb])
    P.op("scalar", lambda s, i_=rstd_c, o=rstd_c:
         s.activation(out=o, in_=i_, func=AF.Exp, scale=-0.5),
         reads=[cb], writes=[cb])
    P.op("vector", lambda v, i_=s_c, o=nmr_c, s1=rstd_c:
         v.tensor_scalar(out=o, in0=i_, scalar1=s1, scalar2=-1.0 / D,
                         op0=ALU.mult, op1=ALU.mult),
         reads=[cb], scalar_reads=[cb], writes=[cb])
    return rstd_c, nmr_c


def emit_ln_tm(ctx, x_src_dram, xn_dram):
    P, A = ctx["P"], ctx["arena"]
    A.mark()
    xt = [A.f32(D) for _ in range(2)]
    xb = [Buf(f"lnx{i}") for i in range(2)]
    xn = [A.bf16(D) for _ in range(2)]
    xnb = [Buf(f"lnxn{i}") for i in range(2)]
    dump = A.bf16(D)
    dumpb = Buf("lndump")
    cols = A.f32(16)
    colb = [Buf(f"lncol{i}") for i in range(2)]
    for i in range(NT):
        sl = i % 2
        x_ap, xbuf = xt[sl], xb[sl]
        xn_ap, xnbuf = xn[sl], xnb[sl]
        cb = colb[sl]
        P.dma("sync", x_ap, x_src_dram[128 * i:128 * (i + 1), :], writes=[xbuf])
        s_c = cols[:, 5 * sl + 0:5 * sl + 1]
        ss_c = cols[:, 5 * sl + 1:5 * sl + 2]
        P.op("vector", lambda v, x=x_ap, o=s_c: v.reduce_sum(out=o, in_=x, axis=AX.X),
             reads=[xbuf], writes=[cb])
        P.op("scalar", lambda s, x=x_ap, o=dump, a=ss_c:
             s.activation(out=o, in_=x, func=AF.Square, accum_out=a),
             reads=[xbuf], writes=[dumpb, cb])
        rstd_c, nmr_c = emit_ln_cols(ctx, cols, sl, cb)
        P.op("vector", lambda v, x=x_ap, o=xn_ap, s1=rstd_c, s2=nmr_c:
             v.tensor_scalar(out=o, in0=x, scalar1=s1, scalar2=s2,
                             op0=ALU.mult, op1=ALU.add),
             reads=[xbuf], scalar_reads=[cb], writes=[xnbuf])
        P.dma("sync", xn_dram[128 * i:128 * (i + 1), :], xn_ap, reads=[xnbuf])
    A.release()


def load_weight_fm(ctx, w_dram, n_din, dout_cols):
    P, A = ctx["P"], ctx["arena"]
    wb = Buf("wfm")
    chunks = []
    for c in range(n_din):
        ap = A.bf16(dout_cols)
        chunks.append(ap)
        P.dma("gpsimd", ap, w_dram[128 * c:128 * (c + 1), :], writes=[wb])
    return chunks, wb


def load_xn_fm(ctx, xn_dram, n_chunks=ND, reads=()):
    P, A = ctx["P"], ctx["arena"]
    xb = Buf("xnfm")
    chunks = []
    for c in range(n_chunks):
        ap = A.bf16(T)
        chunks.append(ap)
        for sl in range(T // 512):
            P.dma("sync", ap[:, 512 * sl:512 * (sl + 1)],
                  xn_dram[512 * sl:512 * (sl + 1), 128 * c:128 * (c + 1)],
                  reads=reads, writes=[xb], transpose=True)
    return chunks, xb


def emit_maxsq_blk(ctx, src_ap, srcbuf, sq, sqb, maxsq, col, init):
    P = ctx["P"]
    pbufs = ctx["psum_bufs"]
    nh = maxsq["nh"]
    P.op("scalar", lambda s, i_=src_ap, o=sq:
         s.activation(out=o, in_=i_, func=AF.Square),
         reads=[srcbuf], writes=[sqb])
    P.op("tensor", lambda t, o=ctx["psum"][0:nh, 512 * 7:512 * 7 + 512], hs=maxsq["hs"]:
         t.matmul(o, hs, sq, start=True, stop=True),
         reads=[sqb, maxsq["hsb"]], writes=[pbufs[7]])
    if init:
        P.op("vector", lambda v, o=maxsq["ap"][0:nh, col:col + 1], nh=nh:
             v.reduce_max(out=o, in_=ctx["psum"][0:nh, 512 * 7:512 * 7 + 512], axis=AX.X),
             reads=[pbufs[7]], writes=[maxsq["buf"]])
    else:
        P.op("vector", lambda v, o=maxsq["tmpcol"], nh=nh:
             v.reduce_max(out=o, in_=ctx["psum"][0:nh, 512 * 7:512 * 7 + 512], axis=AX.X),
             reads=[pbufs[7]], writes=[maxsq["tmpbuf"]])
        P.op("vector", lambda v, o=maxsq["ap"][0:nh, col:col + 1], b=maxsq["tmpcol"], nh=nh:
             v.tensor_tensor(out=o, in0=o, in1=b[0:nh, :], op=ALU.max),
             reads=[maxsq["buf"], maxsq["tmpbuf"]], writes=[maxsq["buf"]])


def emit_proj_formB(ctx, w_chunks, wbuf, xn_chunks, xnbuf, out_dram, out_dramb,
                    n_din, n_dout, bias_col=None, biasbuf=None, maxsq=None):
    P, A = ctx["P"], ctx["arena"]
    A.mark()
    stage = A.bf16(T)
    stageb = Buf("pjstage")
    if maxsq is not None:
        sq = A.bf16(512)
        sqb = Buf("pjsq")
    pbufs = ctx["psum_bufs"]
    for j in range(n_dout):
        for blk in range(NB):
            pb = blk % 4

            def mmfn(t, j=j, blk=blk, pb=pb):
                last = None
                for c in range(n_din):
                    last = t.matmul(_bank(ctx, pb),
                                    w_chunks[c][:, 128 * j:128 * (j + 1)],
                                    xn_chunks[c][:, 512 * blk:512 * (blk + 1)],
                                    start=(c == 0), stop=(c == n_din - 1))
                return last
            P.op("tensor", mmfn, reads=[wbuf, xnbuf], writes=[pbufs[pb]])
            if bias_col is not None:
                P.op("scalar", lambda s, pb=pb, o=stage[:, 512 * blk:512 * (blk + 1)],
                     b=bias_col[:, j:j + 1]:
                     s.activation(out=o, in_=_bank(ctx, pb), func=AF.Identity, bias=b),
                     reads=[pbufs[pb]], scalar_reads=[biasbuf], writes=[stageb])
            else:
                P.op("scalar", lambda s, pb=pb, o=stage[:, 512 * blk:512 * (blk + 1)]:
                     s.activation(out=o, in_=_bank(ctx, pb), func=AF.Copy),
                     reads=[pbufs[pb]], writes=[stageb])
        if maxsq is not None:
            for blk in range(NB):
                emit_maxsq_blk(ctx, stage[:, 512 * blk:512 * (blk + 1)], stageb,
                               sq, sqb, maxsq, j, blk == 0)
        P.dma("sync", out_dram[128 * j:128 * (j + 1), :], stage, reads=[stageb],
              writes=[out_dramb])
    A.release()


def emit_rope_phase(ctx, qk_dram, qkbuf, maxsq, iw):
    P, A = ctx["P"], ctx["arena"]
    A.mark()
    stage = A.bf16(T)
    stageb = Buf("rpstage")
    swap = A.bf16(T)
    swapb = Buf("rpswap")
    sq = A.bf16(512)
    sqb = Buf("rpsq")
    ropecc = A.bf16(T)
    ropess = A.bf16(T)
    ropeb = Buf("rope")
    P.dma("gpsimd", ropecc, iw["c_ropecc"], writes=[ropeb])
    P.dma("gpsimd", ropess, iw["c_ropess"], writes=[ropeb])
    ropeb.const = True
    ctx["ropecc"], ctx["ropeccb"] = ropecc, ropeb
    ctx["ropess"], ctx["ropessb"] = ropess, ropeb
    for c in range(ND):
        P.dma("sync", stage, qk_dram[128 * c:128 * (c + 1), :],
              reads=[qkbuf], writes=[stageb])
        for i in range(4):
            src = (1, 0, 3, 2)[i]
            P.dma("gpsimd", swap[32 * i:32 * (i + 1), :],
                  qk_dram[128 * c + 32 * src:128 * c + 32 * (src + 1), :],
                  reads=[qkbuf], writes=[swapb])
        P.op("vector", lambda v, s_=stage, cc=ctx["ropecc"]:
             v.tensor_mul(out=s_, in0=s_, in1=cc),
             reads=[stageb, ctx["ropeccb"]], writes=[stageb])
        P.op("vector", lambda v, s_=swap, ss=ctx["ropess"]:
             v.tensor_mul(out=s_, in0=s_, in1=ss),
             reads=[swapb, ctx["ropessb"]], writes=[swapb])
        P.op("vector", lambda v, a=stage, b=swap:
             v.tensor_add(out=a, in0=a, in1=b),
             reads=[stageb, swapb], writes=[stageb])
        for blk in range(NB):
            emit_maxsq_blk(ctx, stage[:, 512 * blk:512 * (blk + 1)], stageb,
                           sq, sqb, maxsq, c, blk == 0)
        P.dma("sync", qk_dram[128 * c:128 * (c + 1), :], stage,
              reads=[stageb], writes=[qkbuf])
    A.release()


def emit_proj_formA(ctx, xn_chunks, xnbuf, w_chunks, wbuf, n_din, dout,
                    bias_row=None, biasrowb=None, consume=None, group=4,
                    extra_mm_reads=()):
    P = ctx["P"]
    pbufs = ctx["psum_bufs"]
    nbk = (dout + 511) // 512
    per = group
    assert per * nbk <= 8
    for tg in range(0, NT, per):
        tts = list(range(tg, min(tg + per, NT)))
        for ti, tt in enumerate(tts):
            b0 = ti * nbk

            def mmfn(t, tt=tt, b0=b0):
                last = None
                for c in range(n_din):
                    for k in range(nbk):
                        w = w_chunks[c][:, 512 * k:min(512 * (k + 1), dout)]
                        last = t.matmul(
                            ctx["psum"][:, 512 * (b0 + k):512 * (b0 + k) + w.shape[1]],
                            xn_chunks[c][:, 128 * tt:128 * (tt + 1)],
                            w, start=(c == 0),
                            stop=(c == n_din - 1 and bias_row is None))
                if bias_row is not None:
                    for k in range(nbk):
                        br = bias_row[:, 512 * k:min(512 * (k + 1), dout)]
                        last = t.matmul(
                            ctx["psum"][:, 512 * (b0 + k):512 * (b0 + k) + br.shape[1]],
                            ctx["ones_row"], br, start=False, stop=True)
                return last
            reads = [xnbuf, wbuf] + list(extra_mm_reads)
            if bias_row is not None:
                reads += [biasrowb, ctx["ones_rowb"]]
            P.op("tensor", mmfn, reads=reads,
                 writes=[pbufs[b0 + k] for k in range(nbk)])
        for ti, tt in enumerate(tts):
            b0 = ti * nbk
            consume(tt, ctx["psum"][:, 512 * b0:512 * b0 + dout],
                    [pbufs[b0 + k] for k in range(nbk)])


def emit_outer_ln_residual_consume(ctx, x_src_dram, x_dst_dram):
    P, A = ctx["P"], ctx["arena"]
    xt = [A.f32(D) for _ in range(2)]
    xb = [Buf(f"orx{i}") for i in range(2)]
    x1 = [A.f32(D) for _ in range(2)]
    x1b = [Buf(f"orx1{i}") for i in range(2)]
    dump = A.bf16(D)
    dumpb = Buf("ordump")
    cols = A.f32(16)
    colb = [Buf(f"orc{i}") for i in range(2)]

    def consume(tt, psum_ap, bankbufs):
        sl = tt % 2
        cb = colb[sl]
        x_ap, xbuf = xt[sl], xb[sl]
        x1_ap, x1buf = x1[sl], x1b[sl]
        P.dma("sync", x_ap, x_src_dram[128 * tt:128 * (tt + 1), :], writes=[xbuf])
        s_c = cols[:, 5 * sl + 0:5 * sl + 1]
        ss_c = cols[:, 5 * sl + 1:5 * sl + 2]
        P.op("vector", lambda v, o=s_c, p=psum_ap: v.reduce_sum(out=o, in_=p, axis=AX.X),
             reads=bankbufs, writes=[cb])
        P.op("scalar", lambda s, o=dump, a=ss_c, p=psum_ap:
             s.activation(out=o, in_=p, func=AF.Square, accum_out=a),
             reads=bankbufs, writes=[dumpb, cb])
        rstd_c, nmr_c = emit_ln_cols(ctx, cols, sl, cb)
        P.op("vector", lambda v, x=x_ap, s1=nmr_c:
             v.tensor_scalar_add(out=x, in0=x, scalar1=s1),
             reads=[xbuf], scalar_reads=[cb], writes=[xbuf])
        P.op("vector", lambda v, o=x1_ap, s1=rstd_c, x=x_ap, p=psum_ap:
             v.scalar_tensor_tensor(out=o, in0=p, scalar=s1, in1=x,
                                    op0=ALU.mult, op1=ALU.add),
             reads=list(bankbufs) + [xbuf], scalar_reads=[cb], writes=[x1buf])
        P.dma("sync", x_dst_dram[128 * tt:128 * (tt + 1), :], x1_ap, reads=[x1buf])

    return consume


def emit_attention_core(ctx, H, HD, scale, causal, q_dram, qdramb, k_dram, kdramb,
                        v_tm, v_tmb, negB_all, negBhb, o_dram, odramb,
                        den_dram, dendramb, negb_col):
    P, A = ctx["P"], ctx["arena"]
    VW = HD + 1
    A.mark()
    qp = [A.bf16(T, parts=HD) for _ in range(2)]
    kp = [A.bf16(T, parts=HD) for _ in range(2)]
    qpb = [Buf(f"aqp{i}") for i in range(2)]
    kpb = [Buf(f"akp{i}") for i in range(2)]
    e_sb = [A.bf16(1024) for _ in range(3)]
    e_bf = [Buf(f"ae{i}") for i in range(3)]
    o_stage = A.bf16(T)
    o_stageb = Buf("aostage")
    den = A.f32(512, parts=1)
    denb = Buf("aden")
    pbufs = ctx["psum_bufs"]
    hpc = 128 // HD

    for h in range(H):
        sl = h % 2
        negB = negb_col(h)
        P.dma("sync", qp[sl], q_dram[HD * h:HD * (h + 1), :],
              reads=[qdramb], writes=[qpb[sl]])
        P.dma("sync", kp[sl], k_dram[HD * h:HD * (h + 1), :],
              reads=[kdramb], writes=[kpb[sl]])
        for pi in range(4):
            qbs = (2 * pi, 2 * pi + 1)
            units = []
            for qb in qbs:
                ktm = min(NT, 4 * (qb + 1)) if causal else NT
                for kt in range(0, ktm, 2):
                    units.append((qb, kt, kt + 2 >= ktm))
            N = len(units)

            def emit_pv(n):
                qb, kt, last = units[n]
                ps = n % 3
                es = n % 3
                ob = qb - qbs[0]
                first = (kt == 0)
                for j in range(2):
                    vsl = v_tm[:, (kt + j) * H * VW + VW * h:(kt + j) * H * VW + VW * (h + 1)]
                    P.op("tensor", lambda t, vs=vsl, es=es, j=j, ob=ob,
                         st=(first and j == 0), sp=(last and j == 1):
                         t.matmul(ctx["psum"][0:VW, 512 * ob:512 * ob + 512],
                                  vs, e_sb[es][:, 512 * j:512 * (j + 1)],
                                  start=st, stop=sp),
                         reads=[e_bf[es], v_tmb], writes=[pbufs[ob]])
                if last:
                    P.op("scalar", lambda s, ob=ob, o=den:
                         s.activation(out=o, in_=ctx["psum"][HD:HD + 1, 512 * ob:512 * ob + 512],
                                      func=AF.Copy),
                         reads=[pbufs[ob]], writes=[denb])
                    P.dma("sync", den_dram[h:h + 1, 512 * qb:512 * (qb + 1)], den,
                          reads=[denb], writes=[dendramb], tag=f"denst_{h}_{qb}")
                    P.op("vector", lambda v, ob=ob, qb=qb,
                         o=o_stage[HD * (h % hpc):HD * (h % hpc + 1), 512 * qb:512 * (qb + 1)]:
                         v.tensor_copy(out=o, in_=ctx["psum"][0:HD, 512 * ob:512 * ob + 512]),
                         reads=[pbufs[ob]], writes=[o_stageb])

            for n in range(N):
                qb, kt, last = units[n]
                ps = n % 3
                def smm(t, kt=kt, qb=qb, ps=ps, sl=sl):
                    i1 = t.matmul(_bank(ctx, 2 + 2 * ps),
                                  kp[sl][:, 128 * kt:128 * (kt + 1)],
                                  qp[sl][:, 512 * qb:512 * (qb + 1)],
                                  start=True, stop=True)
                    i2 = t.matmul(_bank(ctx, 3 + 2 * ps),
                                  kp[sl][:, 128 * (kt + 1):128 * (kt + 2)],
                                  qp[sl][:, 512 * qb:512 * (qb + 1)],
                                  start=True, stop=True)
                    return i2
                P.op("tensor", smm, reads=[kpb[sl], qpb[sl]],
                     writes=[pbufs[2 + 2 * ps], pbufs[3 + 2 * ps]])
                es = n % 3
                P.op("scalar", lambda s, ps=ps, es=es, nb_=negB:
                     s.activation(out=e_sb[es], in_=_bank(ctx, 2 + 2 * ps, 2),
                                  func=AF.Exp, scale=scale, bias=nb_),
                     reads=[pbufs[2 + 2 * ps], pbufs[3 + 2 * ps]],
                     scalar_reads=[negBhb], writes=[e_bf[es]])
                if causal:
                    for j in range(2):
                        mi = (kt + j) - 4 * qb
                        if 0 <= mi <= 3:
                            P.op("vector", lambda v, es=es, j=j, m=ctx["masks"][mi]:
                                 v.tensor_mul(out=e_sb[es][:, 512 * j:512 * (j + 1)],
                                              in0=e_sb[es][:, 512 * j:512 * (j + 1)], in1=m),
                                 reads=[e_bf[es], ctx["masksb"]], writes=[e_bf[es]])
                if n >= 2:
                    emit_pv(n - 2)
            for n in (N - 2, N - 1):
                if n >= 0:
                    emit_pv(n)
        if h % hpc == hpc - 1:
            P.dma("sync", o_dram[128 * (h // hpc):128 * (h // hpc) + 128, :], o_stage,
                  reads=[o_stageb], writes=[odramb])
    A.release()


def emit_bulk_normalize(ctx, H, HD, o_chunks, obuf, den_dram, dendramb, rall_dram):
    P, A = ctx["P"], ctx["arena"]
    A.mark()
    rallb = Buf("ralld")
    den_all = A.f32(T, parts=H)
    den_allb = Buf("denall")
    den_tmp = A.f32(T, parts=H)
    den_tmpb = Buf("dentmp")
    P.dma("sync", den_all, den_dram, reads=[dendramb], writes=[den_allb], tag="denload")
    if DBG2:
        P.dma("sync", ctx["dram"]["dsnap"], den_all, reads=[den_allb])
    P.op("scalar", lambda s: s.activation(out=den_tmp, in_=den_all, func=AF.Ln),
         reads=[den_allb], writes=[den_tmpb])
    P.op("scalar", lambda s: s.activation(out=den_all, in_=den_tmp, func=AF.Exp,
                                          scale=-1.0),
         reads=[den_tmpb], writes=[den_allb])
    P.dma("sync", rall_dram[0:H, :], den_all, reads=[den_allb], writes=[rallb])
    P.barrier()
    A.release()
    A.mark()
    rbf = A.f32(T)
    rbfb = Buf("rbalf")
    rb = A.bf16(T)
    rbb = Buf("rball")
    hpc = 128 // HD
    for h in range(H):
        c, j = h // hpc, h % hpc
        P.dma("sync", rbf[HD * j:HD * (j + 1), :],
              bass.AP(rall_dram.tensor, rall_dram.offset + T * h, [[0, HD], [1, T]]),
              reads=[rallb], writes=[rbfb])
        P.op("vector", lambda v, j=j, o=rb[HD * j:HD * (j + 1), :],
             i_=rbf[HD * j:HD * (j + 1), :]:
             v.tensor_copy(out=o, in_=i_),
             reads=[rbfb], writes=[rbb])
        P.op("vector", lambda v, c=c, j=j,
             o=o_chunks[c][HD * j:HD * (j + 1), :], r=rb[HD * j:HD * (j + 1), :]:
             v.tensor_tensor(out=o, in0=o, in1=r, op=ALU.mult),
             reads=[obuf, rbb], writes=[obuf])
    P.barrier()
    A.release()


def emit_block_A(ctx, x_src_dram, x_dst_dram, iw):
    P, A, nc = ctx["P"], ctx["arena"], ctx["nc"]
    H, HD = 16, 64
    VW = HD + 1
    d = ctx["dram"]
    pbufs = ctx["psum_bufs"]
    ctx["odramb"] = Buf("odramA")
    dendramb = Buf("dendramA")

    emit_ln_tm(ctx, x_src_dram, d["xn"])
    P.barrier()

    A.mark()
    v_tm = A.bf16(NT * H * VW)
    v_tmb = Buf("v_tm")
    mall = A.f32(2 * ND, parts=2)
    msq = mall[0:2, 0:ND]
    msk = mall[0:2, ND:2 * ND]
    mtmp = A.f32(2, parts=2)
    msqB = {"ap": msq, "buf": Buf("msq"), "hs": ctx["headsum2"], "hsb": ctx["headsum2b"],
            "nh": 2, "tmpcol": mtmp[:, 0:1], "tmpbuf": Buf("mt1"), "first": True}
    mskB = {"ap": msk, "buf": Buf("msk"), "hs": ctx["headsum2"], "hsb": ctx["headsum2b"],
            "nh": 2, "tmpcol": mtmp[:, 1:2], "tmpbuf": Buf("mt2"), "first": True}
    ones_view = bass.AP(v_tm.tensor, v_tm.offset + HD,
                        [[v_tm.ap[0][0], 128], [VW, NT * H], [1, 1]])
    P.op("vector", lambda v, o=ones_view: v.memset(o, 1.0), writes=[v_tmb])

    A.mark()
    xn_chunks, xnbuf = load_xn_fm(ctx, d["xn"])
    A.mark()
    wq, wqb = load_weight_fm(ctx, iw["A_wq"], ND, D)
    bq_col = A.f32(ND)
    bqb = Buf("bq")
    P.dma("sync", bq_col, iw["A_bq_col"], writes=[bqb])
    qdramb = Buf("qdram")
    emit_proj_formB(ctx, wq, wqb, xn_chunks, xnbuf, d["q"], qdramb, ND, ND,
                    bias_col=bq_col, biasbuf=bqb)
    A.release()
    P.barrier()
    A.mark()
    wk, wkb = load_weight_fm(ctx, iw["A_wk"], ND, D)
    bk_col = A.f32(ND)
    bkb = Buf("bk")
    P.dma("sync", bk_col, iw["A_bk_col"], writes=[bkb])
    kdramb = Buf("kdram")
    emit_proj_formB(ctx, wk, wkb, xn_chunks, xnbuf, d["k"], kdramb, ND, ND,
                    bias_col=bk_col, biasbuf=bkb)
    A.release()
    P.barrier()
    A.mark()
    wv, wvb = load_weight_fm(ctx, iw["A_wv"], ND, D)
    bv_row = A.bf16(D, parts=1)
    bvb = Buf("bv")
    P.dma("gpsimd", bv_row, iw["A_bv"].unsqueeze(0), writes=[bvb])

    def v_consume(tt, psum_ap, bankbufs):
        out = bass.AP(v_tm.tensor, v_tm.offset + tt * H * VW,
                      [[v_tm.ap[0][0], 128], [VW, H], [1, HD]])
        P.op("scalar", lambda s, o=out, i_=psum_ap.rearrange("p (h e) -> p h e", h=H):
             s.activation(out=o, in_=i_, func=AF.Copy),
             reads=bankbufs, writes=[v_tmb])

    emit_proj_formA(ctx, xn_chunks, xnbuf, wv, wvb, ND, D,
                    bias_row=bv_row, biasrowb=bvb, consume=v_consume, group=4)
    A.release()
    A.release()
    P.barrier()

    if DBG:
        P.dma("sync", d["q0"], d["q"], reads=[qdramb])
    emit_rope_phase(ctx, d["q"], qdramb, msqB, iw)
    emit_rope_phase(ctx, d["k"], kdramb, mskB, iw)
    P.barrier()

    A.mark()
    sqh = A.f32(16)
    sqhb = Buf("sqh")
    negm = A.f32(ND, parts=2)
    negmb = Buf("negm")
    negB_all = A.f32(2 * ND)
    negBhb = Buf("negBall")
    P.op("scalar", lambda s: s.activation(out=sqh[0:2, :], in_=mall[0:2, 0:16], func=AF.Sqrt),
         reads=[msqB["buf"], mskB["buf"]], writes=[sqhb])
    P.op("vector", lambda v: v.tensor_tensor(out=negm[0:2, :], in0=sqh[0:2, 0:ND],
                                             in1=sqh[0:2, ND:2 * ND], op=ALU.mult),
         reads=[sqhb], writes=[negmb])
    P.op("vector", lambda v: v.tensor_scalar_mul(out=negm[0:2, :], in0=negm[0:2, :],
                                                 scalar1=-0.125),
         reads=[negmb], writes=[negmb])
    P.dma("sync", d["negd"], negm[0:2, :], reads=[negmb], writes=[ctx["negdb"]])
    P.dma("sync", negB_all,
          bass.AP(d["negd"].tensor, d["negd"].offset, [[0, 128], [1, 2 * ND]]),
          reads=[ctx["negdb"]], writes=[negBhb])
    emit_attention_core(ctx, H, HD, 0.125, True, d["q"], qdramb, d["k"], kdramb,
                        v_tm, v_tmb, negB_all, negBhb, d["o"], ctx["odramb"],
                        d["denh"], dendramb,
                        lambda h: negB_all[:, ND * (h % 2) + h // 2:ND * (h % 2) + h // 2 + 1])
    A.release()
    P.barrier()

    A.mark()
    o_chunks = []
    obuf = Buf("ofm")
    for c in range(ND):
        ap = A.bf16(T)
        o_chunks.append(ap)
        P.dma("sync", ap, d["o"][128 * c:128 * (c + 1), :],
              reads=[ctx["odramb"]], writes=[obuf])
    emit_bulk_normalize(ctx, H, HD, o_chunks, obuf, d["denh"], dendramb, d["ralla"])
    if DBG2:
        for c in range(ND):
            P.dma("sync", d["q"][128 * c:128 * (c + 1), :], o_chunks[c], reads=[obuf])
    wo, wob = load_weight_fm(ctx, iw["A_wo"], ND, D)
    bo_row = A.bf16(D, parts=1)
    bob = Buf("bo")
    P.dma("gpsimd", bo_row, iw["A_bo"].unsqueeze(0), writes=[bob])
    consume = emit_outer_ln_residual_consume(ctx, x_src_dram, x_dst_dram)
    emit_proj_formA(ctx, o_chunks, obuf, wo, wob, ND, D,
                    bias_row=bo_row, biasrowb=bob, consume=consume, group=4)
    A.release()
    A.release()
    P.barrier()


def emit_block_F(ctx, x_src_dram, x_dst_dram, iw):
    P, A, nc = ctx["P"], ctx["arena"], ctx["nc"]
    H, HD = 16, 32
    VW = HD + 1
    SCL = 1.0 / math.sqrt(HD)
    d = ctx["dram"]
    pbufs = ctx["psum_bufs"]
    aodramb = Buf("aodramF")
    dendramb = Buf("dendramF")
    qdramb = Buf("qdramF")
    kdramb = Buf("kdramF")
    mladramb = Buf("mladramF")
    moedramb = Buf("moedramF")

    emit_ln_tm(ctx, x_src_dram, d["xn"])
    P.barrier()

    A.mark()
    mall = A.f32(8, parts=4)
    msq = mall[0:4, 0:4]
    msk = mall[0:4, 4:8]
    mtmp = A.f32(2, parts=4)
    msqB = {"ap": msq, "buf": Buf("msqF"), "hs": ctx["headsum4"], "hsb": ctx["headsum4b"],
            "nh": 4, "tmpcol": mtmp[:, 0:1], "tmpbuf": Buf("mt1F")}
    mskB = {"ap": msk, "buf": Buf("mskF"), "hs": ctx["headsum4"], "hsb": ctx["headsum4b"],
            "nh": 4, "tmpcol": mtmp[:, 1:2], "tmpbuf": Buf("mt2F")}
    cT_sb = A.bf16(T, parts=8)
    cTb = Buf("cT")
    A.mark()
    v_tm = A.bf16(NT * H * VW)
    v_tmb = Buf("v_tmF")
    ones_view = bass.AP(v_tm.tensor, v_tm.offset + HD,
                        [[v_tm.ap[0][0], 128], [VW, NT * H], [1, 1]])
    P.op("vector", lambda v, o=ones_view: v.memset(o, 1.0), writes=[v_tmb])

    A.mark()
    xn_chunks, xnbuf = load_xn_fm(ctx, d["xn"])
    A.mark()
    wq, wqb = load_weight_fm(ctx, iw["F_qw"], ND, 512)
    bq_col = A.f32(4)
    bqb = Buf("bqF")
    P.dma("sync", bq_col, iw["F_qb_col"], writes=[bqb])
    emit_proj_formB(ctx, wq, wqb, xn_chunks, xnbuf, d["q"][0:512, :], qdramb, ND, 4,
                    bias_col=bq_col, biasbuf=bqb, maxsq=msqB)
    A.release()
    P.barrier()
    A.mark()
    wk, wkb = load_weight_fm(ctx, iw["F_kw"], ND, 512)
    bk_col = A.f32(4)
    bkb = Buf("bkF")
    P.dma("sync", bk_col, iw["F_kb_col"], writes=[bkb])
    emit_proj_formB(ctx, wk, wkb, xn_chunks, xnbuf, d["k"][0:512, :], kdramb, ND, 4,
                    bias_col=bk_col, biasbuf=bkb, maxsq=mskB)
    A.release()
    P.barrier()
    A.mark()
    wv, wvb = load_weight_fm(ctx, iw["F_vw"], ND, 512)
    bv_row = A.bf16(512, parts=1)
    bvb = Buf("bvF")
    P.dma("gpsimd", bv_row, iw["F_vb"].unsqueeze(0), writes=[bvb])

    def v_consume(tt, psum_ap, bankbufs):
        out = bass.AP(v_tm.tensor, v_tm.offset + tt * H * VW,
                      [[v_tm.ap[0][0], 128], [VW, H], [1, HD]])
        P.op("scalar", lambda s, o=out, i_=psum_ap.rearrange("p (h e) -> p h e", h=H):
             s.activation(out=o, in_=i_, func=AF.Copy),
             reads=bankbufs, writes=[v_tmb])

    emit_proj_formA(ctx, xn_chunks, xnbuf, wv, wvb, ND, 512,
                    bias_row=bv_row, biasrowb=bvb, consume=v_consume, group=4)
    A.release()
    A.release()
    P.barrier()

    A.mark()
    sqh = A.f32(8, parts=4)
    sqhb = Buf("sqhF")
    negm = A.f32(4, parts=4)
    negmb = Buf("negmF")
    negB_all = A.f32(16)
    negBhb = Buf("negBallF")
    P.op("scalar", lambda s: s.activation(out=sqh, in_=mall[0:4, 0:8], func=AF.Sqrt),
         reads=[msqB["buf"], mskB["buf"]], writes=[sqhb])
    P.op("vector", lambda v: v.tensor_tensor(out=negm[0:4, :], in0=sqh[0:4, 0:4],
                                             in1=sqh[0:4, 4:8], op=ALU.mult),
         reads=[sqhb], writes=[negmb])
    P.op("vector", lambda v: v.tensor_scalar_mul(out=negm[0:4, :], in0=negm[0:4, :],
                                                 scalar1=-SCL),
         reads=[negmb], writes=[negmb])
    P.dma("sync", d["negd2"], negm[0:4, :], reads=[ctx["negd2b"]] and [negmb], writes=[ctx["negd2b"]])
    P.dma("sync", negB_all,
          bass.AP(d["negd2"].tensor, d["negd2"].offset, [[0, 128], [1, 16]]),
          reads=[ctx["negd2b"]], writes=[negBhb])
    emit_attention_core(ctx, H, HD, SCL, False, d["q"][0:512, :], qdramb,
                        d["k"][0:512, :], kdramb, v_tm, v_tmb, negB_all, negBhb,
                        d["ao"], aodramb, d["denh"], dendramb,
                        lambda h: negB_all[:, 4 * (h % 4) + h // 4:4 * (h % 4) + h // 4 + 1])
    A.release()
    A.release()
    P.barrier()

    A.mark()
    ao_chunks = []
    aobuf = Buf("aofm")
    for c in range(4):
        ap = A.bf16(T)
        ao_chunks.append(ap)
        P.dma("sync", ap, d["ao"][128 * c:128 * (c + 1), :],
              reads=[aodramb], writes=[aobuf])
    emit_bulk_normalize(ctx, 16, 32, ao_chunks, aobuf, d["denh"], dendramb, d["ralla"])
    wo, wob = load_weight_fm(ctx, iw["F_ow"], 4, D)
    bo_col = A.f32(ND)
    bob = Buf("boF")
    P.dma("sync", bo_col, iw["F_ob_col"], writes=[bob])
    emit_proj_formB(ctx, wo, wob, ao_chunks, aobuf, d["mla"], mladramb, 4, ND,
                    bias_col=bo_col, biasbuf=bob)
    A.release()
    P.barrier()

    A.mark()
    mla_chunks = []
    mlabuf = Buf("mlafm")
    for c in range(ND):
        ap = A.bf16(T)
        mla_chunks.append(ap)
        P.dma("sync", ap, d["mla"][128 * c:128 * (c + 1), :],
              reads=[mladramb], writes=[mlabuf])
    rw, rwb = load_weight_fm(ctx, iw["F_rw"], ND, 8)
    rb_row = A.bf16(8, parts=1)
    rbb = Buf("rbrow")
    P.dma("gpsimd", rb_row, iw["F_rb"].unsqueeze(0), writes=[rbb])
    rcols = A.f32(8)
    rcolsb = Buf("rcols")
    rlg_dbg = A.f32(8)
    rlgdbgb = Buf("rlgdbg")
    el = A.f32(8)
    elb = Buf("el")
    probs = A.f32(8)
    probsb = Buf("probs")
    mask1 = A.f32(8)
    mask1b = Buf("mask1")
    comb = A.bf16(8)
    combb = Buf("comb")

    def r_consume(tt, psum_ap, bankbufs):
        if DBG2:
            P.op("vector", lambda v, p=psum_ap, o=rlg_dbg:
                 v.tensor_copy(out=o, in_=p), reads=bankbufs, writes=[rlgdbgb])
            P.dma("sync", d["rldbg"][128 * tt:128 * (tt + 1), :], rlg_dbg,
                  reads=[rlgdbgb])
        m_c, nm_c, s_c, m2_c, d2_c = (rcols[:, i:i + 1] for i in range(5))
        P.op("vector", lambda v, p=psum_ap, o=m_c: v.reduce_max(out=o, in_=p, axis=AX.X),
             reads=bankbufs, writes=[rcolsb])
        P.op("vector", lambda v, o=nm_c, i_=m_c:
             v.tensor_scalar_mul(out=o, in0=i_, scalar1=-1.0),
             reads=[rcolsb], scalar_reads=[rcolsb], writes=[rcolsb])
        P.op("scalar", lambda s, p=psum_ap, o=el, b=nm_c:
             s.activation(out=o, in_=p, func=AF.Exp, bias=b, accum_out=s_c),
             reads=bankbufs, scalar_reads=[rcolsb], writes=[elb, rcolsb])
        P.op("scalar", lambda s, o=s_c, i_=s_c: s.activation(out=o, in_=i_, func=AF.Ln),
             reads=[rcolsb], writes=[rcolsb])
        P.op("scalar", lambda s, o=s_c, i_=s_c: s.activation(out=o, in_=i_, func=AF.Exp,
                                                             scale=-1.0),
             reads=[rcolsb], writes=[rcolsb])
        P.op("vector", lambda v, o=probs, i_=el, s1=s_c:
             v.tensor_scalar_mul(out=o, in0=i_, scalar1=s1),
             reads=[elb], scalar_reads=[rcolsb], writes=[probsb])
        P.op("vector", lambda v, o=m_c, i_=probs: v.reduce_max(out=o, in_=i_, axis=AX.X),
             reads=[probsb], writes=[rcolsb])
        P.op("vector", lambda v, o=mask1, i_=probs, s1=m_c:
             v.tensor_single_scalar(out=o, in_=i_, scalar=s1, op=ALU.is_ge),
             reads=[probsb], scalar_reads=[rcolsb], writes=[mask1b])
        P.op("vector", lambda v, o=el:
             v.scalar_tensor_tensor(out=o, in0=mask1, scalar=-1e30, in1=probs,
                                    op0=ALU.mult, op1=ALU.add),
             reads=[mask1b, probsb], writes=[elb])
        P.op("vector", lambda v, o=m2_c, i_=el: v.reduce_max(out=o, in_=i_, axis=AX.X),
             reads=[elb], writes=[rcolsb])
        P.op("vector", lambda v, o=mask1, i_=probs, s1=m2_c:
             v.tensor_single_scalar(out=o, in_=i_, scalar=s1, op=ALU.is_ge),
             reads=[probsb], scalar_reads=[rcolsb], writes=[mask1b])
        P.op("vector", lambda v, o=d2_c, i_=m_c, s1=m2_c:
             v.tensor_scalar(out=o, in0=i_, scalar1=s1, scalar2=1e-8,
                             op0=ALU.add, op1=ALU.add),
             reads=[rcolsb], scalar_reads=[rcolsb], writes=[rcolsb])
        P.op("scalar", lambda s, o=d2_c, i_=d2_c: s.activation(out=o, in_=i_, func=AF.Ln),
             reads=[rcolsb], writes=[rcolsb])
        P.op("scalar", lambda s, o=d2_c, i_=d2_c: s.activation(out=o, in_=i_, func=AF.Exp,
                                                               scale=-1.0),
             reads=[rcolsb], writes=[rcolsb])
        P.op("vector", lambda v, o=comb, s1=d2_c:
             v.scalar_tensor_tensor(out=o, in0=probs, scalar=s1, in1=mask1,
                                    op0=ALU.mult, op1=ALU.mult),
             reads=[probsb, mask1b], scalar_reads=[rcolsb], writes=[combb])
        if DBG2:
            P.dma("sync", d["rp_dbg"][128 * tt:128 * (tt + 1), 0:8], probs,
                  reads=[probsb])
            P.dma("sync", d["rp_dbg"][128 * tt:128 * (tt + 1), 8:16], mask1,
                  reads=[mask1b])
            P.dma("sync", d["rp_dbg"][128 * tt:128 * (tt + 1), 16:24], rcols[:, 0:8],
                  reads=[rcolsb])
        P.op("tensor", lambda t, tt=tt:
             t.matmul(ctx["psum"][0:8, 512 * 4:512 * 4 + 128], comb, ctx["ident"],
                      start=True, stop=True),
             reads=[combb, ctx["identb"]], writes=[pbufs[4]])
        P.op("vector", lambda v, tt=tt, o=cT_sb[0:8, 128 * tt:128 * (tt + 1)]:
             v.tensor_copy(out=o, in_=ctx["psum"][0:8, 512 * 4:512 * 4 + 128]),
             reads=[pbufs[4]], writes=[cTb])

    emit_proj_formA(ctx, mla_chunks, mlabuf, rw, rwb, ND, 8,
                    bias_row=rb_row, biasrowb=rbb, consume=r_consume, group=4)
    if DBG2:
        P.dma("sync", d["ctdbg"], cT_sb[0:8, :], reads=[cTb])
    A.release()
    P.barrier()

    A.mark()
    mla_win = [A.bf16(1024) for _ in range(ND)]
    mlawb = Buf("mlawin")
    w1sb = [A.bf16(2048) for _ in range(ND)]
    w1b = Buf("w1e")
    w2sb = [A.bf16(1024) for _ in range(16)]
    w2b = Buf("w2e")
    b1col = A.f32(16)
    b1b = Buf("b1e")
    b2all = A.bf16(1024, parts=8)
    b2b = Buf("b2all")
    P.dma("gpsimd", b2all, iw["F_b2"], writes=[b2b])
    cb_sb = A.bf16(T)
    cbb = Buf("cb")
    h1g = A.bf16(16 * 1024)
    h1gb = Buf("h1g")
    mstage = [A.f32(1024) for _ in range(2)]
    mstageb = [Buf(f"mst{i}") for i in range(2)]

    for e in range(8):
        for c in range(ND):
            P.dma("gpsimd", w1sb[c], iw["F_w1"][e, 128 * c:128 * (c + 1), :], writes=[w1b])
        for f in range(16):
            P.dma("gpsimd", w2sb[f], iw["F_w2"][e, 128 * f:128 * (f + 1), :], writes=[w2b])
        P.dma("sync", b1col, iw["F_b1_col"][e], writes=[b1b])
        for blk in range(NB):
            P.op("tensor", lambda t, e=e, blk=blk:
                 t.matmul(ctx["psum"][:, 512 * 4:512 * 5],
                          ctx["e8"][0:8, 128 * e:128 * (e + 1)],
                          cT_sb[0:8, 512 * blk:512 * (blk + 1)],
                          start=True, stop=True),
                 reads=[cTb, ctx["e8b"]], writes=[pbufs[4]])
            P.op("vector", lambda v, blk=blk, o=cb_sb[:, 512 * blk:512 * (blk + 1)]:
                 v.tensor_copy(out=o, in_=ctx["psum"][:, 512 * 4:512 * 5]),
                 reads=[pbufs[4]], writes=[cbb])
        for ps in range(4):
            w0 = 1024 * ps
            for c in range(ND):
                P.dma("sync", mla_win[c], d["mla"][128 * c:128 * (c + 1), w0:w0 + 1024],
                      reads=[mladramb], writes=[mlawb])
            for f in range(16):
                pb = 4 + 2 * (f % 2)
                def h1fn(t, f=f, pb=pb):
                    last = None
                    for nb in range(2):
                        for c in range(ND):
                            last = t.matmul(_bank(ctx, pb + nb),
                                            w1sb[c][:, 128 * f:128 * (f + 1)],
                                            mla_win[c][:, 512 * nb:512 * (nb + 1)],
                                            start=(c == 0), stop=(c == ND - 1))
                    return last
                P.op("tensor", h1fn, reads=[w1b, mlawb],
                     writes=[pbufs[pb], pbufs[pb + 1]])
                P.op("scalar", lambda s, f=f, pb=pb, o=h1g[:, 1024 * f:1024 * (f + 1)],
                     b=b1col[:, f:f + 1]:
                     s.activation(out=o, in_=_bank(ctx, pb, 2), func=AF.Gelu, bias=b),
                     reads=[pbufs[pb], pbufs[pb + 1]], scalar_reads=[b1b], writes=[h1gb])
                P.op("vector", lambda v, f=f, ps=ps, o=h1g[:, 1024 * f:1024 * (f + 1)],
                     cbw=cb_sb[:, w0:w0 + 1024]:
                     v.tensor_mul(out=o, in0=o, in1=cbw),
                     reads=[h1gb, cbb], writes=[h1gb])
            for tti in range(8):
                pb = 2 * (tti % 2)
                def h2fn(t, tti=tti, pb=pb, e=e, ps=ps):
                    last = None
                    for nb in range(2):
                        for f in range(16):
                            last = t.matmul(_bank(ctx, pb + nb),
                                            h1g[:, 1024 * f + 128 * tti:1024 * f + 128 * (tti + 1)],
                                            w2sb[f][:, 512 * nb:512 * (nb + 1)],
                                            start=(f == 0), stop=(f == 15 and e != 0))
                        if e == 0:
                            tt = ps * 8 + tti
                            last = t.matmul(_bank(ctx, pb + nb),
                                            cT_sb[0:8, 128 * tt:128 * (tt + 1)],
                                            b2all[0:8, 512 * nb:512 * (nb + 1)],
                                            start=False, stop=True)
                    return last
                reads = [h1gb, w2b] + ([cTb, b2b] if e == 0 else [])
                P.op("tensor", h2fn, reads=reads, writes=[pbufs[pb], pbufs[pb + 1]])
                msl = tti % 2
                P.op("vector", lambda v, pb=pb, o=mstage[msl]:
                     v.tensor_copy(out=o, in_=_bank(ctx, pb, 2)),
                     reads=[pbufs[pb], pbufs[pb + 1]], writes=[mstageb[msl]])
                tt = ps * 8 + tti
                if e == 0:
                    P.dma("gpsimd", d["moe"][128 * tt:128 * (tt + 1), :], mstage[msl],
                          reads=[mstageb[msl]], writes=[moedramb])
                else:
                    P.op("gpsimd", lambda g, tt=tt, msl=msl:
                         g.dma_start(out=d["moe"][128 * tt:128 * (tt + 1), :],
                                     in_=mstage[msl], accum_op=ALU.add),
                         reads=[mstageb[msl]], writes=[moedramb], dma=True)
    A.release()
    P.barrier()

    A.mark()
    moet = [A.f32(D) for _ in range(2)]
    moetb = [Buf(f"moet{i}") for i in range(2)]
    xnt = [A.bf16(D) for _ in range(2)]
    xntb = [Buf(f"xnt{i}") for i in range(2)]
    bo_t = [A.f32(D) for _ in range(2)]
    bo_tb = [Buf(f"bot{i}") for i in range(2)]
    xt = [A.f32(D) for _ in range(2)]
    xb = [Buf(f"fx{i}") for i in range(2)]
    x1 = [A.f32(D) for _ in range(2)]
    x1b = [Buf(f"fx1{i}") for i in range(2)]
    dump = A.bf16(D)
    dumpb = Buf("fdump")
    cols = A.f32(16)
    colb = [Buf(f"fc{i}") for i in range(2)]
    for tt in range(NT):
        sl = tt % 2
        cb = colb[sl]
        P.dma("sync", moet[sl], d["moe"][128 * tt:128 * (tt + 1), :],
              reads=[moedramb], writes=[moetb[sl]])
        P.dma("sync", xnt[sl], d["xn"][128 * tt:128 * (tt + 1), :], writes=[xntb[sl]])
        P.dma("sync", xt[sl], x_src_dram[128 * tt:128 * (tt + 1), :], writes=[xb[sl]])
        P.op("vector", lambda v, sl=sl, o=bo_t[sl]:
             v.scalar_tensor_tensor(out=o, in0=moet[sl], scalar=0.3, in1=xnt[sl],
                                    op0=ALU.mult, op1=ALU.add),
             reads=[moetb[sl], xntb[sl]], writes=[bo_tb[sl]])
        s_c = cols[:, 5 * sl + 0:5 * sl + 1]
        ss_c = cols[:, 5 * sl + 1:5 * sl + 2]
        P.op("vector", lambda v, sl=sl, o=s_c: v.reduce_sum(out=o, in_=bo_t[sl], axis=AX.X),
             reads=[bo_tb[sl]], writes=[cb])
        P.op("scalar", lambda s, sl=sl, o=dump, a=ss_c:
             s.activation(out=o, in_=bo_t[sl], func=AF.Square, accum_out=a),
             reads=[bo_tb[sl]], writes=[dumpb, cb])
        rstd_c, nmr_c = emit_ln_cols(ctx, cols, sl, cb)
        P.op("vector", lambda v, sl=sl, s1=nmr_c:
             v.tensor_scalar_add(out=xt[sl], in0=xt[sl], scalar1=s1),
             reads=[xb[sl]], scalar_reads=[cb], writes=[xb[sl]])
        P.op("vector", lambda v, sl=sl, o=x1[sl], s1=rstd_c:
             v.scalar_tensor_tensor(out=o, in0=bo_t[sl], scalar=s1, in1=xt[sl],
                                    op0=ALU.mult, op1=ALU.add),
             reads=[bo_tb[sl], xb[sl]], scalar_reads=[cb], writes=[x1b[sl]])
        P.dma("sync", x_dst_dram[128 * tt:128 * (tt + 1), :], x1[sl], reads=[x1b[sl]])
    A.release()
    A.release()
    P.barrier()


KSOFT_C = 40.0

def emit_block_B(ctx, x_src_dram, x_dst_dram, iw):
    P, A, nc = ctx["P"], ctx["arena"], ctx["nc"]
    H, HD = 16, 64
    d = ctx["dram"]
    pbufs = ctx["psum_bufs"]
    qhb, khb, vhb, zb = Buf("qhB"), Buf("khB"), Buf("vhB"), Buf("zB")
    rowsb = Buf("rowsB")

    emit_ln_tm(ctx, x_src_dram, d["xn"])
    P.barrier()

    A.mark()
    rk_row = A.f32(D, parts=1)
    rk_rowb = Buf("rkrow")
    rcol = [A.f32(1) for _ in range(ND)]
    rcolb = Buf("rcolB")

    A.mark()
    xn_chunks, xnbuf = load_xn_fm(ctx, d["xn"])

    A.mark()
    wq, wqb = load_weight_fm(ctx, iw["B_wq"], ND, D)
    bq_row = A.bf16(D, parts=1)
    bqb = Buf("bqB")
    P.dma("gpsimd", bq_row, iw["B_bq"].unsqueeze(0), writes=[bqb])
    qt = A.f32(D)
    qtb = Buf("qtB")
    qe = A.bf16(D)
    qeb = Buf("qeB")
    seg = A.f32(16)
    segb = Buf("segB")

    def q_consume(tt, psum_ap, bankbufs):
        pv = psum_ap.rearrange("p (s i) -> p s i", s=16)
        P.op("vector", lambda v, o=seg[:, 0:16], pv=pv:
             v.reduce_max(out=o, in_=pv, axis=AX.X),
             reads=bankbufs, writes=[segb])
        P.op("vector", lambda v, pv=pv, o=qt[:].rearrange("p (s i) -> p s i", s=16),
             b=seg[:, 0:16].unsqueeze(2).broadcast_to([128, 16, 64]):
             v.tensor_tensor(out=o, in0=pv, in1=b, op=ALU.subtract),
             reads=bankbufs, scalar_reads=[segb], writes=[qtb])
        P.op("scalar", lambda s: s.activation(out=qe, in_=qt, func=AF.Exp),
             reads=[qtb], writes=[qeb])
        P.op("vector", lambda v, o=seg[:, 0:16], i_=qe[:].rearrange("p (s i) -> p s i", s=16):
             v.reduce_sum(out=o, in_=i_, axis=AX.X),
             reads=[qeb], writes=[segb])
        P.op("scalar", lambda s, o=seg[:, 0:16], i_=seg[:, 0:16]:
             s.activation(out=o, in_=i_, func=AF.Ln),
             reads=[segb], writes=[segb])
        P.op("scalar", lambda s, o=seg[:, 0:16], i_=seg[:, 0:16]:
             s.activation(out=o, in_=i_, func=AF.Exp, scale=-1.0),
             reads=[segb], writes=[segb])
        P.op("vector", lambda v, o=qe[:].rearrange("p (s i) -> p s i", s=16),
             i_=qe[:].rearrange("p (s i) -> p s i", s=16),
             b=seg[:, 0:16].unsqueeze(2).broadcast_to([128, 16, 64]):
             v.tensor_tensor(out=o, in0=i_, in1=b, op=ALU.mult),
             reads=[qeb], scalar_reads=[segb], writes=[qeb])
        P.dma("sync", d["qh"][128 * tt:128 * (tt + 1), :], qe,
              reads=[qeb], writes=[qhb])

    emit_proj_formA(ctx, xn_chunks, xnbuf, wq, wqb, ND, D,
                    bias_row=bq_row, biasrowb=bqb, consume=q_consume, group=4)
    A.release()
    P.barrier()

    A.mark()
    wk, wkb = load_weight_fm(ctx, iw["B_wk"], ND, D)
    bk_row = A.bf16(D, parts=1)
    bkb = Buf("bkB")
    P.dma("gpsimd", bk_row, iw["B_bk"].unsqueeze(0), writes=[bkb])
    ke = A.bf16(D)
    keb = Buf("keB")
    kssum_started = [False]

    def k_consume(tt, psum_ap, bankbufs):
        P.op("scalar", lambda s, p=psum_ap, ng=ctx["negc"]:
             s.activation(out=ke, in_=p, func=AF.Exp, bias=ng),
             reads=bankbufs, scalar_reads=[ctx["negcb"]], writes=[keb])
        st = not kssum_started[0]
        kssum_started[0] = True
        def sumfn(t, st=st, tt=tt):
            last = None
            for half in range(2):
                last = t.matmul(ctx["psum"][0:1, 512 * (6 + half):512 * (6 + half) + 512],
                                ctx["ones_colb"], ke[:, 512 * half:512 * (half + 1)],
                                start=st, stop=(tt == NT - 1))
            return last
        P.op("tensor", sumfn, reads=[keb, ctx["ones_colbb"]],
             writes=[pbufs[6], pbufs[7]])
        P.dma("sync", d["kh"][128 * tt:128 * (tt + 1), :], ke,
              reads=[keb], writes=[khb])

    emit_proj_formA(ctx, xn_chunks, xnbuf, wk, wkb, ND, D,
                    bias_row=bk_row, biasrowb=bkb, consume=k_consume, group=2)
    P.op("scalar", lambda s: s.activation(out=rk_row[:, 0:512],
                                          in_=ctx["psum"][0:1, 512 * 6:512 * 6 + 512],
                                          func=AF.Ln),
         reads=[pbufs[6]], writes=[rk_rowb])
    P.op("scalar", lambda s: s.activation(out=rk_row[:, 512:1024],
                                          in_=ctx["psum"][0:1, 512 * 7:512 * 7 + 512],
                                          func=AF.Ln),
         reads=[pbufs[7]], writes=[rk_rowb])
    P.op("scalar", lambda s: s.activation(out=rk_row, in_=rk_row, func=AF.Exp, scale=-1.0),
         reads=[rk_rowb], writes=[rk_rowb])
    P.dma("sync", d["row2"][0:1, :], rk_row, reads=[rk_rowb], writes=[rowsb])
    for c in range(ND):
        P.dma("sync", rcol[c],
              bass.AP(d["row2"].tensor, d["row2"].offset + 128 * c, [[1, 128], [1, 1]]),
              reads=[rowsb], writes=[rcolb])
    A.release()
    P.barrier()

    A.mark()
    wv, wvb = load_weight_fm(ctx, iw["B_wv"], ND, D)
    bv_row = A.bf16(D, parts=1)
    bvb = Buf("bvB")
    P.dma("gpsimd", bv_row, iw["B_bv"].unsqueeze(0), writes=[bvb])
    vstage = A.bf16(D)
    vstageb = Buf("vstB")

    def v_consume(tt, psum_ap, bankbufs):
        P.op("scalar", lambda s, p=psum_ap: s.activation(out=vstage, in_=p, func=AF.Copy),
             reads=bankbufs, writes=[vstageb])
        P.dma("sync", d["vh"][128 * tt:128 * (tt + 1), :], vstage,
              reads=[vstageb], writes=[vhb])

    emit_proj_formA(ctx, xn_chunks, xnbuf, wv, wvb, ND, D,
                    bias_row=bv_row, biasrowb=bvb, consume=v_consume, group=4)
    A.release()
    A.release()
    P.barrier()

    A.mark()
    kp2 = [A.bf16(NT * 128) for _ in range(2)]
    vp2 = [A.bf16(NT * 128) for _ in range(2)]
    qf2 = [A.bf16(T) for _ in range(2)]
    qpan = [A.bf16(T, parts=64) for _ in range(2)]
    kp2b = [Buf(f"kp2{i}") for i in range(2)]
    vp2b = [Buf(f"vp2{i}") for i in range(2)]
    qf2b = [Buf(f"qf2{i}") for i in range(2)]
    qpanb = [Buf(f"qpan{i}") for i in range(2)]
    state_sb = A.f32(128)
    state_bf = A.bf16(128)
    stateb = Buf("state")
    z_stage = A.bf16(T)
    z_stageb = Buf("zstage")

    for c in range(ND):
        sl = c % 2
        for part in range(4):
            src_k = bass.AP(d["kh"].tensor, d["kh"].offset + 1024 * part * D + 128 * c,
                            [[D, 128], [128 * D, 8], [1, 128]])
            src_v = bass.AP(d["vh"].tensor, d["vh"].offset + 1024 * part * D + 128 * c,
                            [[D, 128], [128 * D, 8], [1, 128]])
            P.dma("sync", kp2[sl][:, 1024 * part:1024 * (part + 1)],
                  src_k, reads=[khb], writes=[kp2b[sl]])
            P.dma("sync", vp2[sl][:, 1024 * part:1024 * (part + 1)],
                  src_v, reads=[vhb], writes=[vp2b[sl]])
        for slab in range(T // 512):
            P.dma("sync", qf2[sl][:, 512 * slab:512 * (slab + 1)],
                  d["qh"][512 * slab:512 * (slab + 1), 128 * c:128 * (c + 1)],
                  reads=[qhb], writes=[qf2b[sl]], transpose=True)
        P.dma("gpsimd", qpan[sl], qf2[sl][64:128, :], reads=[qf2b[sl]], writes=[qpanb[sl]])
        for half in range(2):
            h = 2 * c + half
            sbank = 4 + half
            def statefn(t, sl=sl, half=half, sbank=sbank):
                last = None
                for tt in range(NT):
                    last = t.matmul(ctx["psum"][0:64, 512 * sbank:512 * sbank + 64],
                                    kp2[sl][:, 128 * tt + 64 * half:128 * tt + 64 * (half + 1)],
                                    vp2[sl][:, 128 * tt + 64 * half:128 * tt + 64 * (half + 1)],
                                    start=(tt == 0), stop=(tt == NT - 1))
                return last
            P.op("tensor", statefn, reads=[kp2b[sl], vp2b[sl]],
                 writes=[pbufs[sbank]])
            P.op("vector", lambda v, half=half, c=c, sbank=sbank,
                 o=state_sb[0:64, 64 * half:64 * (half + 1)],
                 s1=rcol[c][64 * half:64 * (half + 1), 0:1]:
                 v.tensor_scalar_mul(out=o, in0=ctx["psum"][0:64, 512 * sbank:512 * sbank + 64],
                                     scalar1=s1),
                 reads=[pbufs[sbank]], scalar_reads=[rcolb], writes=[stateb])
            P.op("vector", lambda v, half=half,
                 o=state_bf[0:64, 64 * half:64 * (half + 1)],
                 i_=state_sb[0:64, 64 * half:64 * (half + 1)]:
                 v.tensor_copy(out=o, in_=i_),
                 reads=[stateb], writes=[stateb])
            rhs_src = qf2[sl] if half == 0 else qpan[sl]
            rhs_b = qf2b[sl] if half == 0 else qpanb[sl]
            for blk in range(NB):
                ob = blk % 4
                P.op("tensor", lambda t, half=half, blk=blk, ob=ob, rs=rhs_src:
                     t.matmul(ctx["psum"][0:64, 512 * ob:512 * ob + 512],
                              state_bf[0:64, 64 * half:64 * (half + 1)],
                              rs[0:64, 512 * blk:512 * (blk + 1)],
                              start=True, stop=True),
                     reads=[stateb, rhs_b], writes=[pbufs[ob]])
                P.op("vector", lambda v, half=half, blk=blk, ob=ob,
                     o=z_stage[64 * half:64 * (half + 1), 512 * blk:512 * (blk + 1)]:
                     v.tensor_copy(out=o, in_=ctx["psum"][0:64, 512 * ob:512 * ob + 512]),
                     reads=[pbufs[ob]], writes=[z_stageb])
        P.dma("sync", d["z"][128 * c:128 * (c + 1), :], z_stage,
              reads=[z_stageb], writes=[zb])
    A.release()
    P.barrier()

    A.mark()
    z_chunks = []
    zfmb = Buf("zfm")
    for c in range(ND):
        ap = A.bf16(T)
        z_chunks.append(ap)
        P.dma("sync", ap, d["z"][128 * c:128 * (c + 1), :], reads=[zb], writes=[zfmb])
    zsq = A.bf16(512)
    zsqb = Buf("zsqB")
    stats_sb = A.f32(512, parts=2)
    stats_sbb = Buf("statsB")
    stats_tm = A.f32(2 * NT)
    stats_tmb = Buf("statstmB")
    for blk in range(NB):
        for c in range(ND):
            P.op("scalar", lambda s, c=c, blk=blk, o=zsq:
                 s.activation(out=o, in_=z_chunks[c][:, 512 * blk:512 * (blk + 1)],
                              func=AF.Square),
                 reads=[zfmb], writes=[zsqb])
            def stfn(t, c=c, blk=blk):
                i1 = t.matmul(ctx["psum"][0:2, 512 * 7:512 * 7 + 512],
                              ctx["statpair"][:, 0:2],
                              z_chunks[c][:, 512 * blk:512 * (blk + 1)],
                              start=(c == 0), stop=False)
                i2 = t.matmul(ctx["psum"][0:2, 512 * 7:512 * 7 + 512],
                              ctx["statpair"][:, 2:4], zsq,
                              start=False, stop=(c == ND - 1))
                return i2
            P.op("tensor", stfn, reads=[zfmb, zsqb, ctx["statpairb"]],
                 writes=[pbufs[7]])
        P.op("vector", lambda v, blk=blk: v.tensor_copy(
             out=stats_sb[0:2, :], in_=ctx["psum"][0:2, 512 * 7:512 * 7 + 512]),
             reads=[pbufs[7]], writes=[stats_sbb])
        for q in range(4):
            tt = 4 * blk + q
            P.op("tensor", lambda t, q=q: t.transpose(
                 ctx["psum"][0:128, 512 * 6:512 * 6 + 2],
                 stats_sb[0:2, 128 * q:128 * (q + 1)], ctx["identf"][0:2, 0:2]),
                 reads=[stats_sbb, ctx["identfb"]], writes=[pbufs[6]])
            P.op("vector", lambda v, tt=tt, o=stats_tm[:, 2 * tt:2 * (tt + 1)]:
                 v.tensor_copy(out=o, in_=ctx["psum"][0:128, 512 * 6:512 * 6 + 2]),
                 reads=[pbufs[6]], writes=[stats_tmb])
    P.barrier()
    wo, wob = load_weight_fm(ctx, iw["B_wo"], ND, D)
    g_col = A.f32(ND)
    gb = Buf("gcolB")
    P.dma("sync", g_col, iw["B_g_col"], writes=[gb])
    be_col = A.f32(ND)
    beb = Buf("becolB")
    P.dma("sync", be_col, iw["B_b_col"], writes=[beb])
    gbf = A.bf16(ND)
    bbf = A.bf16(ND)
    gbfb = Buf("gbf")
    P.op("vector", lambda v: v.tensor_copy(out=gbf, in_=g_col), reads=[gb], writes=[gbfb])
    P.op("vector", lambda v: v.tensor_copy(out=bbf, in_=be_col), reads=[beb], writes=[gbfb])
    bo_row_f = A.f32(D, parts=1)
    bo_rowb = Buf("borowB")
    P.dma("sync", bo_row_f, iw["B_bo"].unsqueeze(0), writes=[bo_rowb])
    rowtmp = A.f32(D, parts=1)
    rowtmpb = Buf("rowtmp")
    def w1sumfn(t):
        last = None
        for c in range(ND):
            for half in range(2):
                last = t.matmul(ctx["psum"][0:1, 512 * (6 + half):512 * (6 + half) + 512],
                                gbf[:, c:c + 1], wo[c][:, 512 * half:512 * (half + 1)],
                                start=(c == 0), stop=(c == ND - 1))
        return last
    P.op("tensor", w1sumfn, reads=[wob, gbfb], writes=[pbufs[6], pbufs[7]])
    P.op("vector", lambda v: v.tensor_copy(out=rowtmp[:, 0:512],
                                           in_=ctx["psum"][0:1, 512 * 6:512 * 6 + 512]),
         reads=[pbufs[6]], writes=[rowtmpb])
    P.op("vector", lambda v: v.tensor_copy(out=rowtmp[:, 512:1024],
                                           in_=ctx["psum"][0:1, 512 * 7:512 * 7 + 512]),
         reads=[pbufs[7]], writes=[rowtmpb])
    P.dma("sync", d["row2"][0:1, :], rowtmp, reads=[rowtmpb], writes=[rowsb])
    def befn(t):
        last = None
        for c in range(ND):
            for half in range(2):
                last = t.matmul(ctx["psum"][0:1, 512 * (6 + half):512 * (6 + half) + 512],
                                bbf[:, c:c + 1], wo[c][:, 512 * half:512 * (half + 1)],
                                start=(c == 0), stop=(c == ND - 1))
        return last
    P.op("tensor", befn, reads=[wob, gbfb], writes=[pbufs[6], pbufs[7]])
    for half in range(2):
        P.op("vector", lambda v, half=half:
             v.tensor_add(out=rowtmp[:, 512 * half:512 * (half + 1)],
                          in0=ctx["psum"][0:1, 512 * (6 + half):512 * (6 + half) + 512],
                          in1=bo_row_f[:, 512 * half:512 * (half + 1)]),
             reads=[pbufs[6 + half], bo_rowb], writes=[rowtmpb])
    P.dma("sync", d["row2"][1:2, :], rowtmp, reads=[rowtmpb], writes=[rowsb])
    w1s_bc = A.f32(D)
    beff_bc = A.f32(D)
    bcb = Buf("bcB")
    P.dma("sync", w1s_bc, bass.AP(d["row2"].tensor, d["row2"].offset, [[0, 128], [1, D]]),
          reads=[rowsb], writes=[bcb])
    P.dma("sync", beff_bc, bass.AP(d["row2"].tensor, d["row2"].offset + D, [[0, 128], [1, D]]),
          reads=[rowsb], writes=[bcb])
    for c in range(ND):
        P.op("vector", lambda v, c=c, o=wo[c], s1=g_col[:, c:c + 1]:
             v.tensor_scalar_mul(out=o, in0=o, scalar1=s1),
             reads=[wob], scalar_reads=[gb], writes=[wob])
    xt = [A.f32(D) for _ in range(2)]
    xb_ = [Buf(f"bx{i}") for i in range(2)]
    x1 = [A.f32(D) for _ in range(2)]
    x1b = [Buf(f"bx1{i}") for i in range(2)]
    t1 = [A.f32(D) for _ in range(2)]
    t1b = [Buf(f"bt1{i}") for i in range(2)]
    dump = A.bf16(D)
    dumpb = Buf("bdump")
    cols = A.f32(16)
    colb = [Buf(f"bc{i}") for i in range(2)]
    lcols = A.f32(8)
    lcolb = [Buf(f"blc{i}") for i in range(2)]

    def final_consume(tt, psum_ap, bankbufs):
        sl = tt % 2
        cb = colb[sl]
        lcb = lcolb[sl]
        mu_c = stats_tm[:, 2 * tt:2 * tt + 1]
        msq_c = stats_tm[:, 2 * tt + 1:2 * tt + 2]
        v_c = lcols[:, 4 * sl + 0:4 * sl + 1]
        rst_c = lcols[:, 4 * sl + 1:4 * sl + 2]
        nrm_c = lcols[:, 4 * sl + 2:4 * sl + 3]
        P.op("scalar", lambda s: s.activation(out=v_c, in_=mu_c, func=AF.Square),
             reads=[stats_tmb], writes=[lcb])
        P.op("vector", lambda v: v.tensor_scalar(out=v_c, in0=msq_c, scalar1=v_c,
                                                 scalar2=None, op0=ALU.subtract),
             reads=[stats_tmb], scalar_reads=[lcb], writes=[lcb])
        P.op("scalar", lambda s, e=ctx["epscol"]:
             s.activation(out=rst_c, in_=v_c, func=AF.Ln, bias=e),
             reads=[lcb], scalar_reads=[ctx["epscolb"]], writes=[lcb])
        P.op("scalar", lambda s: s.activation(out=rst_c, in_=rst_c, func=AF.Exp, scale=-0.5),
             reads=[lcb], writes=[lcb])
        P.op("vector", lambda v: v.tensor_scalar(out=nrm_c, in0=mu_c, scalar1=rst_c,
                                                 scalar2=-1.0, op0=ALU.mult, op1=ALU.mult),
             reads=[stats_tmb], scalar_reads=[lcb], writes=[lcb])
        P.op("vector", lambda v, sl=sl, p=psum_ap, o=t1[sl], s1=rst_c:
             v.tensor_scalar_mul(out=o, in0=p, scalar1=s1),
             reads=bankbufs, scalar_reads=[lcb], writes=[t1b[sl]])
        P.op("vector", lambda v, sl=sl, o=t1[sl], s1=nrm_c:
             v.scalar_tensor_tensor(out=o, in0=w1s_bc, scalar=s1, in1=o,
                                    op0=ALU.mult, op1=ALU.add),
             reads=[bcb, t1b[sl]], scalar_reads=[lcb], writes=[t1b[sl]])
        P.op("vector", lambda v, sl=sl, o=t1[sl]:
             v.tensor_add(out=o, in0=o, in1=beff_bc),
             reads=[t1b[sl], bcb], writes=[t1b[sl]])
        P.dma("sync", xt[sl], x_src_dram[128 * tt:128 * (tt + 1), :], writes=[xb_[sl]])
        s_c = cols[:, 5 * sl + 0:5 * sl + 1]
        ss_c = cols[:, 5 * sl + 1:5 * sl + 2]
        P.op("vector", lambda v, sl=sl, o=s_c: v.reduce_sum(out=o, in_=t1[sl], axis=AX.X),
             reads=[t1b[sl]], writes=[cb])
        P.op("scalar", lambda s, sl=sl, o=dump, a=ss_c:
             s.activation(out=o, in_=t1[sl], func=AF.Square, accum_out=a),
             reads=[t1b[sl]], writes=[dumpb, cb])
        rstd_c, nmr_c = emit_ln_cols(ctx, cols, sl, cb)
        P.op("vector", lambda v, sl=sl, s1=nmr_c:
             v.tensor_scalar_add(out=xt[sl], in0=xt[sl], scalar1=s1),
             reads=[xb_[sl]], scalar_reads=[cb], writes=[xb_[sl]])
        P.op("vector", lambda v, sl=sl, o=x1[sl], s1=rstd_c:
             v.scalar_tensor_tensor(out=o, in0=t1[sl], scalar=s1, in1=xt[sl],
                                    op0=ALU.mult, op1=ALU.add),
             reads=[t1b[sl], xb_[sl]], scalar_reads=[cb], writes=[x1b[sl]])
        P.dma("sync", x_dst_dram[128 * tt:128 * (tt + 1), :], x1[sl], reads=[x1b[sl]])

    emit_proj_formA(ctx, z_chunks, zfmb, wo, wob, ND, D,
                    bias_row=None, biasrowb=None, consume=final_consume, group=4)
    A.release()
    A.release()
    P.barrier()


def emit_block_G(ctx, x_src_dram, x_dst_dram, iw):
    P, A, nc = ctx["P"], ctx["arena"], ctx["nc"]
    NE, PS = 64, 32
    NP = NE * PS
    d = ctx["dram"]
    pbufs = ctx["psum_bufs"]
    ddb, ccb2, ccinb, ccoutb = Buf("ddG"), Buf("ccG"), Buf("ccinG"), Buf("ccoutG")

    emit_ln_tm(ctx, x_src_dram, d["xn"])
    P.barrier()

    A.mark()
    A.mark()
    xn_chunks, xnbuf = load_xn_fm(ctx, d["xn"])
    phi, phib = load_weight_fm(ctx, iw["G_phi"], ND, NP)
    pb_row = A.bf16(NP, parts=1)
    pbb = Buf("pbG")
    P.dma("gpsimd", pb_row, iw["G_phib"].unsqueeze(0), writes=[pbb])
    eE = A.bf16(NP)
    eEb = Buf("eEG")
    dd_t = A.bf16(NP)
    dd_tb = Buf("ddt")
    cc_t = A.bf16(NP)
    cc_tb = Buf("cct")
    segD = A.f32(NE)
    segDb = Buf("segD")
    segC = A.f32(PS)
    segCb = Buf("segC")
    gcols = A.f32(2)
    gcolsb = Buf("gcols")

    def lg_consume(tt, psum_ap, bankbufs):
        m_c = gcols[:, 0:1]
        nm_c = gcols[:, 1:2]
        P.op("vector", lambda v, p=psum_ap, o=m_c: v.reduce_max(out=o, in_=p, axis=AX.X),
             reads=bankbufs, writes=[gcolsb])
        P.op("vector", lambda v: v.tensor_scalar_mul(out=nm_c, in0=m_c, scalar1=-1.0),
             reads=[gcolsb], writes=[gcolsb])
        P.op("scalar", lambda s, p=psum_ap: s.activation(out=eE, in_=p, func=AF.Exp,
                                                         bias=nm_c),
             reads=bankbufs, scalar_reads=[gcolsb], writes=[eEb])
        ev_d = eE[:].rearrange("p (n q) -> p n q", n=NE)
        P.op("vector", lambda v, o=segD[:, 0:NE], i_=ev_d:
             v.reduce_sum(out=o, in_=i_, axis=AX.X),
             reads=[eEb], writes=[segDb])
        P.op("scalar", lambda s: s.activation(out=segD[:, 0:NE], in_=segD[:, 0:NE],
                                              func=AF.Ln),
             reads=[segDb], writes=[segDb])
        P.op("scalar", lambda s: s.activation(out=segD[:, 0:NE], in_=segD[:, 0:NE],
                                              func=AF.Exp, scale=-1.0),
             reads=[segDb], writes=[segDb])
        P.op("vector", lambda v, o=dd_t[:].rearrange("p (n q) -> p n q", n=NE), i_=ev_d,
             b=segD[:, 0:NE].unsqueeze(2).broadcast_to([128, NE, PS]):
             v.tensor_tensor(out=o, in0=i_, in1=b, op=ALU.mult),
             reads=[eEb], scalar_reads=[segDb], writes=[dd_tb])
        ev_c = eE[:].rearrange("p (n q) -> p q n", n=NE)
        P.op("vector", lambda v, o=segC[:, 0:PS], i_=ev_c:
             v.reduce_sum(out=o, in_=i_, axis=AX.X),
             reads=[eEb], writes=[segCb])
        P.op("scalar", lambda s: s.activation(out=segC[:, 0:PS], in_=segC[:, 0:PS],
                                              func=AF.Ln),
             reads=[segCb], writes=[segCb])
        P.op("scalar", lambda s: s.activation(out=segC[:, 0:PS], in_=segC[:, 0:PS],
                                              func=AF.Exp, scale=-1.0),
             reads=[segCb], writes=[segCb])
        P.op("vector", lambda v, o=cc_t[:].rearrange("p (n q) -> p q n", n=NE), i_=ev_c,
             b=segC[:, 0:PS].unsqueeze(2).broadcast_to([128, PS, NE]):
             v.tensor_tensor(out=o, in0=i_, in1=b, op=ALU.mult),
             reads=[eEb], scalar_reads=[segCb], writes=[cc_tb])
        P.dma("sync", d["dd"][128 * tt:128 * (tt + 1), :], dd_t,
              reads=[dd_tb], writes=[ddb])
        P.dma("sync", d["cc"][128 * tt:128 * (tt + 1), :], cc_t,
              reads=[cc_tb], writes=[ccb2])

    emit_proj_formA(ctx, xn_chunks, xnbuf, phi, phib, ND, NP,
                    bias_row=pb_row, biasrowb=pbb, consume=lg_consume, group=2)
    A.release()
    P.barrier()

    A.mark()
    xn_tm = A.bf16(NT * D)
    xn_tmb = Buf("xntmG")
    for tt in range(NT):
        P.dma("sync", xn_tm[:, D * tt:D * (tt + 1)], d["xn"][128 * tt:128 * (tt + 1), :],
              writes=[xn_tmb])
    X_sb = A.f32(16 * D // 2)
    X_sb = A.f32(16 * D)
    X_sbb = Buf("XsbG")
    dslice = [A.bf16(512) for _ in range(2)]
    dsliceb = [Buf(f"dsl{i}") for i in range(2)]
    for ps in range(4):
        for tt in range(NT):
            sl = tt % 2
            P.dma("sync", dslice[sl], d["dd"][128 * tt:128 * (tt + 1), 512 * ps:512 * (ps + 1)],
                  reads=[ddb], writes=[dsliceb[sl]])
            def dfn(t, tt=tt, sl=sl):
                last = None
                for npc in range(4):
                    for half in range(2):
                        last = t.matmul(ctx["psum"][:, 512 * (2 * npc + half):512 * (2 * npc + half) + 512],
                                        dslice[sl][:, 128 * npc:128 * (npc + 1)],
                                        xn_tm[:, D * tt + 512 * half:D * tt + 512 * (half + 1)],
                                        start=(tt == 0), stop=(tt == NT - 1))
                return last
            P.op("tensor", dfn, reads=[dsliceb[sl], xn_tmb], writes=list(pbufs))
        for npc in range(4):
            npi = 4 * ps + npc
            P.op("vector", lambda v, npc=npc, npi=npi,
                 o=X_sb[:, D * npi:D * (npi + 1)]:
                 v.tensor_copy(out=o, in_=ctx["psum"][:, 1024 * npc:1024 * (npc + 1)]),
                 reads=[pbufs[2 * npc], pbufs[2 * npc + 1]], writes=[X_sbb])
    for npi in range(16):
        P.dma("sync", d["ccin"][128 * npi:128 * (npi + 1), :],
              X_sb[:, D * npi:D * (npi + 1)], reads=[X_sbb], writes=[ccinb])
    A.release()
    P.op("gpsimd", lambda g: g.collective_compute(
         "AllReduce", ALU.add, ins=[d["ccin"].opt()], outs=[d["ccout"].opt()],
         replica_groups=[[0, 1, 2, 3, 4, 5, 6, 7]]),
         reads=[ccinb], writes=[ccoutb])
    P.barrier()

    XtT = A.bf16(ND * NP)
    XtTb = Buf("XtT")
    Yt = A.bf16(16 * D)
    Ytb = Buf("Yt")
    A.mark()
    Xt = A.bf16(16 * D)
    Xtb = Buf("Xt")
    xrow = A.f32(D)
    xrowb = Buf("xrowG")
    cols = A.f32(16)
    colb = [Buf(f"gc{i}") for i in range(2)]
    dump = A.bf16(D)
    dumpb = Buf("gdump")
    for npi in range(16):
        sl = npi % 2
        cb = colb[sl]
        P.dma("sync", xrow, d["ccout"][128 * npi:128 * (npi + 1), :],
              reads=[ccoutb], writes=[xrowb])
        P.op("vector", lambda v: v.tensor_scalar_min(out=xrow, in0=xrow, scalar1=65000.0),
             reads=[xrowb], writes=[xrowb])
        P.op("vector", lambda v: v.tensor_scalar_max(out=xrow, in0=xrow, scalar1=-33000.0),
             reads=[xrowb], writes=[xrowb])
        s_c = cols[:, 5 * sl + 0:5 * sl + 1]
        ss_c = cols[:, 5 * sl + 1:5 * sl + 2]
        P.op("vector", lambda v, o=s_c: v.reduce_sum(out=o, in_=xrow, axis=AX.X),
             reads=[xrowb], writes=[cb])
        P.op("scalar", lambda s, o=dump, a=ss_c:
             s.activation(out=o, in_=xrow, func=AF.Square, accum_out=a),
             reads=[xrowb], writes=[dumpb, cb])
        rstd_c, nmr_c = emit_ln_cols(ctx, cols, sl, cb)
        P.op("vector", lambda v, s1=rstd_c, s2=nmr_c:
             v.tensor_scalar(out=xrow, in0=xrow, scalar1=s1, scalar2=s2,
                             op0=ALU.mult, op1=ALU.add),
             reads=[xrowb], scalar_reads=[cb], writes=[xrowb])
        P.op("scalar", lambda s, npi=npi, o=Xt[:, D * npi:D * (npi + 1)]:
             s.activation(out=o, in_=xrow, func=AF.Tanh),
             reads=[xrowb], writes=[Xtb])
    psbf = ctx["psum"].bitcast(BF16)
    for npi in range(16):
        for j in range(ND):
            P.op("tensor", lambda t, npi=npi, j=j:
                 t.transpose(psbf[:, 1024 * 5:1024 * 5 + 128],
                             Xt[:, D * npi + 128 * j:D * npi + 128 * (j + 1)],
                             ctx["ident"]),
                 reads=[Xtb, ctx["identb"]], writes=[pbufs[5]])
            P.op("vector", lambda v, npi=npi, j=j,
                 o=XtT[:, NP * j + 128 * npi:NP * j + 128 * (npi + 1)]:
                 v.tensor_copy(out=o, in_=psbf[:, 1024 * 5:1024 * 5 + 128]),
                 reads=[pbufs[5]], writes=[XtTb])
    A.release()
    P.barrier()

    A.mark()
    ewc = [A.bf16(D) for _ in range(2)]
    ewcb = [Buf(f"ewc{i}") for i in range(2)]
    ebrow = A.bf16(D, parts=1)
    ebrowb = Buf("ebrow")
    for e in range(NE):
        for j in range(ND):
            sl = (e * ND + j) % 2
            P.dma("gpsimd", ewc[sl], iw["G_ew"][e, 128 * j:128 * (j + 1), :],
                  writes=[ewcb[sl]])
            def efn(t, e=e, j=j, sl=sl):
                last = None
                for half in range(2):
                    last = t.matmul(ctx["psum"][0:PS, 512 * (6 + half):512 * (6 + half) + 512],
                                    XtT[:, NP * j + 32 * e:NP * j + 32 * (e + 1)],
                                    ewc[sl][:, 512 * half:512 * (half + 1)],
                                    start=(j == 0), stop=False)
                return last
            P.op("tensor", efn, reads=[XtTb, ewcb[sl]], writes=[pbufs[6], pbufs[7]])
        P.dma("gpsimd", ebrow, iw["G_eb"][e:e + 1, :], writes=[ebrowb])
        def ebfn(t, e=e):
            last = None
            for half in range(2):
                last = t.matmul(ctx["psum"][0:PS, 512 * (6 + half):512 * (6 + half) + 512],
                                ctx["ones_row"][:, 0:PS],
                                ebrow[:, 512 * half:512 * (half + 1)],
                                start=False, stop=True)
            return last
        P.op("tensor", ebfn, reads=[ebrowb, ctx["ones_rowb"]],
             writes=[pbufs[6], pbufs[7]])
        P.op("vector", lambda v, e=e,
             o=Yt[32 * (e % 4):32 * (e % 4 + 1), D * (e // 4):D * (e // 4 + 1)]:
             v.tensor_copy(out=o, in_=ctx["psum"][0:PS, 512 * 6:512 * 6 + 1024]),
             reads=[pbufs[6], pbufs[7]], writes=[Ytb])
    A.release()
    P.barrier()

    A.mark()
    cfm = [A.bf16(16 * 512) for _ in range(2)]
    cfmb = [Buf(f"cfm{i}") for i in range(2)]
    consume = emit_outer_ln_residual_consume(ctx, x_src_dram, x_dst_dram)
    for tg in range(8):
        sl = tg % 2
        for npi in range(16):
            P.dma("sync", cfm[sl][:, 512 * npi:512 * (npi + 1)],
                  d["cc"][512 * tg:512 * (tg + 1), 128 * npi:128 * (npi + 1)],
                  reads=[ccb2], writes=[cfmb[sl]], transpose=True)
        for tti in range(4):
            tt = 4 * tg + tti
            b0 = tti * 2
            def cfn(t, tti=tti, sl=sl, b0=b0):
                last = None
                for npi in range(16):
                    for half in range(2):
                        last = t.matmul(
                            ctx["psum"][:, 512 * (b0 + half):512 * (b0 + half) + 512],
                            cfm[sl][:, 512 * npi + 128 * tti:512 * npi + 128 * (tti + 1)],
                            Yt[:, D * npi + 512 * half:D * npi + 512 * (half + 1)],
                            start=(npi == 0), stop=(npi == 15))
                return last
            P.op("tensor", cfn, reads=[cfmb[sl], Ytb],
                 writes=[pbufs[b0], pbufs[b0 + 1]])
            consume(tt, ctx["psum"][:, 512 * b0:512 * b0 + D],
                    [pbufs[b0], pbufs[b0 + 1]])
    A.release()
    A.release()
    P.barrier()


def build_program(blocks=("A", "F", "B", "G"), debug_taps=()):
    import contextlib
    nc = bass.Bass("TRN2", target_bir_lowering=False, debug=False)
    stack = contextlib.ExitStack()

    iw = {}

    def ein(name, shape, dtype=F32):
        iw[name] = nc.dram_tensor(name, list(shape), dtype, kind="ExternalInput").ap()
        return iw[name]

    ein("x_in", (T, D))
    if "A" in blocks:
        ein("A_wq", (D, D)); ein("A_wk", (D, D)); ein("A_wv", (D, D)); ein("A_wo", (D, D))
        ein("A_bq_col", (128, ND)); ein("A_bk_col", (128, ND))
        ein("A_bv", (D,)); ein("A_bo", (D,))
    if "B" in blocks:
        ein("B_wq", (D, D)); ein("B_bq", (D,))
        ein("B_wk", (D, D)); ein("B_bk", (D,))
        ein("B_wv", (D, D)); ein("B_bv", (D,))
        ein("B_wo", (D, D)); ein("B_bo", (D,))
        ein("B_g_col", (128, ND)); ein("B_b_col", (128, ND))
    if "G" in blocks:
        ein("G_phi", (D, 2048)); ein("G_phib", (2048,))
        ein("G_ew", (64, D, D)); ein("G_eb", (64, D))
    if "F" in blocks:
        ein("F_qw", (D, 512)); ein("F_qb_col", (128, 4))
        ein("F_kw", (D, 512)); ein("F_kb_col", (128, 4))
        ein("F_vw", (D, 512)); ein("F_vb", (512,))
        ein("F_ow", (512, D)); ein("F_ob_col", (128, ND))
        ein("F_rw", (D, 8)); ein("F_rb", (8,))
        ein("F_w1", (8, D, 2048)); ein("F_b1_col", (8, 128, 16))
        ein("F_w2", (8, 2048, D)); ein("F_b2", (8, D))
    ein("c_ident", (128, 128))
    ein("c_ones_row", (1, 128))
    ein("c_onesf_row", (1, 128))
    ein("c_masks", (128, 2048))
    ein("c_ropecc", (128, T))
    ein("c_ropess", (128, T))
    ein("c_headsum2", (128, 2))
    ein("c_headsum4", (128, 4))
    ein("c_sel2", (2, 256))
    ein("c_epscol", (128, 1))
    ein("c_negc", (128, 1))
    ein("c_statpair", (128, 4))
    ein("c_identf", (128, 128))
    ein("c_onescol", (128, 1))
    ein("c_e8", (8, 1024))

    out = nc.dram_tensor("x_out", [T, D], F32, kind="ExternalOutput").ap()

    dram = {}
    for nm, shape, dt in [
        ("xn", (T, D), BF16), ("q", (D, T), BF16), ("k", (D, T), BF16),
        ("q0", (D, T), BF16),
        ("edbg", (4 * 128, 512), BF16), ("ddbg", (4, 512), F32),
        ("bdbg", (128, 16), F32), ("mdbg", (2, 16), F32),
        ("negd", (2, 8), F32), ("rrow", (1, 512), F32),
        ("negd2", (4, 4), F32), ("ao", (512, T), BF16),
        ("mla", (D, T), BF16), ("moe", (T, D), F32),
        ("qh", (T, D), BF16), ("kh", (T, D), BF16), ("vh", (T, D), BF16),
        ("z", (D, T), BF16), ("row2", (2, D), F32),
        ("dd", (T, 2048), BF16), ("cc", (T, 2048), BF16),
        ("denh", (16, T), F32), ("ralla", (16, T), F32), ("dsnap", (16, T), F32),
        ("ctdbg", (8, T), BF16), ("rldbg", (T, 8), F32), ("rp_dbg", (T, 24), F32),
        ("v", (T, 16 * 65), BF16), ("o", (D, T), BF16),
        ("x1", (T, D), F32), ("x2", (T, D), F32), ("x3", (T, D), F32),
    ]:
        dram[nm] = nc.dram_tensor("scr_" + nm, list(shape), dt, kind="Internal").ap()

    dram["ccin"] = nc.dram_tensor("ccin", [2048, D], F32, kind="Internal").ap()
    dram["ccout"] = nc.dram_tensor("ccout", [2048, D], F32, kind="Internal").ap()
    P = Prog(nc)
    ctx = {"nc": nc, "P": P, "dram": dram,
           "psum_bufs": [Buf(f"psum{i}") for i in range(8)],
           "negd2b": Buf("negd2"), "rrowb": Buf("rrow"), "negdb": Buf("negd")}

    with stack:
        arena = Arena(nc, stack)
        ctx["arena"] = arena
        psum = stack.enter_context(nc.psum_tensor("psum", [128, 4096], F32))
        ctx["psum"] = psum[:]

        def const_sb(name, cols, parts=128, f32=False):
            ap = arena.f32(cols, parts=parts) if f32 else arena.bf16(cols, parts=parts)
            b = Buf(name)
            q = "gpsimd" if not f32 else "sync"
            P.dma(q, ap, iw[name], writes=[b])
            b.const = True
            return ap, b

        ctx["ident"], ctx["identb"] = const_sb("c_ident", 128)
        ctx["ones_row"], ctx["ones_rowb"] = const_sb("c_ones_row", 128, parts=1)
        ctx["ones_col_f"], ctx["ones_col_fb"] = const_sb("c_onesf_row", 128, parts=1, f32=True)
        mm_ap = arena.bf16(4 * 512)
        mb = Buf("masks")
        P.dma("gpsimd", mm_ap, iw["c_masks"], writes=[mb])
        mb.const = True
        ctx["masks"] = [mm_ap[:, 512 * i:512 * (i + 1)] for i in range(4)]
        ctx["masksb"] = mb
        ctx["headsum2"], ctx["headsum2b"] = const_sb("c_headsum2", 2)
        ctx["headsum4"], ctx["headsum4b"] = const_sb("c_headsum4", 4)
        ctx["sel2"], ctx["sel2b"] = const_sb("c_sel2", 256, parts=2, f32=True)
        ctx["epscol"], ctx["epscolb"] = const_sb("c_epscol", 1, f32=True)
        ctx["negc"], ctx["negcb"] = const_sb("c_negc", 1, f32=True)
        ctx["statpair"], ctx["statpairb"] = const_sb("c_statpair", 4)
        ctx["identf"], ctx["identfb"] = const_sb("c_identf", 128, f32=True)
        ctx["ones_colb"], ctx["ones_colbb"] = const_sb("c_onescol", 1)
        ctx["e8"], ctx["e8b"] = const_sb("c_e8", 1024, parts=8)

        cur = iw["x_in"]
        scratch_x = [dram["x1"], dram["x2"], dram["x3"]]
        for bi, b in enumerate(blocks):
            dst = out if bi == len(blocks) - 1 else scratch_x[bi]
            if b == "A":
                emit_block_A(ctx, cur, dst, iw)
            elif b == "F":
                emit_block_F(ctx, cur, dst, iw)
            elif b == "B":
                emit_block_B(ctx, cur, dst, iw)
            elif b == "G":
                emit_block_G(ctx, cur, dst, iw)
            else:
                raise NotImplementedError(b)
            cur = dst
        P.barrier()
        for nm in debug_taps:
            src = dram[nm]
            tap = nc.dram_tensor("tap_" + nm, list(src.shape), src.dtype,
                                 kind="ExternalOutput").ap()
            P.dma("sync", tap, src)
        P.barrier()
        P.finalize_and_emit(stack)
    return nc


def _bias_col(b):
    return np.ascontiguousarray(np.asarray(b, dtype=np.float32).reshape(-1, 128).T)


def _rope_perm_np(H=16, HD=64):
    p = []
    for h in range(H):
        p += [h * HD + 2 * i for i in range(HD // 2)]
        p += [h * HD + 2 * i + 1 for i in range(HD // 2)]
    return np.array(p, dtype=np.int64)


def make_consts():
    c = {}
    c["c_ident"] = np.eye(128, dtype=np.float32)
    c["c_ones_row"] = np.ones((1, 128), dtype=np.float32)
    c["c_onesf_row"] = np.ones((1, 128), dtype=np.float32)
    masks = []
    for off in (0, 128, 256, 384):
        kk = np.arange(128)[:, None]
        qq = np.arange(512)[None, :]
        masks.append(((kk + off) <= qq).astype(np.float32))
    c["c_masks"] = np.concatenate(masks, 1)
    theta = 10000.0 ** (-np.arange(0, 64, 2, dtype=np.float64) / 64)
    fr = np.outer(theta, np.arange(T, dtype=np.float64))
    sin, cos = np.sin(fr), np.cos(fr)
    cc_blk = np.concatenate([cos, cos], 0)
    ss_blk = np.concatenate([-sin, sin], 0)
    c["c_ropecc"] = np.tile(cc_blk, (2, 1)).astype(np.float32)
    c["c_ropess"] = np.tile(ss_blk, (2, 1)).astype(np.float32)
    hs2 = np.zeros((128, 2), dtype=np.float32)
    hs2[:64, 0] = 1.0
    hs2[64:, 1] = 1.0
    c["c_headsum2"] = hs2
    hs4 = np.zeros((128, 4), dtype=np.float32)
    for j in range(4):
        hs4[32 * j:32 * (j + 1), j] = 1.0
    c["c_headsum4"] = hs4
    sel = np.zeros((2, 256), dtype=np.float32)
    sel[0, :128] = 1.0
    sel[1, 128:] = 1.0
    c["c_sel2"] = sel
    c["c_epscol"] = np.full((128, 1), 1e-5, dtype=np.float32)
    c["c_negc"] = np.full((128, 1), -KSOFT_C, dtype=np.float32)
    sp = np.zeros((128, 4), dtype=np.float32)
    sp[:, 0] = 1.0 / D
    sp[:, 3] = 1.0 / D
    c["c_statpair"] = sp
    c["c_identf"] = np.eye(128, dtype=np.float32)
    c["c_onescol"] = np.ones((128, 1), dtype=np.float32)
    e8 = np.zeros((8, 1024), dtype=np.float32)
    for e in range(8):
        e8[e, 128 * e:128 * (e + 1)] = 1.0
    c["c_e8"] = e8
    return c


def prep_inputs_A(inputs):
    perm = _rope_perm_np()
    m = {}
    m["A_wq"] = np.ascontiguousarray(np.asarray(inputs["L0_attn_Wq"], np.float32).T[:, perm])
    m["A_wk"] = np.ascontiguousarray(np.asarray(inputs["L0_attn_Wk"], np.float32).T[:, perm])
    m["A_wv"] = np.ascontiguousarray(np.asarray(inputs["L0_attn_Wv"], np.float32).T)
    m["A_wo"] = np.ascontiguousarray(np.asarray(inputs["L0_attn_Wo"], np.float32).T)
    m["A_bq_col"] = _bias_col(np.asarray(inputs["L0_attn_bq"])[perm])
    m["A_bk_col"] = _bias_col(np.asarray(inputs["L0_attn_bk"])[perm])
    m["A_bv"] = np.ascontiguousarray(np.asarray(inputs["L0_attn_bv"], np.float32))
    m["A_bo"] = np.ascontiguousarray(np.asarray(inputs["L0_attn_bo"], np.float32))
    return m


def prep_inputs_F(inputs):
    m = {}
    qw = np.asarray(inputs["L0_ffn_q_w"], np.float32)
    used = np.concatenate([np.arange(64 * h, 64 * h + 32) for h in range(16)])
    m["F_qw"] = np.ascontiguousarray(qw[:, used])
    qb = np.asarray(inputs["L0_ffn_q_b"], np.float32)[used]
    m["F_qb_col"] = _bias_col(qb)
    m["F_kw"] = np.ascontiguousarray(np.asarray(inputs["L0_ffn_k_w"], np.float32))
    m["F_kb_col"] = _bias_col(np.asarray(inputs["L0_ffn_k_b"], np.float32))
    m["F_vw"] = np.ascontiguousarray(np.asarray(inputs["L0_ffn_v_w"], np.float32))
    m["F_vb"] = np.ascontiguousarray(np.asarray(inputs["L0_ffn_v_b"], np.float32))
    m["F_ow"] = np.ascontiguousarray(np.asarray(inputs["L0_ffn_o_w"], np.float32)[:512])
    m["F_ob_col"] = _bias_col(np.asarray(inputs["L0_ffn_o_b"], np.float32))
    m["F_rw"] = np.ascontiguousarray(np.asarray(inputs["L0_ffn_router_w"], np.float32))
    m["F_rb"] = np.ascontiguousarray(np.asarray(inputs["L0_ffn_router_b"], np.float32))
    m["F_w1"] = np.ascontiguousarray(np.asarray(inputs["L0_ffn_w1"], np.float32))
    b1 = np.asarray(inputs["L0_ffn_b1"], np.float32)
    m["F_b1_col"] = np.ascontiguousarray(b1.reshape(8, 16, 128).transpose(0, 2, 1))
    m["F_w2"] = np.ascontiguousarray(np.asarray(inputs["L0_ffn_w2"], np.float32))
    m["F_b2"] = np.ascontiguousarray(np.asarray(inputs["L0_ffn_b2"], np.float32))
    return m


def prep_inputs_B(inputs):
    m = {}
    for nm, w in (("wq", "Wq"), ("wk", "Wk"), ("wv", "Wv"), ("wo", "Wo")):
        m["B_" + nm] = np.ascontiguousarray(
            np.asarray(inputs["L1_attn_" + w], np.float32).T)
    for nm, b in (("bq", "bq"), ("bk", "bk"), ("bv", "bv"), ("bo", "bo")):
        m["B_" + nm] = np.ascontiguousarray(np.asarray(inputs["L1_attn_" + b], np.float32))
    m["B_g_col"] = _bias_col(np.asarray(inputs["L1_attn_ln_g"], np.float32))
    m["B_b_col"] = _bias_col(np.asarray(inputs["L1_attn_ln_b"], np.float32))
    return m


def prep_inputs_G(inputs):
    m = {}
    m["G_phi"] = np.ascontiguousarray(np.asarray(inputs["L1_ffn_phi_w"], np.float32).T)
    m["G_phib"] = np.ascontiguousarray(np.asarray(inputs["L1_ffn_phi_b"], np.float32))
    m["G_ew"] = np.ascontiguousarray(np.asarray(inputs["L1_ffn_expert_w"], np.float32))
    m["G_eb"] = np.ascontiguousarray(np.asarray(inputs["L1_ffn_expert_b"], np.float32))
    return m


def run_blocks(inputs, blocks=("A",), x_override=None, debug_taps=(), raw=False):
    nc = build_program(blocks=blocks, debug_taps=debug_taps)
    shared = make_consts()
    shared.update(prep_inputs_A(inputs))
    shared.update(prep_inputs_F(inputs))
    shared.update(prep_inputs_B(inputs))
    shared.update(prep_inputs_G(inputs))
    x = np.asarray(x_override if x_override is not None else inputs["L0_attn_x"],
                   dtype=np.float32)
    in_maps = []
    for b in range(8):
        im = dict(shared)
        im["x_in"] = np.ascontiguousarray(x[b])
        in_maps.append(im)
    res = run_bass_kernel_spmd(nc, in_maps, core_ids=list(range(8)))
    if raw:
        return res
    return np.stack([r["x_out"] for r in res.results], 0)


def kernel(**inputs):
    return run_blocks(inputs, blocks=("A", "F", "B", "G"))


if __name__ == "__main__":
    pass


# version 04
# speedup vs baseline: 1.1008x; 1.1008x over previous
import math
import numpy as np
import ml_dtypes
import concourse.bass as bass
import concourse.mybir as mybir
from concourse.bass_utils import run_bass_kernel_spmd

F32 = mybir.dt.float32
BF16 = mybir.dt.bfloat16
AF = mybir.ActivationFunctionType
ALU = mybir.AluOpType
AX = mybir.AxisListType

ENGINES = ("tensor", "vector", "scalar", "gpsimd", "sync")
DMA_QUEUES = ("sync", "gpsimd", "scalar")

T = 4096
D = 1024
NT = T // 128
ND = D // 128
NB = T // 512
LN_EPS = 1e-5
DBG = False
DBG2 = False


class Buf:
    __slots__ = ("name", "writers", "readers", "const")

    def __init__(self, name="", const=False):
        self.name = name
        self.writers = []
        self.readers = []
        self.const = const


class _Op:
    __slots__ = ("idx", "eng", "fn", "deps", "dma", "needs_inc", "sem", "val", "tag",
                 "small")

    def __init__(self, idx, eng, fn, dma):
        self.tag = ""
        self.small = False
        self.idx = idx
        self.eng = eng
        self.fn = fn
        self.dma = dma
        self.deps = {}
        self.needs_inc = False
        self.sem = None
        self.val = None


class Prog:
    N_DMA_SEMS = 20
    N_EXEC_SEMS = 4

    def __init__(self, nc):
        self.nc = nc
        self.ops = []
        self.eng_ops = {e: [] for e in ENGINES}
        from collections import deque
        self.recent_dma = {e: deque(maxlen=self.N_DMA_SEMS) for e in ENGINES}

    def op(self, eng, fn, reads=(), scalar_reads=(), writes=(), dma=False):
        o = _Op(len(self.ops), eng, fn, dma)
        for b in reads:
            for w in b.writers:
                o.deps.setdefault(w, False)
        for b in scalar_reads:
            for w in b.writers:
                o.deps[w] = True
        for b in writes:
            for w in b.writers:
                o.deps.setdefault(w, False)
            for r in b.readers:
                o.deps.setdefault(r, False)
        o.deps.pop(o.idx, None)
        self.ops.append(o)
        self.eng_ops[eng].append(o)
        if dma:
            self.recent_dma[eng].append(o.idx)
        for b in reads:
            if not b.const:
                b.readers.append(o.idx)
        for b in scalar_reads:
            if not b.const:
                b.readers.append(o.idx)
        for b in writes:
            b.readers = []
            b.writers.append(o.idx)
        return o.idx

    def dma(self, queue, out_ap, in_ap, reads=(), writes=(), transpose=False, tag=""):
        if transpose:
            fn = lambda e, o=out_ap, i=in_ap: e.dma_start_transpose(o, i)
        else:
            fn = lambda e, o=out_ap, i=in_ap: e.dma_start(out=o, in_=i)
        i = self.op(queue, fn, reads=reads, writes=writes, dma=True)
        self.ops[i].tag = tag
        try:
            self.ops[i].small = out_ap.nbytes() <= 32 * 1024
        except Exception:
            self.ops[i].small = True
        return i

    def barrier(self):
        lasts = [self.eng_ops[e][-1].idx for e in ENGINES if self.eng_ops[e]]
        for e in ENGINES:
            lasts.extend(self.recent_dma[e])
        lasts = sorted(set(lasts))
        for e in ENGINES:
            o = _Op(len(self.ops), e, None, False)
            for dd in lasts:
                o.deps[dd] = False
            o.deps.pop(o.idx, None)
            self.ops.append(o)
            self.eng_ops[e].append(o)

    def finalize_and_emit(self, stack):
        nc = self.nc
        for o in self.ops:
            if o.dma:
                o.needs_inc = True
            for dd in o.deps:
                self.ops[dd].needs_inc = True
        exec_sems = {e: [stack.enter_context(nc.semaphore(f"x_{e}_{i}"))
                         for i in range(self.N_EXEC_SEMS)] for e in ENGINES}
        dma_sems = {q: [stack.enter_context(nc.semaphore(f"d_{q}_{i}"))
                        for i in range(self.N_DMA_SEMS)] for q in DMA_QUEUES}
        for e in ENGINES:
            nexec = 0
            ndma = 0
            for o in self.eng_ops[e]:
                if not o.needs_inc:
                    continue
                if o.dma:
                    pool = dma_sems[e]
                    o.sem = pool[ndma % len(pool)]
                    o.val = 16 * (ndma // len(pool) + 1)
                    ndma += 1
                else:
                    pool = exec_sems[e]
                    o.sem = pool[nexec % len(pool)]
                    o.val = nexec // len(pool) + 1
                    nexec += 1
        waited = {e: {} for e in ENGINES}
        MAX_DMA_INFLIGHT = 20

        def run_engine(eng_obj, ename):
            w = waited[ename]
            my_dmas = [o for o in self.eng_ops[ename] if o.dma]
            dma_idx = {o.idx: i for i, o in enumerate(my_dmas)}
            for o in self.eng_ops[ename]:
                tgt = {}
                for dd, force in o.deps.items():
                    p = self.ops[dd]
                    if p.sem is None:
                        continue
                    k = p.sem.name
                    if tgt.get(k, (None, -1))[1] < p.val:
                        tgt[k] = (p.sem, p.val)
                if o.dma:
                    i = dma_idx[o.idx]
                    gate = MAX_DMA_INFLIGHT
                    if i >= gate:
                        p = my_dmas[i - gate]
                        k = p.sem.name
                        if tgt.get(k, (None, -1))[1] < p.val:
                            tgt[k] = (p.sem, p.val)
                for k in sorted(tgt):
                    sem, val = tgt[k]
                    if w.get(k, 0) < val:
                        eng_obj.wait_ge(sem, val)
                        w[k] = val
                inst = o.fn(eng_obj) if o.fn is not None else eng_obj.nop()
                if o.needs_inc:
                    inst.then_inc(o.sem, 16 if o.dma else 1)

        with nc.Block() as block:
            block.tensor(lambda t: run_engine(t, "tensor"))
            block.vector(lambda v: run_engine(v, "vector"))
            block.scalar(lambda s: run_engine(s, "scalar"))
            block.gpsimd(lambda g: run_engine(g, "gpsimd"))
            block.sync(lambda y: run_engine(y, "sync"))


class Arena:
    def __init__(self, nc, stack, cols=88 * 1024):
        self.t = stack.enter_context(nc.sbuf_tensor("arena", [128, cols], BF16))
        self.cols = cols
        self.off = 0
        self.marks = []
        self.peak = 0

    def bf16(self, cols, parts=128):
        alloc = (cols + 15) & ~15
        assert self.off + alloc <= self.cols, f"arena overflow {self.off}+{alloc}"
        ap = self.t[0:parts, self.off:self.off + cols]
        self.off += alloc
        self.peak = max(self.peak, self.off)
        return ap

    def f32(self, cols, parts=128):
        return self.bf16(2 * cols, parts).bitcast(F32)

    def mark(self):
        self.marks.append(self.off)

    def release(self):
        self.off = self.marks.pop()


def _bank(ctx, i, n=1):
    return ctx["psum"][:, 512 * i:512 * (i + n)]


def emit_ln_cols(ctx, cols, sl, cb):
    P = ctx["P"]
    s_c = cols[:, 5 * sl + 0:5 * sl + 1]
    ss_c = cols[:, 5 * sl + 1:5 * sl + 2]
    t1_c = cols[:, 5 * sl + 2:5 * sl + 3]
    rstd_c = cols[:, 5 * sl + 3:5 * sl + 4]
    nmr_c = cols[:, 5 * sl + 4:5 * sl + 5]
    P.op("scalar", lambda s, i_=s_c, o=t1_c:
         s.activation(out=o, in_=i_, func=AF.Square, scale=float(D) ** -0.5),
         reads=[cb], writes=[cb])
    P.op("vector", lambda v, i_=ss_c, o=t1_c, s1=t1_c:
         v.tensor_scalar(out=o, in0=i_, scalar1=s1, scalar2=1.0 / D,
                         op0=ALU.subtract, op1=ALU.mult),
         reads=[cb], scalar_reads=[cb], writes=[cb])
    P.op("scalar", lambda s, i_=t1_c, o=rstd_c, e=ctx["epscol"]:
         s.activation(out=o, in_=i_, func=AF.Ln, bias=e),
         reads=[cb], scalar_reads=[ctx["epscolb"]], writes=[cb])
    P.op("scalar", lambda s, i_=rstd_c, o=rstd_c:
         s.activation(out=o, in_=i_, func=AF.Exp, scale=-0.5),
         reads=[cb], writes=[cb])
    P.op("vector", lambda v, i_=s_c, o=nmr_c, s1=rstd_c:
         v.tensor_scalar(out=o, in0=i_, scalar1=s1, scalar2=-1.0 / D,
                         op0=ALU.mult, op1=ALU.mult),
         reads=[cb], scalar_reads=[cb], writes=[cb])
    return rstd_c, nmr_c


def emit_ln_tm(ctx, x_src_dram, xn_dram):
    P, A = ctx["P"], ctx["arena"]
    A.mark()
    xt = [A.f32(D) for _ in range(2)]
    xb = [Buf(f"lnx{i}") for i in range(2)]
    xn = [A.bf16(D) for _ in range(2)]
    xnb = [Buf(f"lnxn{i}") for i in range(2)]
    dump = A.bf16(D)
    dumpb = Buf("lndump")
    cols = A.f32(16)
    colb = [Buf(f"lncol{i}") for i in range(2)]
    for i in range(NT):
        sl = i % 2
        x_ap, xbuf = xt[sl], xb[sl]
        xn_ap, xnbuf = xn[sl], xnb[sl]
        cb = colb[sl]
        P.dma("sync", x_ap, x_src_dram[128 * i:128 * (i + 1), :], writes=[xbuf])
        s_c = cols[:, 5 * sl + 0:5 * sl + 1]
        ss_c = cols[:, 5 * sl + 1:5 * sl + 2]
        P.op("vector", lambda v, x=x_ap, o=s_c: v.reduce_sum(out=o, in_=x, axis=AX.X),
             reads=[xbuf], writes=[cb])
        P.op("scalar", lambda s, x=x_ap, o=dump, a=ss_c:
             s.activation(out=o, in_=x, func=AF.Square, accum_out=a),
             reads=[xbuf], writes=[dumpb, cb])
        rstd_c, nmr_c = emit_ln_cols(ctx, cols, sl, cb)
        P.op("vector", lambda v, x=x_ap, o=xn_ap, s1=rstd_c, s2=nmr_c:
             v.tensor_scalar(out=o, in0=x, scalar1=s1, scalar2=s2,
                             op0=ALU.mult, op1=ALU.add),
             reads=[xbuf], scalar_reads=[cb], writes=[xnbuf])
        P.dma("sync", xn_dram[128 * i:128 * (i + 1), :], xn_ap, reads=[xnbuf])
    A.release()


def load_weight_fm(ctx, w_dram, n_din, dout_cols):
    P, A = ctx["P"], ctx["arena"]
    wb = Buf("wfm")
    chunks = []
    for c in range(n_din):
        ap = A.bf16(dout_cols)
        chunks.append(ap)
        P.dma("gpsimd", ap, w_dram[128 * c:128 * (c + 1), :], writes=[wb])
    return chunks, wb


def load_xn_fm(ctx, xn_dram, n_chunks=ND, reads=()):
    P, A = ctx["P"], ctx["arena"]
    xb = Buf("xnfm")
    chunks = []
    for c in range(n_chunks):
        ap = A.bf16(T)
        chunks.append(ap)
        for sl in range(T // 512):
            P.dma("sync", ap[:, 512 * sl:512 * (sl + 1)],
                  xn_dram[512 * sl:512 * (sl + 1), 128 * c:128 * (c + 1)],
                  reads=reads, writes=[xb], transpose=True)
    return chunks, xb


def emit_maxsq_blk(ctx, src_ap, srcbuf, sq, sqb, maxsq, col, init):
    P = ctx["P"]
    pbufs = ctx["psum_bufs"]
    nh = maxsq["nh"]
    P.op("scalar", lambda s, i_=src_ap, o=sq:
         s.activation(out=o, in_=i_, func=AF.Square),
         reads=[srcbuf], writes=[sqb])
    P.op("tensor", lambda t, o=ctx["psum"][0:nh, 512 * 7:512 * 7 + 512], hs=maxsq["hs"]:
         t.matmul(o, hs, sq, start=True, stop=True),
         reads=[sqb, maxsq["hsb"]], writes=[pbufs[7]])
    if init:
        P.op("vector", lambda v, o=maxsq["ap"][0:nh, col:col + 1], nh=nh:
             v.reduce_max(out=o, in_=ctx["psum"][0:nh, 512 * 7:512 * 7 + 512], axis=AX.X),
             reads=[pbufs[7]], writes=[maxsq["buf"]])
    else:
        P.op("vector", lambda v, o=maxsq["tmpcol"], nh=nh:
             v.reduce_max(out=o, in_=ctx["psum"][0:nh, 512 * 7:512 * 7 + 512], axis=AX.X),
             reads=[pbufs[7]], writes=[maxsq["tmpbuf"]])
        P.op("vector", lambda v, o=maxsq["ap"][0:nh, col:col + 1], b=maxsq["tmpcol"], nh=nh:
             v.tensor_tensor(out=o, in0=o, in1=b[0:nh, :], op=ALU.max),
             reads=[maxsq["buf"], maxsq["tmpbuf"]], writes=[maxsq["buf"]])


def emit_proj_formB(ctx, w_chunks, wbuf, xn_chunks, xnbuf, out_dram, out_dramb,
                    n_din, n_dout, bias_col=None, biasbuf=None, maxsq=None):
    P, A = ctx["P"], ctx["arena"]
    A.mark()
    stage = A.bf16(T)
    stageb = Buf("pjstage")
    if maxsq is not None:
        sq = A.bf16(512)
        sqb = Buf("pjsq")
    pbufs = ctx["psum_bufs"]
    for j in range(n_dout):
        for blk in range(NB):
            pb = blk % 4

            def mmfn(t, j=j, blk=blk, pb=pb):
                last = None
                for c in range(n_din):
                    last = t.matmul(_bank(ctx, pb),
                                    w_chunks[c][:, 128 * j:128 * (j + 1)],
                                    xn_chunks[c][:, 512 * blk:512 * (blk + 1)],
                                    start=(c == 0), stop=(c == n_din - 1))
                return last
            P.op("tensor", mmfn, reads=[wbuf, xnbuf], writes=[pbufs[pb]])
            if bias_col is not None:
                P.op("scalar", lambda s, pb=pb, o=stage[:, 512 * blk:512 * (blk + 1)],
                     b=bias_col[:, j:j + 1]:
                     s.activation(out=o, in_=_bank(ctx, pb), func=AF.Identity, bias=b),
                     reads=[pbufs[pb]], scalar_reads=[biasbuf], writes=[stageb])
            else:
                P.op("scalar", lambda s, pb=pb, o=stage[:, 512 * blk:512 * (blk + 1)]:
                     s.activation(out=o, in_=_bank(ctx, pb), func=AF.Copy),
                     reads=[pbufs[pb]], writes=[stageb])
        if maxsq is not None:
            for blk in range(NB):
                emit_maxsq_blk(ctx, stage[:, 512 * blk:512 * (blk + 1)], stageb,
                               sq, sqb, maxsq, j, blk == 0)
        P.dma("sync", out_dram[128 * j:128 * (j + 1), :], stage, reads=[stageb],
              writes=[out_dramb])
    A.release()


def emit_rope_phase(ctx, qk_dram, qkbuf, maxsq, iw):
    P, A = ctx["P"], ctx["arena"]
    A.mark()
    stage = A.bf16(T)
    stageb = Buf("rpstage")
    swap = A.bf16(T)
    swapb = Buf("rpswap")
    sq = A.bf16(512)
    sqb = Buf("rpsq")
    ropecc = A.bf16(T)
    ropess = A.bf16(T)
    ropeb = Buf("rope")
    P.dma("gpsimd", ropecc, iw["c_ropecc"], writes=[ropeb])
    P.dma("gpsimd", ropess, iw["c_ropess"], writes=[ropeb])
    ropeb.const = True
    ctx["ropecc"], ctx["ropeccb"] = ropecc, ropeb
    ctx["ropess"], ctx["ropessb"] = ropess, ropeb
    for c in range(ND):
        P.dma("sync", stage, qk_dram[128 * c:128 * (c + 1), :],
              reads=[qkbuf], writes=[stageb])
        for i in range(4):
            src = (1, 0, 3, 2)[i]
            P.dma("gpsimd", swap[32 * i:32 * (i + 1), :],
                  qk_dram[128 * c + 32 * src:128 * c + 32 * (src + 1), :],
                  reads=[qkbuf], writes=[swapb])
        P.op("vector", lambda v, s_=stage, cc=ctx["ropecc"]:
             v.tensor_mul(out=s_, in0=s_, in1=cc),
             reads=[stageb, ctx["ropeccb"]], writes=[stageb])
        P.op("vector", lambda v, s_=swap, ss=ctx["ropess"]:
             v.tensor_mul(out=s_, in0=s_, in1=ss),
             reads=[swapb, ctx["ropessb"]], writes=[swapb])
        P.op("vector", lambda v, a=stage, b=swap:
             v.tensor_add(out=a, in0=a, in1=b),
             reads=[stageb, swapb], writes=[stageb])
        for blk in range(NB):
            emit_maxsq_blk(ctx, stage[:, 512 * blk:512 * (blk + 1)], stageb,
                           sq, sqb, maxsq, c, blk == 0)
        P.dma("sync", qk_dram[128 * c:128 * (c + 1), :], stage,
              reads=[stageb], writes=[qkbuf])
    A.release()


def emit_proj_formA(ctx, xn_chunks, xnbuf, w_chunks, wbuf, n_din, dout,
                    bias_row=None, biasrowb=None, consume=None, group=4,
                    extra_mm_reads=()):
    P = ctx["P"]
    pbufs = ctx["psum_bufs"]
    nbk = (dout + 511) // 512
    per = group
    assert per * nbk <= 8
    for tg in range(0, NT, per):
        tts = list(range(tg, min(tg + per, NT)))
        for ti, tt in enumerate(tts):
            b0 = ti * nbk

            def mmfn(t, tt=tt, b0=b0):
                last = None
                for c in range(n_din):
                    for k in range(nbk):
                        w = w_chunks[c][:, 512 * k:min(512 * (k + 1), dout)]
                        last = t.matmul(
                            ctx["psum"][:, 512 * (b0 + k):512 * (b0 + k) + w.shape[1]],
                            xn_chunks[c][:, 128 * tt:128 * (tt + 1)],
                            w, start=(c == 0),
                            stop=(c == n_din - 1 and bias_row is None))
                if bias_row is not None:
                    for k in range(nbk):
                        br = bias_row[:, 512 * k:min(512 * (k + 1), dout)]
                        last = t.matmul(
                            ctx["psum"][:, 512 * (b0 + k):512 * (b0 + k) + br.shape[1]],
                            ctx["ones_row"], br, start=False, stop=True)
                return last
            reads = [xnbuf, wbuf] + list(extra_mm_reads)
            if bias_row is not None:
                reads += [biasrowb, ctx["ones_rowb"]]
            P.op("tensor", mmfn, reads=reads,
                 writes=[pbufs[b0 + k] for k in range(nbk)])
        for ti, tt in enumerate(tts):
            b0 = ti * nbk
            consume(tt, ctx["psum"][:, 512 * b0:512 * b0 + dout],
                    [pbufs[b0 + k] for k in range(nbk)])


def emit_outer_ln_residual_consume(ctx, x_src_dram, x_dst_dram):
    P, A = ctx["P"], ctx["arena"]
    xt = [A.f32(D) for _ in range(2)]
    xb = [Buf(f"orx{i}") for i in range(2)]
    x1 = [A.f32(D) for _ in range(2)]
    x1b = [Buf(f"orx1{i}") for i in range(2)]
    dump = A.bf16(D)
    dumpb = Buf("ordump")
    cols = A.f32(16)
    colb = [Buf(f"orc{i}") for i in range(2)]

    def consume(tt, psum_ap, bankbufs):
        sl = tt % 2
        cb = colb[sl]
        x_ap, xbuf = xt[sl], xb[sl]
        x1_ap, x1buf = x1[sl], x1b[sl]
        P.dma("sync", x_ap, x_src_dram[128 * tt:128 * (tt + 1), :], writes=[xbuf])
        s_c = cols[:, 5 * sl + 0:5 * sl + 1]
        ss_c = cols[:, 5 * sl + 1:5 * sl + 2]
        P.op("vector", lambda v, o=s_c, p=psum_ap: v.reduce_sum(out=o, in_=p, axis=AX.X),
             reads=bankbufs, writes=[cb])
        P.op("scalar", lambda s, o=dump, a=ss_c, p=psum_ap:
             s.activation(out=o, in_=p, func=AF.Square, accum_out=a),
             reads=bankbufs, writes=[dumpb, cb])
        rstd_c, nmr_c = emit_ln_cols(ctx, cols, sl, cb)
        P.op("vector", lambda v, x=x_ap, s1=nmr_c:
             v.tensor_scalar_add(out=x, in0=x, scalar1=s1),
             reads=[xbuf], scalar_reads=[cb], writes=[xbuf])
        P.op("vector", lambda v, o=x1_ap, s1=rstd_c, x=x_ap, p=psum_ap:
             v.scalar_tensor_tensor(out=o, in0=p, scalar=s1, in1=x,
                                    op0=ALU.mult, op1=ALU.add),
             reads=list(bankbufs) + [xbuf], scalar_reads=[cb], writes=[x1buf])
        P.dma("sync", x_dst_dram[128 * tt:128 * (tt + 1), :], x1_ap, reads=[x1buf])

    return consume


def emit_attention_core(ctx, H, HD, scale, causal, q_dram, qdramb, k_dram, kdramb,
                        v_tm, v_tmb, negB_all, negBhb, o_dram, odramb,
                        den_dram, dendramb, negb_col):
    P, A = ctx["P"], ctx["arena"]
    VW = HD + 1
    A.mark()
    qp = [A.bf16(T, parts=HD) for _ in range(2)]
    kp = [A.bf16(T, parts=HD) for _ in range(2)]
    qpb = [Buf(f"aqp{i}") for i in range(2)]
    kpb = [Buf(f"akp{i}") for i in range(2)]
    e_sb = [A.bf16(1024) for _ in range(3)]
    e_bf = [Buf(f"ae{i}") for i in range(3)]
    o_stage = A.bf16(T)
    o_stageb = Buf("aostage")
    den = A.f32(512, parts=1)
    denb = Buf("aden")
    pbufs = ctx["psum_bufs"]
    hpc = 128 // HD

    for h in range(H):
        sl = h % 2
        negB = negb_col(h)
        P.dma("sync", qp[sl], q_dram[HD * h:HD * (h + 1), :],
              reads=[qdramb], writes=[qpb[sl]])
        P.dma("sync", kp[sl], k_dram[HD * h:HD * (h + 1), :],
              reads=[kdramb], writes=[kpb[sl]])
        for pi in range(4):
            qbs = (2 * pi, 2 * pi + 1)
            units = []
            for qb in qbs:
                ktm = min(NT, 4 * (qb + 1)) if causal else NT
                for kt in range(0, ktm, 2):
                    units.append((qb, kt, kt + 2 >= ktm))
            N = len(units)

            def emit_pv(n):
                qb, kt, last = units[n]
                ps = n % 3
                es = n % 3
                ob = qb - qbs[0]
                first = (kt == 0)
                for j in range(2):
                    vsl = v_tm[:, (kt + j) * H * VW + VW * h:(kt + j) * H * VW + VW * (h + 1)]
                    P.op("tensor", lambda t, vs=vsl, es=es, j=j, ob=ob,
                         st=(first and j == 0), sp=(last and j == 1):
                         t.matmul(ctx["psum"][0:VW, 512 * ob:512 * ob + 512],
                                  vs, e_sb[es][:, 512 * j:512 * (j + 1)],
                                  start=st, stop=sp),
                         reads=[e_bf[es], v_tmb], writes=[pbufs[ob]])
                if last:
                    P.op("scalar", lambda s, ob=ob, o=den:
                         s.activation(out=o, in_=ctx["psum"][HD:HD + 1, 512 * ob:512 * ob + 512],
                                      func=AF.Copy),
                         reads=[pbufs[ob]], writes=[denb])
                    P.dma("sync", den_dram[h:h + 1, 512 * qb:512 * (qb + 1)], den,
                          reads=[denb], writes=[dendramb], tag=f"denst_{h}_{qb}")
                    P.op("vector", lambda v, ob=ob, qb=qb,
                         o=o_stage[HD * (h % hpc):HD * (h % hpc + 1), 512 * qb:512 * (qb + 1)]:
                         v.tensor_copy(out=o, in_=ctx["psum"][0:HD, 512 * ob:512 * ob + 512]),
                         reads=[pbufs[ob]], writes=[o_stageb])

            for n in range(N):
                qb, kt, last = units[n]
                ps = n % 3
                def smm(t, kt=kt, qb=qb, ps=ps, sl=sl):
                    i1 = t.matmul(_bank(ctx, 2 + 2 * ps),
                                  kp[sl][:, 128 * kt:128 * (kt + 1)],
                                  qp[sl][:, 512 * qb:512 * (qb + 1)],
                                  start=True, stop=True)
                    i2 = t.matmul(_bank(ctx, 3 + 2 * ps),
                                  kp[sl][:, 128 * (kt + 1):128 * (kt + 2)],
                                  qp[sl][:, 512 * qb:512 * (qb + 1)],
                                  start=True, stop=True)
                    return i2
                P.op("tensor", smm, reads=[kpb[sl], qpb[sl]],
                     writes=[pbufs[2 + 2 * ps], pbufs[3 + 2 * ps]])
                es = n % 3
                P.op("scalar", lambda s, ps=ps, es=es, nb_=negB:
                     s.activation(out=e_sb[es], in_=_bank(ctx, 2 + 2 * ps, 2),
                                  func=AF.Exp, scale=scale, bias=nb_),
                     reads=[pbufs[2 + 2 * ps], pbufs[3 + 2 * ps]],
                     scalar_reads=[negBhb], writes=[e_bf[es]])
                if causal:
                    for j in range(2):
                        mi = (kt + j) - 4 * qb
                        if 0 <= mi <= 3:
                            P.op("vector", lambda v, es=es, j=j, m=ctx["masks"][mi]:
                                 v.tensor_mul(out=e_sb[es][:, 512 * j:512 * (j + 1)],
                                              in0=e_sb[es][:, 512 * j:512 * (j + 1)], in1=m),
                                 reads=[e_bf[es], ctx["masksb"]], writes=[e_bf[es]])
                if n >= 2:
                    emit_pv(n - 2)
            for n in (N - 2, N - 1):
                if n >= 0:
                    emit_pv(n)
        if h % hpc == hpc - 1:
            P.dma("sync", o_dram[128 * (h // hpc):128 * (h // hpc) + 128, :], o_stage,
                  reads=[o_stageb], writes=[odramb])
    A.release()


def emit_bulk_normalize(ctx, H, HD, o_chunks, obuf, den_dram, dendramb, rall_dram):
    P, A = ctx["P"], ctx["arena"]
    A.mark()
    rallb = Buf("ralld")
    den_all = A.f32(T, parts=H)
    den_allb = Buf("denall")
    den_tmp = A.f32(T, parts=H)
    den_tmpb = Buf("dentmp")
    P.dma("sync", den_all, den_dram, reads=[dendramb], writes=[den_allb], tag="denload")
    if DBG2:
        P.dma("sync", ctx["dram"]["dsnap"], den_all, reads=[den_allb])
    P.op("scalar", lambda s: s.activation(out=den_tmp, in_=den_all, func=AF.Ln),
         reads=[den_allb], writes=[den_tmpb])
    P.op("scalar", lambda s: s.activation(out=den_all, in_=den_tmp, func=AF.Exp,
                                          scale=-1.0),
         reads=[den_tmpb], writes=[den_allb])
    P.dma("sync", rall_dram[0:H, :], den_all, reads=[den_allb], writes=[rallb])
    P.barrier()
    A.release()
    A.mark()
    rbf = A.f32(T)
    rbfb = Buf("rbalf")
    rb = A.bf16(T)
    rbb = Buf("rball")
    hpc = 128 // HD
    for h in range(H):
        c, j = h // hpc, h % hpc
        P.dma("sync", rbf[HD * j:HD * (j + 1), :],
              bass.AP(rall_dram.tensor, rall_dram.offset + T * h, [[0, HD], [1, T]]),
              reads=[rallb], writes=[rbfb])
        P.op("vector", lambda v, j=j, o=rb[HD * j:HD * (j + 1), :],
             i_=rbf[HD * j:HD * (j + 1), :]:
             v.tensor_copy(out=o, in_=i_),
             reads=[rbfb], writes=[rbb])
        P.op("vector", lambda v, c=c, j=j,
             o=o_chunks[c][HD * j:HD * (j + 1), :], r=rb[HD * j:HD * (j + 1), :]:
             v.tensor_tensor(out=o, in0=o, in1=r, op=ALU.mult),
             reads=[obuf, rbb], writes=[obuf])
    P.barrier()
    A.release()


def emit_block_A(ctx, x_src_dram, x_dst_dram, iw):
    P, A, nc = ctx["P"], ctx["arena"], ctx["nc"]
    H, HD = 16, 64
    VW = HD + 1
    d = ctx["dram"]
    pbufs = ctx["psum_bufs"]
    ctx["odramb"] = Buf("odramA")
    dendramb = Buf("dendramA")

    emit_ln_tm(ctx, x_src_dram, d["xn"])
    P.barrier()

    A.mark()
    v_tm = A.bf16(NT * H * VW)
    v_tmb = Buf("v_tm")
    mall = A.f32(2 * ND, parts=2)
    msq = mall[0:2, 0:ND]
    msk = mall[0:2, ND:2 * ND]
    mtmp = A.f32(2, parts=2)
    msqB = {"ap": msq, "buf": Buf("msq"), "hs": ctx["headsum2"], "hsb": ctx["headsum2b"],
            "nh": 2, "tmpcol": mtmp[:, 0:1], "tmpbuf": Buf("mt1"), "first": True}
    mskB = {"ap": msk, "buf": Buf("msk"), "hs": ctx["headsum2"], "hsb": ctx["headsum2b"],
            "nh": 2, "tmpcol": mtmp[:, 1:2], "tmpbuf": Buf("mt2"), "first": True}
    ones_view = bass.AP(v_tm.tensor, v_tm.offset + HD,
                        [[v_tm.ap[0][0], 128], [VW, NT * H], [1, 1]])
    P.op("vector", lambda v, o=ones_view: v.memset(o, 1.0), writes=[v_tmb])

    A.mark()
    xn_chunks, xnbuf = load_xn_fm(ctx, d["xn"])
    A.mark()
    wq, wqb = load_weight_fm(ctx, iw["A_wq"], ND, D)
    bq_col = A.f32(ND)
    bqb = Buf("bq")
    P.dma("sync", bq_col, iw["A_bq_col"], writes=[bqb])
    qdramb = Buf("qdram")
    emit_proj_formB(ctx, wq, wqb, xn_chunks, xnbuf, d["q"], qdramb, ND, ND,
                    bias_col=bq_col, biasbuf=bqb)
    A.release()
    P.barrier()
    A.mark()
    wk, wkb = load_weight_fm(ctx, iw["A_wk"], ND, D)
    bk_col = A.f32(ND)
    bkb = Buf("bk")
    P.dma("sync", bk_col, iw["A_bk_col"], writes=[bkb])
    kdramb = Buf("kdram")
    emit_proj_formB(ctx, wk, wkb, xn_chunks, xnbuf, d["k"], kdramb, ND, ND,
                    bias_col=bk_col, biasbuf=bkb)
    A.release()
    P.barrier()
    A.mark()
    wv, wvb = load_weight_fm(ctx, iw["A_wv"], ND, D)
    bv_row = A.bf16(D, parts=1)
    bvb = Buf("bv")
    P.dma("gpsimd", bv_row, iw["A_bv"].unsqueeze(0), writes=[bvb])

    def v_consume(tt, psum_ap, bankbufs):
        out = bass.AP(v_tm.tensor, v_tm.offset + tt * H * VW,
                      [[v_tm.ap[0][0], 128], [VW, H], [1, HD]])
        P.op("scalar", lambda s, o=out, i_=psum_ap.rearrange("p (h e) -> p h e", h=H):
             s.activation(out=o, in_=i_, func=AF.Copy),
             reads=bankbufs, writes=[v_tmb])

    emit_proj_formA(ctx, xn_chunks, xnbuf, wv, wvb, ND, D,
                    bias_row=bv_row, biasrowb=bvb, consume=v_consume, group=4)
    A.release()
    A.release()
    P.barrier()

    if DBG:
        P.dma("sync", d["q0"], d["q"], reads=[qdramb])
    emit_rope_phase(ctx, d["q"], qdramb, msqB, iw)
    emit_rope_phase(ctx, d["k"], kdramb, mskB, iw)
    P.barrier()

    A.mark()
    sqh = A.f32(16)
    sqhb = Buf("sqh")
    negm = A.f32(ND, parts=2)
    negmb = Buf("negm")
    negB_all = A.f32(2 * ND)
    negBhb = Buf("negBall")
    P.op("scalar", lambda s: s.activation(out=sqh[0:2, :], in_=mall[0:2, 0:16], func=AF.Sqrt),
         reads=[msqB["buf"], mskB["buf"]], writes=[sqhb])
    P.op("vector", lambda v: v.tensor_tensor(out=negm[0:2, :], in0=sqh[0:2, 0:ND],
                                             in1=sqh[0:2, ND:2 * ND], op=ALU.mult),
         reads=[sqhb], writes=[negmb])
    P.op("vector", lambda v: v.tensor_scalar_mul(out=negm[0:2, :], in0=negm[0:2, :],
                                                 scalar1=-0.125),
         reads=[negmb], writes=[negmb])
    P.dma("sync", d["negd"], negm[0:2, :], reads=[negmb], writes=[ctx["negdb"]])
    P.dma("sync", negB_all,
          bass.AP(d["negd"].tensor, d["negd"].offset, [[0, 128], [1, 2 * ND]]),
          reads=[ctx["negdb"]], writes=[negBhb])
    emit_attention_core(ctx, H, HD, 0.125, True, d["q"], qdramb, d["k"], kdramb,
                        v_tm, v_tmb, negB_all, negBhb, d["o"], ctx["odramb"],
                        d["denh"], dendramb,
                        lambda h: negB_all[:, ND * (h % 2) + h // 2:ND * (h % 2) + h // 2 + 1])
    A.release()
    P.barrier()

    A.mark()
    o_chunks = []
    obuf = Buf("ofm")
    for c in range(ND):
        ap = A.bf16(T)
        o_chunks.append(ap)
        P.dma("sync", ap, d["o"][128 * c:128 * (c + 1), :],
              reads=[ctx["odramb"]], writes=[obuf])
    emit_bulk_normalize(ctx, H, HD, o_chunks, obuf, d["denh"], dendramb, d["ralla"])
    if DBG2:
        for c in range(ND):
            P.dma("sync", d["q"][128 * c:128 * (c + 1), :], o_chunks[c], reads=[obuf])
    wo, wob = load_weight_fm(ctx, iw["A_wo"], ND, D)
    bo_row = A.bf16(D, parts=1)
    bob = Buf("bo")
    P.dma("gpsimd", bo_row, iw["A_bo"].unsqueeze(0), writes=[bob])
    consume = emit_outer_ln_residual_consume(ctx, x_src_dram, x_dst_dram)
    emit_proj_formA(ctx, o_chunks, obuf, wo, wob, ND, D,
                    bias_row=bo_row, biasrowb=bob, consume=consume, group=4)
    A.release()
    A.release()
    P.barrier()


def emit_block_F(ctx, x_src_dram, x_dst_dram, iw):
    P, A, nc = ctx["P"], ctx["arena"], ctx["nc"]
    H, HD = 16, 32
    VW = HD + 1
    SCL = 1.0 / math.sqrt(HD)
    d = ctx["dram"]
    pbufs = ctx["psum_bufs"]
    aodramb = Buf("aodramF")
    dendramb = Buf("dendramF")
    qdramb = Buf("qdramF")
    kdramb = Buf("kdramF")
    mladramb = Buf("mladramF")
    moedramb = Buf("moedramF")

    emit_ln_tm(ctx, x_src_dram, d["xn"])
    P.barrier()

    A.mark()
    mall = A.f32(8, parts=4)
    msq = mall[0:4, 0:4]
    msk = mall[0:4, 4:8]
    mtmp = A.f32(2, parts=4)
    msqB = {"ap": msq, "buf": Buf("msqF"), "hs": ctx["headsum4"], "hsb": ctx["headsum4b"],
            "nh": 4, "tmpcol": mtmp[:, 0:1], "tmpbuf": Buf("mt1F")}
    mskB = {"ap": msk, "buf": Buf("mskF"), "hs": ctx["headsum4"], "hsb": ctx["headsum4b"],
            "nh": 4, "tmpcol": mtmp[:, 1:2], "tmpbuf": Buf("mt2F")}
    cT_sb = A.bf16(T, parts=8)
    cTb = Buf("cT")
    A.mark()
    v_tm = A.bf16(NT * H * VW)
    v_tmb = Buf("v_tmF")
    ones_view = bass.AP(v_tm.tensor, v_tm.offset + HD,
                        [[v_tm.ap[0][0], 128], [VW, NT * H], [1, 1]])
    P.op("vector", lambda v, o=ones_view: v.memset(o, 1.0), writes=[v_tmb])

    A.mark()
    xn_chunks, xnbuf = load_xn_fm(ctx, d["xn"])
    A.mark()
    wq, wqb = load_weight_fm(ctx, iw["F_qw"], ND, 512)
    bq_col = A.f32(4)
    bqb = Buf("bqF")
    P.dma("sync", bq_col, iw["F_qb_col"], writes=[bqb])
    emit_proj_formB(ctx, wq, wqb, xn_chunks, xnbuf, d["q"][0:512, :], qdramb, ND, 4,
                    bias_col=bq_col, biasbuf=bqb, maxsq=msqB)
    A.release()
    P.barrier()
    A.mark()
    wk, wkb = load_weight_fm(ctx, iw["F_kw"], ND, 512)
    bk_col = A.f32(4)
    bkb = Buf("bkF")
    P.dma("sync", bk_col, iw["F_kb_col"], writes=[bkb])
    emit_proj_formB(ctx, wk, wkb, xn_chunks, xnbuf, d["k"][0:512, :], kdramb, ND, 4,
                    bias_col=bk_col, biasbuf=bkb, maxsq=mskB)
    A.release()
    P.barrier()
    A.mark()
    wv, wvb = load_weight_fm(ctx, iw["F_vw"], ND, 512)
    bv_row = A.bf16(512, parts=1)
    bvb = Buf("bvF")
    P.dma("gpsimd", bv_row, iw["F_vb"].unsqueeze(0), writes=[bvb])

    def v_consume(tt, psum_ap, bankbufs):
        out = bass.AP(v_tm.tensor, v_tm.offset + tt * H * VW,
                      [[v_tm.ap[0][0], 128], [VW, H], [1, HD]])
        P.op("scalar", lambda s, o=out, i_=psum_ap.rearrange("p (h e) -> p h e", h=H):
             s.activation(out=o, in_=i_, func=AF.Copy),
             reads=bankbufs, writes=[v_tmb])

    emit_proj_formA(ctx, xn_chunks, xnbuf, wv, wvb, ND, 512,
                    bias_row=bv_row, biasrowb=bvb, consume=v_consume, group=4)
    A.release()
    A.release()
    P.barrier()

    A.mark()
    sqh = A.f32(8, parts=4)
    sqhb = Buf("sqhF")
    negm = A.f32(4, parts=4)
    negmb = Buf("negmF")
    negB_all = A.f32(16)
    negBhb = Buf("negBallF")
    P.op("scalar", lambda s: s.activation(out=sqh, in_=mall[0:4, 0:8], func=AF.Sqrt),
         reads=[msqB["buf"], mskB["buf"]], writes=[sqhb])
    P.op("vector", lambda v: v.tensor_tensor(out=negm[0:4, :], in0=sqh[0:4, 0:4],
                                             in1=sqh[0:4, 4:8], op=ALU.mult),
         reads=[sqhb], writes=[negmb])
    P.op("vector", lambda v: v.tensor_scalar_mul(out=negm[0:4, :], in0=negm[0:4, :],
                                                 scalar1=-SCL),
         reads=[negmb], writes=[negmb])
    P.dma("sync", d["negd2"], negm[0:4, :], reads=[ctx["negd2b"]] and [negmb], writes=[ctx["negd2b"]])
    P.dma("sync", negB_all,
          bass.AP(d["negd2"].tensor, d["negd2"].offset, [[0, 128], [1, 16]]),
          reads=[ctx["negd2b"]], writes=[negBhb])
    emit_attention_core(ctx, H, HD, SCL, False, d["q"][0:512, :], qdramb,
                        d["k"][0:512, :], kdramb, v_tm, v_tmb, negB_all, negBhb,
                        d["ao"], aodramb, d["denh"], dendramb,
                        lambda h: negB_all[:, 4 * (h % 4) + h // 4:4 * (h % 4) + h // 4 + 1])
    A.release()
    A.release()
    P.barrier()

    A.mark()
    ao_chunks = []
    aobuf = Buf("aofm")
    for c in range(4):
        ap = A.bf16(T)
        ao_chunks.append(ap)
        P.dma("sync", ap, d["ao"][128 * c:128 * (c + 1), :],
              reads=[aodramb], writes=[aobuf])
    emit_bulk_normalize(ctx, 16, 32, ao_chunks, aobuf, d["denh"], dendramb, d["ralla"])
    wo, wob = load_weight_fm(ctx, iw["F_ow"], 4, D)
    bo_col = A.f32(ND)
    bob = Buf("boF")
    P.dma("sync", bo_col, iw["F_ob_col"], writes=[bob])
    emit_proj_formB(ctx, wo, wob, ao_chunks, aobuf, d["mla"], mladramb, 4, ND,
                    bias_col=bo_col, biasbuf=bob)
    A.release()
    P.barrier()

    A.mark()
    mla_chunks = []
    mlabuf = Buf("mlafm")
    for c in range(ND):
        ap = A.bf16(T)
        mla_chunks.append(ap)
        P.dma("sync", ap, d["mla"][128 * c:128 * (c + 1), :],
              reads=[mladramb], writes=[mlabuf])
    rw, rwb = load_weight_fm(ctx, iw["F_rw"], ND, 8)
    rb_row = A.bf16(8, parts=1)
    rbb = Buf("rbrow")
    P.dma("gpsimd", rb_row, iw["F_rb"].unsqueeze(0), writes=[rbb])
    rcolsL = [A.f32(8) for _ in range(2)]
    rcolsbL = [Buf(f"rcols{i}") for i in range(2)]
    rlg_dbg = A.f32(8)
    rlgdbgb = Buf("rlgdbg")
    elL = [A.f32(8) for _ in range(2)]
    elbL = [Buf(f"el{i}") for i in range(2)]
    probsL = [A.f32(8) for _ in range(2)]
    probsbL = [Buf(f"probs{i}") for i in range(2)]
    mask1L = [A.f32(8) for _ in range(2)]
    mask1bL = [Buf(f"mask1{i}") for i in range(2)]
    combL = [A.bf16(8) for _ in range(2)]
    combbL = [Buf(f"comb{i}") for i in range(2)]

    def r_consume(tt, psum_ap, bankbufs):
        sl = tt % 2
        rcols, rcolsb = rcolsL[sl], rcolsbL[sl]
        el, elb = elL[sl], elbL[sl]
        probs, probsb = probsL[sl], probsbL[sl]
        mask1, mask1b = mask1L[sl], mask1bL[sl]
        comb, combb = combL[sl], combbL[sl]
        if DBG2:
            P.op("vector", lambda v, p=psum_ap, o=rlg_dbg:
                 v.tensor_copy(out=o, in_=p), reads=bankbufs, writes=[rlgdbgb])
            P.dma("sync", d["rldbg"][128 * tt:128 * (tt + 1), :], rlg_dbg,
                  reads=[rlgdbgb])
        m_c, nm_c, s_c, m2_c, d2_c = (rcols[:, i:i + 1] for i in range(5))
        P.op("vector", lambda v, p=psum_ap, o=m_c: v.reduce_max(out=o, in_=p, axis=AX.X),
             reads=bankbufs, writes=[rcolsb])
        P.op("vector", lambda v, o=nm_c, i_=m_c:
             v.tensor_scalar_mul(out=o, in0=i_, scalar1=-1.0),
             reads=[rcolsb], scalar_reads=[rcolsb], writes=[rcolsb])
        P.op("scalar", lambda s, p=psum_ap, o=el, b=nm_c:
             s.activation(out=o, in_=p, func=AF.Exp, bias=b, accum_out=s_c),
             reads=bankbufs, scalar_reads=[rcolsb], writes=[elb, rcolsb])
        P.op("scalar", lambda s, o=s_c, i_=s_c: s.activation(out=o, in_=i_, func=AF.Ln),
             reads=[rcolsb], writes=[rcolsb])
        P.op("scalar", lambda s, o=s_c, i_=s_c: s.activation(out=o, in_=i_, func=AF.Exp,
                                                             scale=-1.0),
             reads=[rcolsb], writes=[rcolsb])
        P.op("vector", lambda v, o=probs, i_=el, s1=s_c:
             v.tensor_scalar_mul(out=o, in0=i_, scalar1=s1),
             reads=[elb], scalar_reads=[rcolsb], writes=[probsb])
        P.op("vector", lambda v, o=m_c, i_=probs: v.reduce_max(out=o, in_=i_, axis=AX.X),
             reads=[probsb], writes=[rcolsb])
        P.op("vector", lambda v, o=mask1, i_=probs, s1=m_c:
             v.tensor_single_scalar(out=o, in_=i_, scalar=s1, op=ALU.is_ge),
             reads=[probsb], scalar_reads=[rcolsb], writes=[mask1b])
        P.op("vector", lambda v, o=el:
             v.scalar_tensor_tensor(out=o, in0=mask1, scalar=-1e30, in1=probs,
                                    op0=ALU.mult, op1=ALU.add),
             reads=[mask1b, probsb], writes=[elb])
        P.op("vector", lambda v, o=m2_c, i_=el: v.reduce_max(out=o, in_=i_, axis=AX.X),
             reads=[elb], writes=[rcolsb])
        P.op("vector", lambda v, o=mask1, i_=probs, s1=m2_c:
             v.tensor_single_scalar(out=o, in_=i_, scalar=s1, op=ALU.is_ge),
             reads=[probsb], scalar_reads=[rcolsb], writes=[mask1b])
        P.op("vector", lambda v, o=d2_c, i_=m_c, s1=m2_c:
             v.tensor_scalar(out=o, in0=i_, scalar1=s1, scalar2=1e-8,
                             op0=ALU.add, op1=ALU.add),
             reads=[rcolsb], scalar_reads=[rcolsb], writes=[rcolsb])
        P.op("scalar", lambda s, o=d2_c, i_=d2_c: s.activation(out=o, in_=i_, func=AF.Ln),
             reads=[rcolsb], writes=[rcolsb])
        P.op("scalar", lambda s, o=d2_c, i_=d2_c: s.activation(out=o, in_=i_, func=AF.Exp,
                                                               scale=-1.0),
             reads=[rcolsb], writes=[rcolsb])
        P.op("vector", lambda v, o=comb, s1=d2_c:
             v.scalar_tensor_tensor(out=o, in0=probs, scalar=s1, in1=mask1,
                                    op0=ALU.mult, op1=ALU.mult),
             reads=[probsb, mask1b], scalar_reads=[rcolsb], writes=[combb])
        if DBG2:
            P.dma("sync", d["rp_dbg"][128 * tt:128 * (tt + 1), 0:8], probs,
                  reads=[probsb])
            P.dma("sync", d["rp_dbg"][128 * tt:128 * (tt + 1), 8:16], mask1,
                  reads=[mask1b])
            P.dma("sync", d["rp_dbg"][128 * tt:128 * (tt + 1), 16:24], rcols[:, 0:8],
                  reads=[rcolsb])
        P.op("tensor", lambda t, tt=tt, sl=sl, comb=comb, combb=combb:
             t.matmul(ctx["psum"][0:8, 512 * (4 + sl):512 * (4 + sl) + 128],
                      comb, ctx["ident"], start=True, stop=True),
             reads=[combb, ctx["identb"]], writes=[pbufs[4 + sl]])
        P.op("vector", lambda v, tt=tt, sl=sl, o=cT_sb[0:8, 128 * tt:128 * (tt + 1)]:
             v.tensor_copy(out=o, in_=ctx["psum"][0:8, 512 * (4 + sl):512 * (4 + sl) + 128]),
             reads=[pbufs[4 + sl]], writes=[cTb])

    emit_proj_formA(ctx, mla_chunks, mlabuf, rw, rwb, ND, 8,
                    bias_row=rb_row, biasrowb=rbb, consume=r_consume, group=4)
    if DBG2:
        P.dma("sync", d["ctdbg"], cT_sb[0:8, :], reads=[cTb])
    A.release()
    P.barrier()

    A.mark()
    mla_win = [A.bf16(1024) for _ in range(ND)]
    mlawb = Buf("mlawin")
    w1sb = [A.bf16(2048) for _ in range(ND)]
    w1b = Buf("w1e")
    w2sb = [A.bf16(1024) for _ in range(16)]
    w2b = Buf("w2e")
    b1col = A.f32(16)
    b1b = Buf("b1e")
    b2all = A.bf16(1024, parts=8)
    b2b = Buf("b2all")
    P.dma("gpsimd", b2all, iw["F_b2"], writes=[b2b])
    cb_sb = A.bf16(T)
    cbb = Buf("cb")
    h1g = A.bf16(16 * 1024)
    h1gb = Buf("h1g")
    mstage = [A.f32(1024) for _ in range(2)]
    mstageb = [Buf(f"mst{i}") for i in range(2)]

    for e in range(8):
        for c in range(ND):
            P.dma("gpsimd", w1sb[c], iw["F_w1"][e, 128 * c:128 * (c + 1), :], writes=[w1b])
        for f in range(16):
            P.dma("gpsimd", w2sb[f], iw["F_w2"][e, 128 * f:128 * (f + 1), :], writes=[w2b])
        P.dma("sync", b1col, iw["F_b1_col"][e], writes=[b1b])
        for blk in range(NB):
            P.op("tensor", lambda t, e=e, blk=blk:
                 t.matmul(ctx["psum"][:, 512 * 4:512 * 5],
                          ctx["e8"][0:8, 128 * e:128 * (e + 1)],
                          cT_sb[0:8, 512 * blk:512 * (blk + 1)],
                          start=True, stop=True),
                 reads=[cTb, ctx["e8b"]], writes=[pbufs[4]])
            P.op("vector", lambda v, blk=blk, o=cb_sb[:, 512 * blk:512 * (blk + 1)]:
                 v.tensor_copy(out=o, in_=ctx["psum"][:, 512 * 4:512 * 5]),
                 reads=[pbufs[4]], writes=[cbb])
        for ps in range(4):
            w0 = 1024 * ps
            for c in range(ND):
                P.dma("sync", mla_win[c], d["mla"][128 * c:128 * (c + 1), w0:w0 + 1024],
                      reads=[mladramb], writes=[mlawb])
            for f in range(16):
                pb = 4 + 2 * (f % 2)
                def h1fn(t, f=f, pb=pb):
                    last = None
                    for nb in range(2):
                        for c in range(ND):
                            last = t.matmul(_bank(ctx, pb + nb),
                                            w1sb[c][:, 128 * f:128 * (f + 1)],
                                            mla_win[c][:, 512 * nb:512 * (nb + 1)],
                                            start=(c == 0), stop=(c == ND - 1))
                    return last
                P.op("tensor", h1fn, reads=[w1b, mlawb],
                     writes=[pbufs[pb], pbufs[pb + 1]])
                P.op("scalar", lambda s, f=f, pb=pb, o=h1g[:, 1024 * f:1024 * (f + 1)],
                     b=b1col[:, f:f + 1]:
                     s.activation(out=o, in_=_bank(ctx, pb, 2), func=AF.Gelu, bias=b),
                     reads=[pbufs[pb], pbufs[pb + 1]], scalar_reads=[b1b], writes=[h1gb])
                P.op("vector", lambda v, f=f, ps=ps, o=h1g[:, 1024 * f:1024 * (f + 1)],
                     cbw=cb_sb[:, w0:w0 + 1024]:
                     v.tensor_mul(out=o, in0=o, in1=cbw),
                     reads=[h1gb, cbb], writes=[h1gb])
            for tti in range(8):
                pb = 2 * (tti % 2)
                def h2fn(t, tti=tti, pb=pb, e=e, ps=ps):
                    last = None
                    for nb in range(2):
                        for f in range(16):
                            last = t.matmul(_bank(ctx, pb + nb),
                                            h1g[:, 1024 * f + 128 * tti:1024 * f + 128 * (tti + 1)],
                                            w2sb[f][:, 512 * nb:512 * (nb + 1)],
                                            start=(f == 0), stop=(f == 15 and e != 0))
                        if e == 0:
                            tt = ps * 8 + tti
                            last = t.matmul(_bank(ctx, pb + nb),
                                            cT_sb[0:8, 128 * tt:128 * (tt + 1)],
                                            b2all[0:8, 512 * nb:512 * (nb + 1)],
                                            start=False, stop=True)
                    return last
                reads = [h1gb, w2b] + ([cTb, b2b] if e == 0 else [])
                P.op("tensor", h2fn, reads=reads, writes=[pbufs[pb], pbufs[pb + 1]])
                msl = tti % 2
                P.op("vector", lambda v, pb=pb, o=mstage[msl]:
                     v.tensor_copy(out=o, in_=_bank(ctx, pb, 2)),
                     reads=[pbufs[pb], pbufs[pb + 1]], writes=[mstageb[msl]])
                tt = ps * 8 + tti
                if e == 0:
                    P.dma("gpsimd", d["moe"][128 * tt:128 * (tt + 1), :], mstage[msl],
                          reads=[mstageb[msl]], writes=[moedramb])
                else:
                    P.op("gpsimd", lambda g, tt=tt, msl=msl:
                         g.dma_start(out=d["moe"][128 * tt:128 * (tt + 1), :],
                                     in_=mstage[msl], accum_op=ALU.add),
                         reads=[mstageb[msl]], writes=[moedramb], dma=True)
    A.release()
    P.barrier()

    A.mark()
    moet = [A.f32(D) for _ in range(2)]
    moetb = [Buf(f"moet{i}") for i in range(2)]
    xnt = [A.bf16(D) for _ in range(2)]
    xntb = [Buf(f"xnt{i}") for i in range(2)]
    bo_t = [A.f32(D) for _ in range(2)]
    bo_tb = [Buf(f"bot{i}") for i in range(2)]
    xt = [A.f32(D) for _ in range(2)]
    xb = [Buf(f"fx{i}") for i in range(2)]
    x1 = [A.f32(D) for _ in range(2)]
    x1b = [Buf(f"fx1{i}") for i in range(2)]
    dump = A.bf16(D)
    dumpb = Buf("fdump")
    cols = A.f32(16)
    colb = [Buf(f"fc{i}") for i in range(2)]
    for tt in range(NT):
        sl = tt % 2
        cb = colb[sl]
        P.dma("sync", moet[sl], d["moe"][128 * tt:128 * (tt + 1), :],
              reads=[moedramb], writes=[moetb[sl]])
        P.dma("sync", xnt[sl], d["xn"][128 * tt:128 * (tt + 1), :], writes=[xntb[sl]])
        P.dma("sync", xt[sl], x_src_dram[128 * tt:128 * (tt + 1), :], writes=[xb[sl]])
        P.op("vector", lambda v, sl=sl, o=bo_t[sl]:
             v.scalar_tensor_tensor(out=o, in0=moet[sl], scalar=0.3, in1=xnt[sl],
                                    op0=ALU.mult, op1=ALU.add),
             reads=[moetb[sl], xntb[sl]], writes=[bo_tb[sl]])
        s_c = cols[:, 5 * sl + 0:5 * sl + 1]
        ss_c = cols[:, 5 * sl + 1:5 * sl + 2]
        P.op("vector", lambda v, sl=sl, o=s_c: v.reduce_sum(out=o, in_=bo_t[sl], axis=AX.X),
             reads=[bo_tb[sl]], writes=[cb])
        P.op("scalar", lambda s, sl=sl, o=dump, a=ss_c:
             s.activation(out=o, in_=bo_t[sl], func=AF.Square, accum_out=a),
             reads=[bo_tb[sl]], writes=[dumpb, cb])
        rstd_c, nmr_c = emit_ln_cols(ctx, cols, sl, cb)
        P.op("vector", lambda v, sl=sl, s1=nmr_c:
             v.tensor_scalar_add(out=xt[sl], in0=xt[sl], scalar1=s1),
             reads=[xb[sl]], scalar_reads=[cb], writes=[xb[sl]])
        P.op("vector", lambda v, sl=sl, o=x1[sl], s1=rstd_c:
             v.scalar_tensor_tensor(out=o, in0=bo_t[sl], scalar=s1, in1=xt[sl],
                                    op0=ALU.mult, op1=ALU.add),
             reads=[bo_tb[sl], xb[sl]], scalar_reads=[cb], writes=[x1b[sl]])
        P.dma("sync", x_dst_dram[128 * tt:128 * (tt + 1), :], x1[sl], reads=[x1b[sl]])
    A.release()
    A.release()
    P.barrier()


KSOFT_C = 40.0

def emit_block_B(ctx, x_src_dram, x_dst_dram, iw):
    P, A, nc = ctx["P"], ctx["arena"], ctx["nc"]
    H, HD = 16, 64
    d = ctx["dram"]
    pbufs = ctx["psum_bufs"]
    qhb, khb, vhb, zb = Buf("qhB"), Buf("khB"), Buf("vhB"), Buf("zB")
    rowsb = Buf("rowsB")

    emit_ln_tm(ctx, x_src_dram, d["xn"])
    P.barrier()

    A.mark()
    rk_row = A.f32(D, parts=1)
    rk_rowb = Buf("rkrow")
    rcol = [A.f32(1) for _ in range(ND)]
    rcolb = Buf("rcolB")

    A.mark()
    xn_chunks, xnbuf = load_xn_fm(ctx, d["xn"])

    A.mark()
    wq, wqb = load_weight_fm(ctx, iw["B_wq"], ND, D)
    bq_row = A.bf16(D, parts=1)
    bqb = Buf("bqB")
    P.dma("gpsimd", bq_row, iw["B_bq"].unsqueeze(0), writes=[bqb])
    qtL = [A.f32(D) for _ in range(2)]
    qtbL = [Buf(f"qtB{i}") for i in range(2)]
    qeL = [A.bf16(D) for _ in range(2)]
    qebL = [Buf(f"qeB{i}") for i in range(2)]
    segL = [A.f32(16) for _ in range(2)]
    segbL = [Buf(f"segB{i}") for i in range(2)]

    def q_consume(tt, psum_ap, bankbufs):
        sl = tt % 2
        qt, qtb = qtL[sl], qtbL[sl]
        qe, qeb = qeL[sl], qebL[sl]
        seg, segb = segL[sl], segbL[sl]
        pv = psum_ap.rearrange("p (s i) -> p s i", s=16)
        P.op("vector", lambda v, o=seg[:, 0:16], pv=pv:
             v.reduce_max(out=o, in_=pv, axis=AX.X),
             reads=bankbufs, writes=[segb])
        P.op("vector", lambda v, pv=pv, o=qt[:].rearrange("p (s i) -> p s i", s=16),
             b=seg[:, 0:16].unsqueeze(2).broadcast_to([128, 16, 64]):
             v.tensor_tensor(out=o, in0=pv, in1=b, op=ALU.subtract),
             reads=bankbufs, scalar_reads=[segb], writes=[qtb])
        P.op("scalar", lambda s: s.activation(out=qe, in_=qt, func=AF.Exp),
             reads=[qtb], writes=[qeb])
        P.op("vector", lambda v, o=seg[:, 0:16], i_=qe[:].rearrange("p (s i) -> p s i", s=16):
             v.reduce_sum(out=o, in_=i_, axis=AX.X),
             reads=[qeb], writes=[segb])
        P.op("scalar", lambda s, o=seg[:, 0:16], i_=seg[:, 0:16]:
             s.activation(out=o, in_=i_, func=AF.Ln),
             reads=[segb], writes=[segb])
        P.op("scalar", lambda s, o=seg[:, 0:16], i_=seg[:, 0:16]:
             s.activation(out=o, in_=i_, func=AF.Exp, scale=-1.0),
             reads=[segb], writes=[segb])
        P.op("vector", lambda v, o=qe[:].rearrange("p (s i) -> p s i", s=16),
             i_=qe[:].rearrange("p (s i) -> p s i", s=16),
             b=seg[:, 0:16].unsqueeze(2).broadcast_to([128, 16, 64]):
             v.tensor_tensor(out=o, in0=i_, in1=b, op=ALU.mult),
             reads=[qeb], scalar_reads=[segb], writes=[qeb])
        P.dma("sync", d["qh"][128 * tt:128 * (tt + 1), :], qe,
              reads=[qeb], writes=[qhb])

    emit_proj_formA(ctx, xn_chunks, xnbuf, wq, wqb, ND, D,
                    bias_row=bq_row, biasrowb=bqb, consume=q_consume, group=4)
    A.release()
    P.barrier()

    A.mark()
    wk, wkb = load_weight_fm(ctx, iw["B_wk"], ND, D)
    bk_row = A.bf16(D, parts=1)
    bkb = Buf("bkB")
    P.dma("gpsimd", bk_row, iw["B_bk"].unsqueeze(0), writes=[bkb])
    keL = [A.bf16(D) for _ in range(2)]
    kebL = [Buf(f"keB{i}") for i in range(2)]
    kssum_started = [False]

    def k_consume(tt, psum_ap, bankbufs):
        ke, keb = keL[tt % 2], kebL[tt % 2]
        P.op("scalar", lambda s, p=psum_ap, ng=ctx["negc"]:
             s.activation(out=ke, in_=p, func=AF.Exp, bias=ng),
             reads=bankbufs, scalar_reads=[ctx["negcb"]], writes=[keb])
        st = not kssum_started[0]
        kssum_started[0] = True
        def sumfn(t, st=st, tt=tt):
            last = None
            for half in range(2):
                last = t.matmul(ctx["psum"][0:1, 512 * (6 + half):512 * (6 + half) + 512],
                                ctx["ones_colb"], ke[:, 512 * half:512 * (half + 1)],
                                start=st, stop=(tt == NT - 1))
            return last
        P.op("tensor", sumfn, reads=[keb, ctx["ones_colbb"]],
             writes=[pbufs[6], pbufs[7]])
        P.dma("sync", d["kh"][128 * tt:128 * (tt + 1), :], ke,
              reads=[keb], writes=[khb])

    emit_proj_formA(ctx, xn_chunks, xnbuf, wk, wkb, ND, D,
                    bias_row=bk_row, biasrowb=bkb, consume=k_consume, group=2)
    P.op("scalar", lambda s: s.activation(out=rk_row[:, 0:512],
                                          in_=ctx["psum"][0:1, 512 * 6:512 * 6 + 512],
                                          func=AF.Ln),
         reads=[pbufs[6]], writes=[rk_rowb])
    P.op("scalar", lambda s: s.activation(out=rk_row[:, 512:1024],
                                          in_=ctx["psum"][0:1, 512 * 7:512 * 7 + 512],
                                          func=AF.Ln),
         reads=[pbufs[7]], writes=[rk_rowb])
    P.op("scalar", lambda s: s.activation(out=rk_row, in_=rk_row, func=AF.Exp, scale=-1.0),
         reads=[rk_rowb], writes=[rk_rowb])
    P.dma("sync", d["row2"][0:1, :], rk_row, reads=[rk_rowb], writes=[rowsb])
    for c in range(ND):
        P.dma("sync", rcol[c],
              bass.AP(d["row2"].tensor, d["row2"].offset + 128 * c, [[1, 128], [1, 1]]),
              reads=[rowsb], writes=[rcolb])
    A.release()
    P.barrier()

    A.mark()
    wv, wvb = load_weight_fm(ctx, iw["B_wv"], ND, D)
    bv_row = A.bf16(D, parts=1)
    bvb = Buf("bvB")
    P.dma("gpsimd", bv_row, iw["B_bv"].unsqueeze(0), writes=[bvb])
    vstageL = [A.bf16(D) for _ in range(2)]
    vstagebL = [Buf(f"vstB{i}") for i in range(2)]

    def v_consume(tt, psum_ap, bankbufs):
        vstage, vstageb = vstageL[tt % 2], vstagebL[tt % 2]
        P.op("scalar", lambda s, p=psum_ap: s.activation(out=vstage, in_=p, func=AF.Copy),
             reads=bankbufs, writes=[vstageb])
        P.dma("sync", d["vh"][128 * tt:128 * (tt + 1), :], vstage,
              reads=[vstageb], writes=[vhb])

    emit_proj_formA(ctx, xn_chunks, xnbuf, wv, wvb, ND, D,
                    bias_row=bv_row, biasrowb=bvb, consume=v_consume, group=4)
    A.release()
    A.release()
    P.barrier()

    A.mark()
    kp2 = [A.bf16(NT * 128) for _ in range(2)]
    vp2 = [A.bf16(NT * 128) for _ in range(2)]
    qf2 = [A.bf16(T) for _ in range(2)]
    qpan = [A.bf16(T, parts=64) for _ in range(2)]
    kp2b = [Buf(f"kp2{i}") for i in range(2)]
    vp2b = [Buf(f"vp2{i}") for i in range(2)]
    qf2b = [Buf(f"qf2{i}") for i in range(2)]
    qpanb = [Buf(f"qpan{i}") for i in range(2)]
    state_sb = A.f32(128)
    state_bf = A.bf16(128)
    stateb = Buf("state")
    z_stage = A.bf16(T)
    z_stageb = Buf("zstage")

    for c in range(ND):
        sl = c % 2
        for part in range(4):
            src_k = bass.AP(d["kh"].tensor, d["kh"].offset + 1024 * part * D + 128 * c,
                            [[D, 128], [128 * D, 8], [1, 128]])
            src_v = bass.AP(d["vh"].tensor, d["vh"].offset + 1024 * part * D + 128 * c,
                            [[D, 128], [128 * D, 8], [1, 128]])
            P.dma("sync", kp2[sl][:, 1024 * part:1024 * (part + 1)],
                  src_k, reads=[khb], writes=[kp2b[sl]])
            P.dma("sync", vp2[sl][:, 1024 * part:1024 * (part + 1)],
                  src_v, reads=[vhb], writes=[vp2b[sl]])
        for slab in range(T // 512):
            P.dma("sync", qf2[sl][:, 512 * slab:512 * (slab + 1)],
                  d["qh"][512 * slab:512 * (slab + 1), 128 * c:128 * (c + 1)],
                  reads=[qhb], writes=[qf2b[sl]], transpose=True)
        P.dma("gpsimd", qpan[sl], qf2[sl][64:128, :], reads=[qf2b[sl]], writes=[qpanb[sl]])
        for half in range(2):
            h = 2 * c + half
            sbank = 4 + half
            def statefn(t, sl=sl, half=half, sbank=sbank):
                last = None
                for tt in range(NT):
                    last = t.matmul(ctx["psum"][0:64, 512 * sbank:512 * sbank + 64],
                                    kp2[sl][:, 128 * tt + 64 * half:128 * tt + 64 * (half + 1)],
                                    vp2[sl][:, 128 * tt + 64 * half:128 * tt + 64 * (half + 1)],
                                    start=(tt == 0), stop=(tt == NT - 1))
                return last
            P.op("tensor", statefn, reads=[kp2b[sl], vp2b[sl]],
                 writes=[pbufs[sbank]])
            P.op("vector", lambda v, half=half, c=c, sbank=sbank,
                 o=state_sb[0:64, 64 * half:64 * (half + 1)],
                 s1=rcol[c][64 * half:64 * (half + 1), 0:1]:
                 v.tensor_scalar_mul(out=o, in0=ctx["psum"][0:64, 512 * sbank:512 * sbank + 64],
                                     scalar1=s1),
                 reads=[pbufs[sbank]], scalar_reads=[rcolb], writes=[stateb])
            P.op("vector", lambda v, half=half,
                 o=state_bf[0:64, 64 * half:64 * (half + 1)],
                 i_=state_sb[0:64, 64 * half:64 * (half + 1)]:
                 v.tensor_copy(out=o, in_=i_),
                 reads=[stateb], writes=[stateb])
            rhs_src = qf2[sl] if half == 0 else qpan[sl]
            rhs_b = qf2b[sl] if half == 0 else qpanb[sl]
            for blk in range(NB):
                ob = blk % 4
                P.op("tensor", lambda t, half=half, blk=blk, ob=ob, rs=rhs_src:
                     t.matmul(ctx["psum"][0:64, 512 * ob:512 * ob + 512],
                              state_bf[0:64, 64 * half:64 * (half + 1)],
                              rs[0:64, 512 * blk:512 * (blk + 1)],
                              start=True, stop=True),
                     reads=[stateb, rhs_b], writes=[pbufs[ob]])
                P.op("vector", lambda v, half=half, blk=blk, ob=ob,
                     o=z_stage[64 * half:64 * (half + 1), 512 * blk:512 * (blk + 1)]:
                     v.tensor_copy(out=o, in_=ctx["psum"][0:64, 512 * ob:512 * ob + 512]),
                     reads=[pbufs[ob]], writes=[z_stageb])
        P.dma("sync", d["z"][128 * c:128 * (c + 1), :], z_stage,
              reads=[z_stageb], writes=[zb])
    A.release()
    P.barrier()

    A.mark()
    z_chunks = []
    zfmb = Buf("zfm")
    for c in range(ND):
        ap = A.bf16(T)
        z_chunks.append(ap)
        P.dma("sync", ap, d["z"][128 * c:128 * (c + 1), :], reads=[zb], writes=[zfmb])
    zsq = A.bf16(512)
    zsqb = Buf("zsqB")
    stats_sb = A.f32(512, parts=2)
    stats_sbb = Buf("statsB")
    stats_tm = A.f32(2 * NT)
    stats_tmb = Buf("statstmB")
    for blk in range(NB):
        for c in range(ND):
            P.op("scalar", lambda s, c=c, blk=blk, o=zsq:
                 s.activation(out=o, in_=z_chunks[c][:, 512 * blk:512 * (blk + 1)],
                              func=AF.Square),
                 reads=[zfmb], writes=[zsqb])
            def stfn(t, c=c, blk=blk):
                i1 = t.matmul(ctx["psum"][0:2, 512 * 7:512 * 7 + 512],
                              ctx["statpair"][:, 0:2],
                              z_chunks[c][:, 512 * blk:512 * (blk + 1)],
                              start=(c == 0), stop=False)
                i2 = t.matmul(ctx["psum"][0:2, 512 * 7:512 * 7 + 512],
                              ctx["statpair"][:, 2:4], zsq,
                              start=False, stop=(c == ND - 1))
                return i2
            P.op("tensor", stfn, reads=[zfmb, zsqb, ctx["statpairb"]],
                 writes=[pbufs[7]])
        P.op("vector", lambda v, blk=blk: v.tensor_copy(
             out=stats_sb[0:2, :], in_=ctx["psum"][0:2, 512 * 7:512 * 7 + 512]),
             reads=[pbufs[7]], writes=[stats_sbb])
        for q in range(4):
            tt = 4 * blk + q
            P.op("tensor", lambda t, q=q: t.transpose(
                 ctx["psum"][0:128, 512 * 6:512 * 6 + 2],
                 stats_sb[0:2, 128 * q:128 * (q + 1)], ctx["identf"][0:2, 0:2]),
                 reads=[stats_sbb, ctx["identfb"]], writes=[pbufs[6]])
            P.op("vector", lambda v, tt=tt, o=stats_tm[:, 2 * tt:2 * (tt + 1)]:
                 v.tensor_copy(out=o, in_=ctx["psum"][0:128, 512 * 6:512 * 6 + 2]),
                 reads=[pbufs[6]], writes=[stats_tmb])
    P.barrier()
    wo, wob = load_weight_fm(ctx, iw["B_wo"], ND, D)
    g_col = A.f32(ND)
    gb = Buf("gcolB")
    P.dma("sync", g_col, iw["B_g_col"], writes=[gb])
    be_col = A.f32(ND)
    beb = Buf("becolB")
    P.dma("sync", be_col, iw["B_b_col"], writes=[beb])
    gbf = A.bf16(ND)
    bbf = A.bf16(ND)
    gbfb = Buf("gbf")
    P.op("vector", lambda v: v.tensor_copy(out=gbf, in_=g_col), reads=[gb], writes=[gbfb])
    P.op("vector", lambda v: v.tensor_copy(out=bbf, in_=be_col), reads=[beb], writes=[gbfb])
    bo_row_f = A.f32(D, parts=1)
    bo_rowb = Buf("borowB")
    P.dma("sync", bo_row_f, iw["B_bo"].unsqueeze(0), writes=[bo_rowb])
    rowtmp = A.f32(D, parts=1)
    rowtmpb = Buf("rowtmp")
    def w1sumfn(t):
        last = None
        for c in range(ND):
            for half in range(2):
                last = t.matmul(ctx["psum"][0:1, 512 * (6 + half):512 * (6 + half) + 512],
                                gbf[:, c:c + 1], wo[c][:, 512 * half:512 * (half + 1)],
                                start=(c == 0), stop=(c == ND - 1))
        return last
    P.op("tensor", w1sumfn, reads=[wob, gbfb], writes=[pbufs[6], pbufs[7]])
    P.op("vector", lambda v: v.tensor_copy(out=rowtmp[:, 0:512],
                                           in_=ctx["psum"][0:1, 512 * 6:512 * 6 + 512]),
         reads=[pbufs[6]], writes=[rowtmpb])
    P.op("vector", lambda v: v.tensor_copy(out=rowtmp[:, 512:1024],
                                           in_=ctx["psum"][0:1, 512 * 7:512 * 7 + 512]),
         reads=[pbufs[7]], writes=[rowtmpb])
    P.dma("sync", d["row2"][0:1, :], rowtmp, reads=[rowtmpb], writes=[rowsb])
    def befn(t):
        last = None
        for c in range(ND):
            for half in range(2):
                last = t.matmul(ctx["psum"][0:1, 512 * (6 + half):512 * (6 + half) + 512],
                                bbf[:, c:c + 1], wo[c][:, 512 * half:512 * (half + 1)],
                                start=(c == 0), stop=(c == ND - 1))
        return last
    P.op("tensor", befn, reads=[wob, gbfb], writes=[pbufs[6], pbufs[7]])
    for half in range(2):
        P.op("vector", lambda v, half=half:
             v.tensor_add(out=rowtmp[:, 512 * half:512 * (half + 1)],
                          in0=ctx["psum"][0:1, 512 * (6 + half):512 * (6 + half) + 512],
                          in1=bo_row_f[:, 512 * half:512 * (half + 1)]),
             reads=[pbufs[6 + half], bo_rowb], writes=[rowtmpb])
    P.dma("sync", d["row2"][1:2, :], rowtmp, reads=[rowtmpb], writes=[rowsb])
    w1s_bc = A.f32(D)
    beff_bc = A.f32(D)
    bcb = Buf("bcB")
    P.dma("sync", w1s_bc, bass.AP(d["row2"].tensor, d["row2"].offset, [[0, 128], [1, D]]),
          reads=[rowsb], writes=[bcb])
    P.dma("sync", beff_bc, bass.AP(d["row2"].tensor, d["row2"].offset + D, [[0, 128], [1, D]]),
          reads=[rowsb], writes=[bcb])
    for c in range(ND):
        P.op("vector", lambda v, c=c, o=wo[c], s1=g_col[:, c:c + 1]:
             v.tensor_scalar_mul(out=o, in0=o, scalar1=s1),
             reads=[wob], scalar_reads=[gb], writes=[wob])
    xt = [A.f32(D) for _ in range(2)]
    xb_ = [Buf(f"bx{i}") for i in range(2)]
    x1 = [A.f32(D) for _ in range(2)]
    x1b = [Buf(f"bx1{i}") for i in range(2)]
    t1 = [A.f32(D) for _ in range(2)]
    t1b = [Buf(f"bt1{i}") for i in range(2)]
    dump = A.bf16(D)
    dumpb = Buf("bdump")
    cols = A.f32(16)
    colb = [Buf(f"bc{i}") for i in range(2)]
    lcols = A.f32(8)
    lcolb = [Buf(f"blc{i}") for i in range(2)]

    def final_consume(tt, psum_ap, bankbufs):
        sl = tt % 2
        cb = colb[sl]
        lcb = lcolb[sl]
        mu_c = stats_tm[:, 2 * tt:2 * tt + 1]
        msq_c = stats_tm[:, 2 * tt + 1:2 * tt + 2]
        v_c = lcols[:, 4 * sl + 0:4 * sl + 1]
        rst_c = lcols[:, 4 * sl + 1:4 * sl + 2]
        nrm_c = lcols[:, 4 * sl + 2:4 * sl + 3]
        P.op("scalar", lambda s: s.activation(out=v_c, in_=mu_c, func=AF.Square),
             reads=[stats_tmb], writes=[lcb])
        P.op("vector", lambda v: v.tensor_scalar(out=v_c, in0=msq_c, scalar1=v_c,
                                                 scalar2=None, op0=ALU.subtract),
             reads=[stats_tmb], scalar_reads=[lcb], writes=[lcb])
        P.op("scalar", lambda s, e=ctx["epscol"]:
             s.activation(out=rst_c, in_=v_c, func=AF.Ln, bias=e),
             reads=[lcb], scalar_reads=[ctx["epscolb"]], writes=[lcb])
        P.op("scalar", lambda s: s.activation(out=rst_c, in_=rst_c, func=AF.Exp, scale=-0.5),
             reads=[lcb], writes=[lcb])
        P.op("vector", lambda v: v.tensor_scalar(out=nrm_c, in0=mu_c, scalar1=rst_c,
                                                 scalar2=-1.0, op0=ALU.mult, op1=ALU.mult),
             reads=[stats_tmb], scalar_reads=[lcb], writes=[lcb])
        P.op("vector", lambda v, sl=sl, p=psum_ap, o=t1[sl], s1=rst_c:
             v.tensor_scalar_mul(out=o, in0=p, scalar1=s1),
             reads=bankbufs, scalar_reads=[lcb], writes=[t1b[sl]])
        P.op("vector", lambda v, sl=sl, o=t1[sl], s1=nrm_c:
             v.scalar_tensor_tensor(out=o, in0=w1s_bc, scalar=s1, in1=o,
                                    op0=ALU.mult, op1=ALU.add),
             reads=[bcb, t1b[sl]], scalar_reads=[lcb], writes=[t1b[sl]])
        P.op("vector", lambda v, sl=sl, o=t1[sl]:
             v.tensor_add(out=o, in0=o, in1=beff_bc),
             reads=[t1b[sl], bcb], writes=[t1b[sl]])
        P.dma("sync", xt[sl], x_src_dram[128 * tt:128 * (tt + 1), :], writes=[xb_[sl]])
        s_c = cols[:, 5 * sl + 0:5 * sl + 1]
        ss_c = cols[:, 5 * sl + 1:5 * sl + 2]
        P.op("vector", lambda v, sl=sl, o=s_c: v.reduce_sum(out=o, in_=t1[sl], axis=AX.X),
             reads=[t1b[sl]], writes=[cb])
        P.op("scalar", lambda s, sl=sl, o=dump, a=ss_c:
             s.activation(out=o, in_=t1[sl], func=AF.Square, accum_out=a),
             reads=[t1b[sl]], writes=[dumpb, cb])
        rstd_c, nmr_c = emit_ln_cols(ctx, cols, sl, cb)
        P.op("vector", lambda v, sl=sl, s1=nmr_c:
             v.tensor_scalar_add(out=xt[sl], in0=xt[sl], scalar1=s1),
             reads=[xb_[sl]], scalar_reads=[cb], writes=[xb_[sl]])
        P.op("vector", lambda v, sl=sl, o=x1[sl], s1=rstd_c:
             v.scalar_tensor_tensor(out=o, in0=t1[sl], scalar=s1, in1=xt[sl],
                                    op0=ALU.mult, op1=ALU.add),
             reads=[t1b[sl], xb_[sl]], scalar_reads=[cb], writes=[x1b[sl]])
        P.dma("sync", x_dst_dram[128 * tt:128 * (tt + 1), :], x1[sl], reads=[x1b[sl]])

    emit_proj_formA(ctx, z_chunks, zfmb, wo, wob, ND, D,
                    bias_row=None, biasrowb=None, consume=final_consume, group=4)
    A.release()
    A.release()
    P.barrier()


def emit_block_G(ctx, x_src_dram, x_dst_dram, iw):
    P, A, nc = ctx["P"], ctx["arena"], ctx["nc"]
    NE, PS = 64, 32
    NP = NE * PS
    d = ctx["dram"]
    pbufs = ctx["psum_bufs"]
    ddb, ccb2, ccinb, ccoutb = Buf("ddG"), Buf("ccG"), Buf("ccinG"), Buf("ccoutG")

    emit_ln_tm(ctx, x_src_dram, d["xn"])
    P.barrier()

    A.mark()
    A.mark()
    xn_chunks, xnbuf = load_xn_fm(ctx, d["xn"])
    phi, phib = load_weight_fm(ctx, iw["G_phi"], ND, NP)
    pb_row = A.bf16(NP, parts=1)
    pbb = Buf("pbG")
    P.dma("gpsimd", pb_row, iw["G_phib"].unsqueeze(0), writes=[pbb])
    eEL = [A.bf16(NP) for _ in range(2)]
    eEbL = [Buf(f"eEG{i}") for i in range(2)]
    dd_tL = [A.bf16(NP) for _ in range(2)]
    dd_tbL = [Buf(f"ddt{i}") for i in range(2)]
    cc_tL = [A.bf16(NP) for _ in range(2)]
    cc_tbL = [Buf(f"cct{i}") for i in range(2)]
    segDL = [A.f32(NE) for _ in range(2)]
    segDbL = [Buf(f"segD{i}") for i in range(2)]
    segCL = [A.f32(PS) for _ in range(2)]
    segCbL = [Buf(f"segC{i}") for i in range(2)]
    gcolsL = [A.f32(2) for _ in range(2)]
    gcolsbL = [Buf(f"gcols{i}") for i in range(2)]

    def lg_consume(tt, psum_ap, bankbufs):
        sl = tt % 2
        eE, eEb = eEL[sl], eEbL[sl]
        dd_t, dd_tb = dd_tL[sl], dd_tbL[sl]
        cc_t, cc_tb = cc_tL[sl], cc_tbL[sl]
        segD, segDb = segDL[sl], segDbL[sl]
        segC, segCb = segCL[sl], segCbL[sl]
        gcols, gcolsb = gcolsL[sl], gcolsbL[sl]
        m_c = gcols[:, 0:1]
        nm_c = gcols[:, 1:2]
        P.op("vector", lambda v, p=psum_ap, o=m_c: v.reduce_max(out=o, in_=p, axis=AX.X),
             reads=bankbufs, writes=[gcolsb])
        P.op("vector", lambda v: v.tensor_scalar_mul(out=nm_c, in0=m_c, scalar1=-1.0),
             reads=[gcolsb], writes=[gcolsb])
        P.op("scalar", lambda s, p=psum_ap: s.activation(out=eE, in_=p, func=AF.Exp,
                                                         bias=nm_c),
             reads=bankbufs, scalar_reads=[gcolsb], writes=[eEb])
        ev_d = eE[:].rearrange("p (n q) -> p n q", n=NE)
        P.op("vector", lambda v, o=segD[:, 0:NE], i_=ev_d:
             v.reduce_sum(out=o, in_=i_, axis=AX.X),
             reads=[eEb], writes=[segDb])
        P.op("scalar", lambda s: s.activation(out=segD[:, 0:NE], in_=segD[:, 0:NE],
                                              func=AF.Ln),
             reads=[segDb], writes=[segDb])
        P.op("scalar", lambda s: s.activation(out=segD[:, 0:NE], in_=segD[:, 0:NE],
                                              func=AF.Exp, scale=-1.0),
             reads=[segDb], writes=[segDb])
        P.op("vector", lambda v, o=dd_t[:].rearrange("p (n q) -> p n q", n=NE), i_=ev_d,
             b=segD[:, 0:NE].unsqueeze(2).broadcast_to([128, NE, PS]):
             v.tensor_tensor(out=o, in0=i_, in1=b, op=ALU.mult),
             reads=[eEb], scalar_reads=[segDb], writes=[dd_tb])
        ev_c = eE[:].rearrange("p (n q) -> p q n", n=NE)
        P.op("vector", lambda v, o=segC[:, 0:PS], i_=ev_c:
             v.reduce_sum(out=o, in_=i_, axis=AX.X),
             reads=[eEb], writes=[segCb])
        P.op("scalar", lambda s: s.activation(out=segC[:, 0:PS], in_=segC[:, 0:PS],
                                              func=AF.Ln),
             reads=[segCb], writes=[segCb])
        P.op("scalar", lambda s: s.activation(out=segC[:, 0:PS], in_=segC[:, 0:PS],
                                              func=AF.Exp, scale=-1.0),
             reads=[segCb], writes=[segCb])
        P.op("vector", lambda v, o=cc_t[:].rearrange("p (n q) -> p q n", n=NE), i_=ev_c,
             b=segC[:, 0:PS].unsqueeze(2).broadcast_to([128, PS, NE]):
             v.tensor_tensor(out=o, in0=i_, in1=b, op=ALU.mult),
             reads=[eEb], scalar_reads=[segCb], writes=[cc_tb])
        P.dma("sync", d["dd"][128 * tt:128 * (tt + 1), :], dd_t,
              reads=[dd_tb], writes=[ddb])
        P.dma("sync", d["cc"][128 * tt:128 * (tt + 1), :], cc_t,
              reads=[cc_tb], writes=[ccb2])

    emit_proj_formA(ctx, xn_chunks, xnbuf, phi, phib, ND, NP,
                    bias_row=pb_row, biasrowb=pbb, consume=lg_consume, group=2)
    A.release()
    P.barrier()

    A.mark()
    xn_tm = A.bf16(NT * D)
    xn_tmb = Buf("xntmG")
    for tt in range(NT):
        P.dma("sync", xn_tm[:, D * tt:D * (tt + 1)], d["xn"][128 * tt:128 * (tt + 1), :],
              writes=[xn_tmb])
    X_sb = A.f32(16 * D // 2)
    X_sb = A.f32(16 * D)
    X_sbb = Buf("XsbG")
    dslice = [A.bf16(512) for _ in range(2)]
    dsliceb = [Buf(f"dsl{i}") for i in range(2)]
    for ps in range(4):
        for tt in range(NT):
            sl = tt % 2
            P.dma("sync", dslice[sl], d["dd"][128 * tt:128 * (tt + 1), 512 * ps:512 * (ps + 1)],
                  reads=[ddb], writes=[dsliceb[sl]])
            def dfn(t, tt=tt, sl=sl):
                last = None
                for npc in range(4):
                    for half in range(2):
                        last = t.matmul(ctx["psum"][:, 512 * (2 * npc + half):512 * (2 * npc + half) + 512],
                                        dslice[sl][:, 128 * npc:128 * (npc + 1)],
                                        xn_tm[:, D * tt + 512 * half:D * tt + 512 * (half + 1)],
                                        start=(tt == 0), stop=(tt == NT - 1))
                return last
            P.op("tensor", dfn, reads=[dsliceb[sl], xn_tmb], writes=list(pbufs))
        for npc in range(4):
            npi = 4 * ps + npc
            P.op("vector", lambda v, npc=npc, npi=npi,
                 o=X_sb[:, D * npi:D * (npi + 1)]:
                 v.tensor_copy(out=o, in_=ctx["psum"][:, 1024 * npc:1024 * (npc + 1)]),
                 reads=[pbufs[2 * npc], pbufs[2 * npc + 1]], writes=[X_sbb])
    for npi in range(16):
        P.dma("sync", d["ccin"][128 * npi:128 * (npi + 1), :],
              X_sb[:, D * npi:D * (npi + 1)], reads=[X_sbb], writes=[ccinb])
    A.release()
    P.op("gpsimd", lambda g: g.collective_compute(
         "ReduceScatter", ALU.add, ins=[d["ccin"].opt()], outs=[d["ccout"].opt()],
         replica_groups=[[0, 1, 2, 3, 4, 5, 6, 7]]),
         reads=[ccinb], writes=[ccoutb])
    P.barrier()

    XtT = A.bf16(ND * 256)
    XtTb = Buf("XtT")
    Yt = A.bf16(16 * D)
    Ytb = Buf("Yt")
    cfm = [A.bf16(16 * 512) for _ in range(4)]
    cfmb = [Buf(f"cfm{i}") for i in range(4)]

    def cfm_load(tg):
        sl = tg % 4
        for npi in range(16):
            P.dma("sync", cfm[sl][:, 512 * npi:512 * (npi + 1)],
                  d["cc"][512 * tg:512 * (tg + 1), 128 * npi:128 * (npi + 1)],
                  reads=[ccb2], writes=[cfmb[sl]], transpose=True)
    A.mark()
    Xt = A.bf16(2 * D)
    Xtb = Buf("Xt")
    xrow = A.f32(D)
    xrowb = Buf("xrowG")
    cols = A.f32(16)
    colb = [Buf(f"gc{i}") for i in range(2)]
    dump = A.bf16(D)
    dumpb = Buf("gdump")
    for npi in range(2):
        sl = npi % 2
        cb = colb[sl]
        P.dma("sync", xrow, d["ccout"][128 * npi:128 * (npi + 1), :],
              reads=[ccoutb], writes=[xrowb])
        P.op("vector", lambda v: v.tensor_scalar_min(out=xrow, in0=xrow, scalar1=65000.0),
             reads=[xrowb], writes=[xrowb])
        P.op("vector", lambda v: v.tensor_scalar_max(out=xrow, in0=xrow, scalar1=-33000.0),
             reads=[xrowb], writes=[xrowb])
        s_c = cols[:, 5 * sl + 0:5 * sl + 1]
        ss_c = cols[:, 5 * sl + 1:5 * sl + 2]
        P.op("vector", lambda v, o=s_c: v.reduce_sum(out=o, in_=xrow, axis=AX.X),
             reads=[xrowb], writes=[cb])
        P.op("scalar", lambda s, o=dump, a=ss_c:
             s.activation(out=o, in_=xrow, func=AF.Square, accum_out=a),
             reads=[xrowb], writes=[dumpb, cb])
        rstd_c, nmr_c = emit_ln_cols(ctx, cols, sl, cb)
        P.op("vector", lambda v, s1=rstd_c, s2=nmr_c:
             v.tensor_scalar(out=xrow, in0=xrow, scalar1=s1, scalar2=s2,
                             op0=ALU.mult, op1=ALU.add),
             reads=[xrowb], scalar_reads=[cb], writes=[xrowb])
        P.op("scalar", lambda s, npi=npi, o=Xt[:, D * npi:D * (npi + 1)]:
             s.activation(out=o, in_=xrow, func=AF.Tanh),
             reads=[xrowb], writes=[Xtb])
    psbf = ctx["psum"].bitcast(BF16)
    for npi in range(2):
        for j in range(ND):
            pb = 5 + (npi * ND + j) % 2
            P.op("tensor", lambda t, npi=npi, j=j, pb=pb:
                 t.transpose(psbf[:, 1024 * pb:1024 * pb + 128],
                             Xt[:, D * npi + 128 * j:D * npi + 128 * (j + 1)],
                             ctx["ident"]),
                 reads=[Xtb, ctx["identb"]], writes=[pbufs[pb]])
            P.op("vector", lambda v, npi=npi, j=j, pb=pb,
                 o=XtT[:, 256 * j + 128 * npi:256 * j + 128 * (npi + 1)]:
                 v.tensor_copy(out=o, in_=psbf[:, 1024 * pb:1024 * pb + 128]),
                 reads=[pbufs[pb]], writes=[XtTb])
    A.release()
    P.barrier()

    A.mark()
    ewc = [A.bf16(D) for _ in range(4)]
    ewcb = [Buf(f"ewc{i}") for i in range(4)]
    ebrow = A.bf16(D, parts=1)
    ebrowb = Buf("ebrow")
    Ytl = A.bf16(2 * D)
    Ytlb = Buf("Ytl")
    ytinb = Buf("ytin")
    ytoutb = Buf("ytout")
    for e in range(8):
        for j in range(ND):
            sl = (e * ND + j) % 4
            P.dma("gpsimd", ewc[sl],
                  iw["G_ew"][e, 128 * j:128 * (j + 1), :], writes=[ewcb[sl]])
            def efn(t, e=e, j=j, sl=sl):
                last = None
                for half in range(2):
                    last = t.matmul(ctx["psum"][0:PS, 512 * (6 + half):512 * (6 + half) + 512],
                                    XtT[:, 256 * j + 32 * e:256 * j + 32 * (e + 1)],
                                    ewc[sl][:, 512 * half:512 * (half + 1)],
                                    start=(j == 0), stop=False)
                return last
            P.op("tensor", efn, reads=[XtTb, ewcb[sl]], writes=[pbufs[6], pbufs[7]])
        P.dma("gpsimd", ebrow, iw["G_eb"][e:e + 1, :], writes=[ebrowb])
        def ebfn(t, e=e):
            last = None
            for half in range(2):
                last = t.matmul(ctx["psum"][0:PS, 512 * (6 + half):512 * (6 + half) + 512],
                                ctx["ones_row"][:, 0:PS],
                                ebrow[:, 512 * half:512 * (half + 1)],
                                start=False, stop=True)
            return last
        P.op("tensor", ebfn, reads=[ebrowb, ctx["ones_rowb"]],
             writes=[pbufs[6], pbufs[7]])
        P.op("vector", lambda v, e=e,
             o=Ytl[32 * (e % 4):32 * (e % 4 + 1), D * (e // 4):D * (e // 4 + 1)]:
             v.tensor_copy(out=o, in_=ctx["psum"][0:PS, 512 * 6:512 * 6 + 1024]),
             reads=[pbufs[6], pbufs[7]], writes=[Ytlb])
    cfm_load(0)
    cfm_load(1)
    for i in range(2):
        P.dma("gpsimd", d["ytin"][128 * i:128 * (i + 1), :], Ytl[:, D * i:D * (i + 1)],
              reads=[Ytlb], writes=[ytinb])
    P.op("gpsimd", lambda g: g.collective_compute(
         "AllGather", ALU.bypass, ins=[d["ytin"].opt()], outs=[d["ytout"].opt()],
         replica_groups=[[0, 1, 2, 3, 4, 5, 6, 7]]),
         reads=[ytinb], writes=[ytoutb])
    for npi in range(16):
        P.dma("scalar", Yt[:, D * npi:D * (npi + 1)],
              d["ytout"][128 * npi:128 * (npi + 1), :],
              reads=[ytoutb], writes=[Ytb])
    A.release()
    P.barrier()

    A.mark()
    consume = emit_outer_ln_residual_consume(ctx, x_src_dram, x_dst_dram)
    for tg in range(8):
        sl = tg % 4
        if tg + 2 < 8:
            cfm_load(tg + 2)
        for tti in range(4):
            tt = 4 * tg + tti
            b0 = tti * 2
            def cfn(t, tti=tti, sl=sl, b0=b0):
                last = None
                for npi in range(16):
                    for half in range(2):
                        last = t.matmul(
                            ctx["psum"][:, 512 * (b0 + half):512 * (b0 + half) + 512],
                            cfm[sl][:, 512 * npi + 128 * tti:512 * npi + 128 * (tti + 1)],
                            Yt[:, D * npi + 512 * half:D * npi + 512 * (half + 1)],
                            start=(npi == 0), stop=(npi == 15))
                return last
            P.op("tensor", cfn, reads=[cfmb[sl], Ytb],
                 writes=[pbufs[b0], pbufs[b0 + 1]])
            consume(tt, ctx["psum"][:, 512 * b0:512 * b0 + D],
                    [pbufs[b0], pbufs[b0 + 1]])
    A.release()
    A.release()
    P.barrier()


def build_program(blocks=("A", "F", "B", "G"), debug_taps=()):
    import contextlib
    nc = bass.Bass("TRN2", target_bir_lowering=False, debug=False)
    stack = contextlib.ExitStack()

    iw = {}

    def ein(name, shape, dtype=F32):
        iw[name] = nc.dram_tensor(name, list(shape), dtype, kind="ExternalInput").ap()
        return iw[name]

    ein("x_in", (T, D))
    if "A" in blocks:
        ein("A_wq", (D, D)); ein("A_wk", (D, D)); ein("A_wv", (D, D)); ein("A_wo", (D, D))
        ein("A_bq_col", (128, ND)); ein("A_bk_col", (128, ND))
        ein("A_bv", (D,)); ein("A_bo", (D,))
    if "B" in blocks:
        ein("B_wq", (D, D)); ein("B_bq", (D,))
        ein("B_wk", (D, D)); ein("B_bk", (D,))
        ein("B_wv", (D, D)); ein("B_bv", (D,))
        ein("B_wo", (D, D)); ein("B_bo", (D,))
        ein("B_g_col", (128, ND)); ein("B_b_col", (128, ND))
    if "G" in blocks:
        ein("G_phi", (D, 2048)); ein("G_phib", (2048,))
        ein("G_ew", (8, D, D)); ein("G_eb", (8, D))
    if "F" in blocks:
        ein("F_qw", (D, 512)); ein("F_qb_col", (128, 4))
        ein("F_kw", (D, 512)); ein("F_kb_col", (128, 4))
        ein("F_vw", (D, 512)); ein("F_vb", (512,))
        ein("F_ow", (512, D)); ein("F_ob_col", (128, ND))
        ein("F_rw", (D, 8)); ein("F_rb", (8,))
        ein("F_w1", (8, D, 2048)); ein("F_b1_col", (8, 128, 16))
        ein("F_w2", (8, 2048, D)); ein("F_b2", (8, D))
    ein("c_ident", (128, 128))
    ein("c_ones_row", (1, 128))
    ein("c_onesf_row", (1, 128))
    ein("c_masks", (128, 2048))
    ein("c_ropecc", (128, T))
    ein("c_ropess", (128, T))
    ein("c_headsum2", (128, 2))
    ein("c_headsum4", (128, 4))
    ein("c_sel2", (2, 256))
    ein("c_epscol", (128, 1))
    ein("c_negc", (128, 1))
    ein("c_statpair", (128, 4))
    ein("c_identf", (128, 128))
    ein("c_onescol", (128, 1))
    ein("c_e8", (8, 1024))

    out = nc.dram_tensor("x_out", [T, D], F32, kind="ExternalOutput").ap()

    dram = {}
    for nm, shape, dt in [
        ("xn", (T, D), BF16), ("q", (D, T), BF16), ("k", (D, T), BF16),
        ("q0", (D, T), BF16),
        ("edbg", (4 * 128, 512), BF16), ("ddbg", (4, 512), F32),
        ("bdbg", (128, 16), F32), ("mdbg", (2, 16), F32),
        ("negd", (2, 8), F32), ("rrow", (1, 512), F32),
        ("negd2", (4, 4), F32), ("ao", (512, T), BF16),
        ("mla", (D, T), BF16), ("moe", (T, D), F32),
        ("qh", (T, D), BF16), ("kh", (T, D), BF16), ("vh", (T, D), BF16),
        ("z", (D, T), BF16), ("row2", (2, D), F32),
        ("dd", (T, 2048), BF16), ("cc", (T, 2048), BF16),
        ("denh", (16, T), F32), ("ralla", (16, T), F32), ("dsnap", (16, T), F32),
        ("ctdbg", (8, T), BF16), ("rldbg", (T, 8), F32), ("rp_dbg", (T, 24), F32),
        ("v", (T, 16 * 65), BF16), ("o", (D, T), BF16),
        ("x1", (T, D), F32), ("x2", (T, D), F32), ("x3", (T, D), F32),
    ]:
        dram[nm] = nc.dram_tensor("scr_" + nm, list(shape), dt, kind="Internal").ap()

    dram["ccin"] = nc.dram_tensor("ccin", [2048, D], F32, kind="Internal").ap()
    dram["ccout"] = nc.dram_tensor("ccout", [256, D], F32, kind="Internal").ap()
    dram["ytin"] = nc.dram_tensor("ytin", [256, D], BF16, kind="Internal").ap()
    dram["ytout"] = nc.dram_tensor("ytout", [2048, D], BF16, kind="Internal").ap()
    P = Prog(nc)
    ctx = {"nc": nc, "P": P, "dram": dram,
           "psum_bufs": [Buf(f"psum{i}") for i in range(8)],
           "negd2b": Buf("negd2"), "rrowb": Buf("rrow"), "negdb": Buf("negd")}

    with stack:
        arena = Arena(nc, stack)
        ctx["arena"] = arena
        psum = stack.enter_context(nc.psum_tensor("psum", [128, 4096], F32))
        ctx["psum"] = psum[:]

        def const_sb(name, cols, parts=128, f32=False):
            ap = arena.f32(cols, parts=parts) if f32 else arena.bf16(cols, parts=parts)
            b = Buf(name)
            q = "gpsimd" if not f32 else "sync"
            P.dma(q, ap, iw[name], writes=[b])
            b.const = True
            return ap, b

        ctx["ident"], ctx["identb"] = const_sb("c_ident", 128)
        ctx["ones_row"], ctx["ones_rowb"] = const_sb("c_ones_row", 128, parts=1)
        ctx["ones_col_f"], ctx["ones_col_fb"] = const_sb("c_onesf_row", 128, parts=1, f32=True)
        mm_ap = arena.bf16(4 * 512)
        mb = Buf("masks")
        P.dma("gpsimd", mm_ap, iw["c_masks"], writes=[mb])
        mb.const = True
        ctx["masks"] = [mm_ap[:, 512 * i:512 * (i + 1)] for i in range(4)]
        ctx["masksb"] = mb
        ctx["headsum2"], ctx["headsum2b"] = const_sb("c_headsum2", 2)
        ctx["headsum4"], ctx["headsum4b"] = const_sb("c_headsum4", 4)
        ctx["sel2"], ctx["sel2b"] = const_sb("c_sel2", 256, parts=2, f32=True)
        ctx["epscol"], ctx["epscolb"] = const_sb("c_epscol", 1, f32=True)
        ctx["negc"], ctx["negcb"] = const_sb("c_negc", 1, f32=True)
        ctx["statpair"], ctx["statpairb"] = const_sb("c_statpair", 4)
        ctx["identf"], ctx["identfb"] = const_sb("c_identf", 128, f32=True)
        ctx["ones_colb"], ctx["ones_colbb"] = const_sb("c_onescol", 1)
        ctx["e8"], ctx["e8b"] = const_sb("c_e8", 1024, parts=8)

        cur = iw["x_in"]
        scratch_x = [dram["x1"], dram["x2"], dram["x3"]]
        for bi, b in enumerate(blocks):
            dst = out if bi == len(blocks) - 1 else scratch_x[bi]
            if b == "A":
                emit_block_A(ctx, cur, dst, iw)
            elif b == "F":
                emit_block_F(ctx, cur, dst, iw)
            elif b == "B":
                emit_block_B(ctx, cur, dst, iw)
            elif b == "G":
                emit_block_G(ctx, cur, dst, iw)
            else:
                raise NotImplementedError(b)
            cur = dst
        P.barrier()
        for nm in debug_taps:
            src = dram[nm]
            tap = nc.dram_tensor("tap_" + nm, list(src.shape), src.dtype,
                                 kind="ExternalOutput").ap()
            P.dma("sync", tap, src)
        P.barrier()
        P.finalize_and_emit(stack)
    return nc


def _bias_col(b):
    return np.ascontiguousarray(np.asarray(b, dtype=np.float32).reshape(-1, 128).T)


def _rope_perm_np(H=16, HD=64):
    p = []
    for h in range(H):
        p += [h * HD + 2 * i for i in range(HD // 2)]
        p += [h * HD + 2 * i + 1 for i in range(HD // 2)]
    return np.array(p, dtype=np.int64)


def make_consts():
    c = {}
    c["c_ident"] = np.eye(128, dtype=np.float32)
    c["c_ones_row"] = np.ones((1, 128), dtype=np.float32)
    c["c_onesf_row"] = np.ones((1, 128), dtype=np.float32)
    masks = []
    for off in (0, 128, 256, 384):
        kk = np.arange(128)[:, None]
        qq = np.arange(512)[None, :]
        masks.append(((kk + off) <= qq).astype(np.float32))
    c["c_masks"] = np.concatenate(masks, 1)
    theta = 10000.0 ** (-np.arange(0, 64, 2, dtype=np.float64) / 64)
    fr = np.outer(theta, np.arange(T, dtype=np.float64))
    sin, cos = np.sin(fr), np.cos(fr)
    cc_blk = np.concatenate([cos, cos], 0)
    ss_blk = np.concatenate([-sin, sin], 0)
    c["c_ropecc"] = np.tile(cc_blk, (2, 1)).astype(np.float32)
    c["c_ropess"] = np.tile(ss_blk, (2, 1)).astype(np.float32)
    hs2 = np.zeros((128, 2), dtype=np.float32)
    hs2[:64, 0] = 1.0
    hs2[64:, 1] = 1.0
    c["c_headsum2"] = hs2
    hs4 = np.zeros((128, 4), dtype=np.float32)
    for j in range(4):
        hs4[32 * j:32 * (j + 1), j] = 1.0
    c["c_headsum4"] = hs4
    sel = np.zeros((2, 256), dtype=np.float32)
    sel[0, :128] = 1.0
    sel[1, 128:] = 1.0
    c["c_sel2"] = sel
    c["c_epscol"] = np.full((128, 1), 1e-5, dtype=np.float32)
    c["c_negc"] = np.full((128, 1), -KSOFT_C, dtype=np.float32)
    sp = np.zeros((128, 4), dtype=np.float32)
    sp[:, 0] = 1.0 / D
    sp[:, 3] = 1.0 / D
    c["c_statpair"] = sp
    c["c_identf"] = np.eye(128, dtype=np.float32)
    c["c_onescol"] = np.ones((128, 1), dtype=np.float32)
    e8 = np.zeros((8, 1024), dtype=np.float32)
    for e in range(8):
        e8[e, 128 * e:128 * (e + 1)] = 1.0
    c["c_e8"] = e8
    return c


def prep_inputs_A(inputs):
    perm = _rope_perm_np()
    m = {}
    m["A_wq"] = np.ascontiguousarray(np.asarray(inputs["L0_attn_Wq"], np.float32).T[:, perm])
    m["A_wk"] = np.ascontiguousarray(np.asarray(inputs["L0_attn_Wk"], np.float32).T[:, perm])
    m["A_wv"] = np.ascontiguousarray(np.asarray(inputs["L0_attn_Wv"], np.float32).T)
    m["A_wo"] = np.ascontiguousarray(np.asarray(inputs["L0_attn_Wo"], np.float32).T)
    m["A_bq_col"] = _bias_col(np.asarray(inputs["L0_attn_bq"])[perm])
    m["A_bk_col"] = _bias_col(np.asarray(inputs["L0_attn_bk"])[perm])
    m["A_bv"] = np.ascontiguousarray(np.asarray(inputs["L0_attn_bv"], np.float32))
    m["A_bo"] = np.ascontiguousarray(np.asarray(inputs["L0_attn_bo"], np.float32))
    return m


def prep_inputs_F(inputs):
    m = {}
    qw = np.asarray(inputs["L0_ffn_q_w"], np.float32)
    used = np.concatenate([np.arange(64 * h, 64 * h + 32) for h in range(16)])
    m["F_qw"] = np.ascontiguousarray(qw[:, used])
    qb = np.asarray(inputs["L0_ffn_q_b"], np.float32)[used]
    m["F_qb_col"] = _bias_col(qb)
    m["F_kw"] = np.ascontiguousarray(np.asarray(inputs["L0_ffn_k_w"], np.float32))
    m["F_kb_col"] = _bias_col(np.asarray(inputs["L0_ffn_k_b"], np.float32))
    m["F_vw"] = np.ascontiguousarray(np.asarray(inputs["L0_ffn_v_w"], np.float32))
    m["F_vb"] = np.ascontiguousarray(np.asarray(inputs["L0_ffn_v_b"], np.float32))
    m["F_ow"] = np.ascontiguousarray(np.asarray(inputs["L0_ffn_o_w"], np.float32)[:512])
    m["F_ob_col"] = _bias_col(np.asarray(inputs["L0_ffn_o_b"], np.float32))
    m["F_rw"] = np.ascontiguousarray(np.asarray(inputs["L0_ffn_router_w"], np.float32))
    m["F_rb"] = np.ascontiguousarray(np.asarray(inputs["L0_ffn_router_b"], np.float32))
    m["F_w1"] = np.ascontiguousarray(np.asarray(inputs["L0_ffn_w1"], np.float32))
    b1 = np.asarray(inputs["L0_ffn_b1"], np.float32)
    m["F_b1_col"] = np.ascontiguousarray(b1.reshape(8, 16, 128).transpose(0, 2, 1))
    m["F_w2"] = np.ascontiguousarray(np.asarray(inputs["L0_ffn_w2"], np.float32))
    m["F_b2"] = np.ascontiguousarray(np.asarray(inputs["L0_ffn_b2"], np.float32))
    return m


def prep_inputs_B(inputs):
    m = {}
    for nm, w in (("wq", "Wq"), ("wk", "Wk"), ("wv", "Wv"), ("wo", "Wo")):
        m["B_" + nm] = np.ascontiguousarray(
            np.asarray(inputs["L1_attn_" + w], np.float32).T)
    for nm, b in (("bq", "bq"), ("bk", "bk"), ("bv", "bv"), ("bo", "bo")):
        m["B_" + nm] = np.ascontiguousarray(np.asarray(inputs["L1_attn_" + b], np.float32))
    m["B_g_col"] = _bias_col(np.asarray(inputs["L1_attn_ln_g"], np.float32))
    m["B_b_col"] = _bias_col(np.asarray(inputs["L1_attn_ln_b"], np.float32))
    return m


def prep_inputs_G(inputs):
    m = {}
    m["G_phi"] = np.ascontiguousarray(np.asarray(inputs["L1_ffn_phi_w"], np.float32).T)
    m["G_phib"] = np.ascontiguousarray(np.asarray(inputs["L1_ffn_phi_b"], np.float32))
    return m


def run_blocks(inputs, blocks=("A",), x_override=None, debug_taps=(), raw=False):
    nc = build_program(blocks=blocks, debug_taps=debug_taps)
    shared = make_consts()
    shared.update(prep_inputs_A(inputs))
    shared.update(prep_inputs_F(inputs))
    shared.update(prep_inputs_B(inputs))
    shared.update(prep_inputs_G(inputs))
    x = np.asarray(x_override if x_override is not None else inputs["L0_attn_x"],
                   dtype=np.float32)
    g_ew = np.ascontiguousarray(np.asarray(inputs["L1_ffn_expert_w"], np.float32))
    g_eb = np.ascontiguousarray(np.asarray(inputs["L1_ffn_expert_b"], np.float32))
    in_maps = []
    for b in range(8):
        im = dict(shared)
        im["x_in"] = np.ascontiguousarray(x[b])
        im["G_ew"] = g_ew[8 * b:8 * (b + 1)]
        im["G_eb"] = g_eb[8 * b:8 * (b + 1)]
        in_maps.append(im)
    res = run_bass_kernel_spmd(nc, in_maps, core_ids=list(range(8)))
    if raw:
        return res
    return np.stack([r["x_out"] for r in res.results], 0)


def kernel(**inputs):
    return run_blocks(inputs, blocks=("A", "F", "B", "G"))


if __name__ == "__main__":
    pass


# version 05 (final)
# speedup vs baseline: 1.1368x; 1.0327x over previous
import math
import numpy as np
import ml_dtypes
import concourse.bass as bass
import concourse.mybir as mybir
from concourse.bass_utils import run_bass_kernel_spmd

F32 = mybir.dt.float32
BF16 = mybir.dt.bfloat16
AF = mybir.ActivationFunctionType
ALU = mybir.AluOpType
AX = mybir.AxisListType

ENGINES = ("tensor", "vector", "scalar", "gpsimd", "sync")
DMA_QUEUES = ("sync", "gpsimd", "scalar")

T = 4096
D = 1024
NT = T // 128
ND = D // 128
NB = T // 512
LN_EPS = 1e-5
DBG = False
DBG2 = False


class Buf:
    __slots__ = ("name", "writers", "readers", "const")

    def __init__(self, name="", const=False):
        self.name = name
        self.writers = []
        self.readers = []
        self.const = const


class _Op:
    __slots__ = ("idx", "eng", "fn", "deps", "dma", "needs_inc", "sem", "val", "tag",
                 "small")

    def __init__(self, idx, eng, fn, dma):
        self.tag = ""
        self.small = False
        self.idx = idx
        self.eng = eng
        self.fn = fn
        self.dma = dma
        self.deps = {}
        self.needs_inc = False
        self.sem = None
        self.val = None


class Prog:
    N_DMA_SEMS = 20
    N_EXEC_SEMS = 4

    def __init__(self, nc):
        self.nc = nc
        self.ops = []
        self.eng_ops = {e: [] for e in ENGINES}
        from collections import deque
        self.recent_dma = {e: deque(maxlen=self.N_DMA_SEMS) for e in ENGINES}

    def op(self, eng, fn, reads=(), scalar_reads=(), writes=(), dma=False):
        o = _Op(len(self.ops), eng, fn, dma)
        for b in reads:
            for w in b.writers:
                o.deps.setdefault(w, False)
        for b in scalar_reads:
            for w in b.writers:
                o.deps[w] = True
        for b in writes:
            for w in b.writers:
                o.deps.setdefault(w, False)
            for r in b.readers:
                o.deps.setdefault(r, False)
        o.deps.pop(o.idx, None)
        self.ops.append(o)
        self.eng_ops[eng].append(o)
        if dma:
            self.recent_dma[eng].append(o.idx)
        for b in reads:
            if not b.const:
                b.readers.append(o.idx)
        for b in scalar_reads:
            if not b.const:
                b.readers.append(o.idx)
        for b in writes:
            b.readers = []
            b.writers.append(o.idx)
        return o.idx

    def dma(self, queue, out_ap, in_ap, reads=(), writes=(), transpose=False, tag=""):
        if transpose:
            fn = lambda e, o=out_ap, i=in_ap: e.dma_start_transpose(o, i)
        else:
            fn = lambda e, o=out_ap, i=in_ap: e.dma_start(out=o, in_=i)
        i = self.op(queue, fn, reads=reads, writes=writes, dma=True)
        self.ops[i].tag = tag
        try:
            self.ops[i].small = out_ap.nbytes() <= 32 * 1024
        except Exception:
            self.ops[i].small = True
        return i

    def barrier(self):
        lasts = [self.eng_ops[e][-1].idx for e in ENGINES if self.eng_ops[e]]
        for e in ENGINES:
            lasts.extend(self.recent_dma[e])
        lasts = sorted(set(lasts))
        for e in ENGINES:
            o = _Op(len(self.ops), e, None, False)
            for dd in lasts:
                o.deps[dd] = False
            o.deps.pop(o.idx, None)
            self.ops.append(o)
            self.eng_ops[e].append(o)

    def finalize_and_emit(self, stack):
        nc = self.nc
        for o in self.ops:
            if o.dma:
                o.needs_inc = True
            for dd in o.deps:
                self.ops[dd].needs_inc = True
        exec_sems = {e: [stack.enter_context(nc.semaphore(f"x_{e}_{i}"))
                         for i in range(self.N_EXEC_SEMS)] for e in ENGINES}
        dma_sems = {q: [stack.enter_context(nc.semaphore(f"d_{q}_{i}"))
                        for i in range(self.N_DMA_SEMS)] for q in DMA_QUEUES}
        for e in ENGINES:
            nexec = 0
            ndma = 0
            for o in self.eng_ops[e]:
                if not o.needs_inc:
                    continue
                if o.dma:
                    pool = dma_sems[e]
                    o.sem = pool[ndma % len(pool)]
                    o.val = 16 * (ndma // len(pool) + 1)
                    ndma += 1
                else:
                    pool = exec_sems[e]
                    o.sem = pool[nexec % len(pool)]
                    o.val = nexec // len(pool) + 1
                    nexec += 1
        waited = {e: {} for e in ENGINES}
        MAX_DMA_INFLIGHT = 20

        def run_engine(eng_obj, ename):
            w = waited[ename]
            my_dmas = [o for o in self.eng_ops[ename] if o.dma]
            dma_idx = {o.idx: i for i, o in enumerate(my_dmas)}
            for o in self.eng_ops[ename]:
                tgt = {}
                for dd, force in o.deps.items():
                    p = self.ops[dd]
                    if p.sem is None:
                        continue
                    k = p.sem.name
                    if tgt.get(k, (None, -1))[1] < p.val:
                        tgt[k] = (p.sem, p.val)
                if o.dma:
                    i = dma_idx[o.idx]
                    gate = MAX_DMA_INFLIGHT
                    if i >= gate:
                        p = my_dmas[i - gate]
                        k = p.sem.name
                        if tgt.get(k, (None, -1))[1] < p.val:
                            tgt[k] = (p.sem, p.val)
                for k in sorted(tgt):
                    sem, val = tgt[k]
                    if w.get(k, 0) < val:
                        eng_obj.wait_ge(sem, val)
                        w[k] = val
                inst = o.fn(eng_obj) if o.fn is not None else eng_obj.nop()
                if o.needs_inc:
                    inst.then_inc(o.sem, 16 if o.dma else 1)

        with nc.Block() as block:
            block.tensor(lambda t: run_engine(t, "tensor"))
            block.vector(lambda v: run_engine(v, "vector"))
            block.scalar(lambda s: run_engine(s, "scalar"))
            block.gpsimd(lambda g: run_engine(g, "gpsimd"))
            block.sync(lambda y: run_engine(y, "sync"))


class Arena:
    def __init__(self, nc, stack, cols=96 * 1024):
        self.t = stack.enter_context(nc.sbuf_tensor("arena", [128, cols], BF16))
        self.cols = cols
        self.off = 0
        self.marks = []
        self.peak = 0

    def bf16(self, cols, parts=128):
        alloc = (cols + 15) & ~15
        assert self.off + alloc <= self.cols, f"arena overflow {self.off}+{alloc}"
        ap = self.t[0:parts, self.off:self.off + cols]
        self.off += alloc
        self.peak = max(self.peak, self.off)
        return ap

    def f32(self, cols, parts=128):
        return self.bf16(2 * cols, parts).bitcast(F32)

    def mark(self):
        self.marks.append(self.off)

    def release(self):
        self.off = self.marks.pop()


def _bank(ctx, i, n=1):
    return ctx["psum"][:, 512 * i:512 * (i + n)]


def emit_ln_cols(ctx, cols, sl, cb):
    P = ctx["P"]
    s_c = cols[:, 5 * sl + 0:5 * sl + 1]
    ss_c = cols[:, 5 * sl + 1:5 * sl + 2]
    t1_c = cols[:, 5 * sl + 2:5 * sl + 3]
    rstd_c = cols[:, 5 * sl + 3:5 * sl + 4]
    nmr_c = cols[:, 5 * sl + 4:5 * sl + 5]
    P.op("scalar", lambda s, i_=s_c, o=t1_c:
         s.activation(out=o, in_=i_, func=AF.Square, scale=float(D) ** -0.5),
         reads=[cb], writes=[cb])
    P.op("vector", lambda v, i_=ss_c, o=t1_c, s1=t1_c:
         v.tensor_scalar(out=o, in0=i_, scalar1=s1, scalar2=1.0 / D,
                         op0=ALU.subtract, op1=ALU.mult),
         reads=[cb], scalar_reads=[cb], writes=[cb])
    P.op("scalar", lambda s, i_=t1_c, o=rstd_c, e=ctx["epscol"]:
         s.activation(out=o, in_=i_, func=AF.Ln, bias=e),
         reads=[cb], scalar_reads=[ctx["epscolb"]], writes=[cb])
    P.op("scalar", lambda s, i_=rstd_c, o=rstd_c:
         s.activation(out=o, in_=i_, func=AF.Exp, scale=-0.5),
         reads=[cb], writes=[cb])
    P.op("vector", lambda v, i_=s_c, o=nmr_c, s1=rstd_c:
         v.tensor_scalar(out=o, in0=i_, scalar1=s1, scalar2=-1.0 / D,
                         op0=ALU.mult, op1=ALU.mult),
         reads=[cb], scalar_reads=[cb], writes=[cb])
    return rstd_c, nmr_c


def emit_ln_tm(ctx, x_src_dram, xn_dram):
    P, A = ctx["P"], ctx["arena"]
    A.mark()
    xt = [A.f32(D) for _ in range(2)]
    xb = [Buf(f"lnx{i}") for i in range(2)]
    xn = [A.bf16(D) for _ in range(2)]
    xnb = [Buf(f"lnxn{i}") for i in range(2)]
    dump = A.bf16(D)
    dumpb = Buf("lndump")
    cols = A.f32(16)
    colb = [Buf(f"lncol{i}") for i in range(2)]
    for i in range(NT):
        sl = i % 2
        x_ap, xbuf = xt[sl], xb[sl]
        xn_ap, xnbuf = xn[sl], xnb[sl]
        cb = colb[sl]
        P.dma("sync", x_ap, x_src_dram[128 * i:128 * (i + 1), :], writes=[xbuf])
        s_c = cols[:, 5 * sl + 0:5 * sl + 1]
        ss_c = cols[:, 5 * sl + 1:5 * sl + 2]
        P.op("vector", lambda v, x=x_ap, o=s_c: v.reduce_sum(out=o, in_=x, axis=AX.X),
             reads=[xbuf], writes=[cb])
        P.op("scalar", lambda s, x=x_ap, o=dump, a=ss_c:
             s.activation(out=o, in_=x, func=AF.Square, accum_out=a),
             reads=[xbuf], writes=[dumpb, cb])
        rstd_c, nmr_c = emit_ln_cols(ctx, cols, sl, cb)
        P.op("vector", lambda v, x=x_ap, o=xn_ap, s1=rstd_c, s2=nmr_c:
             v.tensor_scalar(out=o, in0=x, scalar1=s1, scalar2=s2,
                             op0=ALU.mult, op1=ALU.add),
             reads=[xbuf], scalar_reads=[cb], writes=[xnbuf])
        P.dma("sync", xn_dram[128 * i:128 * (i + 1), :], xn_ap, reads=[xnbuf])
    A.release()


def load_weight_fm(ctx, w_dram, n_din, dout_cols):
    P, A = ctx["P"], ctx["arena"]
    wb = Buf("wfm")
    chunks = []
    for c in range(n_din):
        ap = A.bf16(dout_cols)
        chunks.append(ap)
        P.dma("gpsimd", ap, w_dram[128 * c:128 * (c + 1), :], writes=[wb])
    return chunks, wb


def load_xn_fm(ctx, xn_dram, n_chunks=ND, reads=()):
    P, A = ctx["P"], ctx["arena"]
    xb = Buf("xnfm")
    chunks = []
    for c in range(n_chunks):
        ap = A.bf16(T)
        chunks.append(ap)
        for sl in range(T // 512):
            P.dma("sync", ap[:, 512 * sl:512 * (sl + 1)],
                  xn_dram[512 * sl:512 * (sl + 1), 128 * c:128 * (c + 1)],
                  reads=reads, writes=[xb], transpose=True)
    return chunks, xb


def emit_maxsq_blk(ctx, src_ap, srcbuf, sq, sqb, maxsq, col, init):
    P = ctx["P"]
    pbufs = ctx["psum_bufs"]
    nh = maxsq["nh"]
    P.op("scalar", lambda s, i_=src_ap, o=sq:
         s.activation(out=o, in_=i_, func=AF.Square),
         reads=[srcbuf], writes=[sqb])
    P.op("tensor", lambda t, o=ctx["psum"][0:nh, 512 * 7:512 * 7 + 512], hs=maxsq["hs"]:
         t.matmul(o, hs, sq, start=True, stop=True),
         reads=[sqb, maxsq["hsb"]], writes=[pbufs[7]])
    if init:
        P.op("vector", lambda v, o=maxsq["ap"][0:nh, col:col + 1], nh=nh:
             v.reduce_max(out=o, in_=ctx["psum"][0:nh, 512 * 7:512 * 7 + 512], axis=AX.X),
             reads=[pbufs[7]], writes=[maxsq["buf"]])
    else:
        P.op("vector", lambda v, o=maxsq["tmpcol"], nh=nh:
             v.reduce_max(out=o, in_=ctx["psum"][0:nh, 512 * 7:512 * 7 + 512], axis=AX.X),
             reads=[pbufs[7]], writes=[maxsq["tmpbuf"]])
        P.op("vector", lambda v, o=maxsq["ap"][0:nh, col:col + 1], b=maxsq["tmpcol"], nh=nh:
             v.tensor_tensor(out=o, in0=o, in1=b[0:nh, :], op=ALU.max),
             reads=[maxsq["buf"], maxsq["tmpbuf"]], writes=[maxsq["buf"]])


def emit_proj_formB(ctx, w_chunks, wbuf, xn_chunks, xnbuf, out_dram, out_dramb,
                    n_din, n_dout, bias_col=None, biasbuf=None, maxsq=None):
    P, A = ctx["P"], ctx["arena"]
    A.mark()
    stage = A.bf16(T)
    stageb = Buf("pjstage")
    if maxsq is not None:
        sq = A.bf16(512)
        sqb = Buf("pjsq")
    pbufs = ctx["psum_bufs"]
    for j in range(n_dout):
        for blk in range(NB):
            pb = blk % 4

            def mmfn(t, j=j, blk=blk, pb=pb):
                last = None
                for c in range(n_din):
                    last = t.matmul(_bank(ctx, pb),
                                    w_chunks[c][:, 128 * j:128 * (j + 1)],
                                    xn_chunks[c][:, 512 * blk:512 * (blk + 1)],
                                    start=(c == 0), stop=(c == n_din - 1))
                return last
            P.op("tensor", mmfn, reads=[wbuf, xnbuf], writes=[pbufs[pb]])
            if bias_col is not None:
                P.op("scalar", lambda s, pb=pb, o=stage[:, 512 * blk:512 * (blk + 1)],
                     b=bias_col[:, j:j + 1]:
                     s.activation(out=o, in_=_bank(ctx, pb), func=AF.Identity, bias=b),
                     reads=[pbufs[pb]], scalar_reads=[biasbuf], writes=[stageb])
            else:
                P.op("scalar", lambda s, pb=pb, o=stage[:, 512 * blk:512 * (blk + 1)]:
                     s.activation(out=o, in_=_bank(ctx, pb), func=AF.Copy),
                     reads=[pbufs[pb]], writes=[stageb])
        if maxsq is not None:
            for blk in range(NB):
                emit_maxsq_blk(ctx, stage[:, 512 * blk:512 * (blk + 1)], stageb,
                               sq, sqb, maxsq, j, blk == 0)
        P.dma("sync", out_dram[128 * j:128 * (j + 1), :], stage, reads=[stageb],
              writes=[out_dramb])
    A.release()


def emit_rope_phase(ctx, qk_dram, qkbuf, maxsq, iw):
    P, A = ctx["P"], ctx["arena"]
    A.mark()
    stage = A.bf16(T)
    stageb = Buf("rpstage")
    swap = A.bf16(T)
    swapb = Buf("rpswap")
    sq = A.bf16(512)
    sqb = Buf("rpsq")
    ropecc = A.bf16(T)
    ropess = A.bf16(T)
    ropeb = Buf("rope")
    P.dma("gpsimd", ropecc, iw["c_ropecc"], writes=[ropeb])
    P.dma("gpsimd", ropess, iw["c_ropess"], writes=[ropeb])
    ropeb.const = True
    ctx["ropecc"], ctx["ropeccb"] = ropecc, ropeb
    ctx["ropess"], ctx["ropessb"] = ropess, ropeb
    for c in range(ND):
        P.dma("sync", stage, qk_dram[128 * c:128 * (c + 1), :],
              reads=[qkbuf], writes=[stageb])
        for i in range(4):
            src = (1, 0, 3, 2)[i]
            P.dma("gpsimd", swap[32 * i:32 * (i + 1), :],
                  qk_dram[128 * c + 32 * src:128 * c + 32 * (src + 1), :],
                  reads=[qkbuf], writes=[swapb])
        P.op("vector", lambda v, s_=stage, cc=ctx["ropecc"]:
             v.tensor_mul(out=s_, in0=s_, in1=cc),
             reads=[stageb, ctx["ropeccb"]], writes=[stageb])
        P.op("vector", lambda v, s_=swap, ss=ctx["ropess"]:
             v.tensor_mul(out=s_, in0=s_, in1=ss),
             reads=[swapb, ctx["ropessb"]], writes=[swapb])
        P.op("vector", lambda v, a=stage, b=swap:
             v.tensor_add(out=a, in0=a, in1=b),
             reads=[stageb, swapb], writes=[stageb])
        for blk in range(NB):
            emit_maxsq_blk(ctx, stage[:, 512 * blk:512 * (blk + 1)], stageb,
                           sq, sqb, maxsq, c, blk == 0)
        P.dma("sync", qk_dram[128 * c:128 * (c + 1), :], stage,
              reads=[stageb], writes=[qkbuf])
    A.release()


def emit_proj_formA(ctx, xn_chunks, xnbuf, w_chunks, wbuf, n_din, dout,
                    bias_row=None, biasrowb=None, consume=None, group=4,
                    extra_mm_reads=()):
    P = ctx["P"]
    pbufs = ctx["psum_bufs"]
    nbk = (dout + 511) // 512
    per = group
    assert per * nbk <= 8
    for tg in range(0, NT, per):
        tts = list(range(tg, min(tg + per, NT)))
        for ti, tt in enumerate(tts):
            b0 = ti * nbk

            def mmfn(t, tt=tt, b0=b0):
                last = None
                for c in range(n_din):
                    for k in range(nbk):
                        w = w_chunks[c][:, 512 * k:min(512 * (k + 1), dout)]
                        last = t.matmul(
                            ctx["psum"][:, 512 * (b0 + k):512 * (b0 + k) + w.shape[1]],
                            xn_chunks[c][:, 128 * tt:128 * (tt + 1)],
                            w, start=(c == 0),
                            stop=(c == n_din - 1 and bias_row is None))
                if bias_row is not None:
                    for k in range(nbk):
                        br = bias_row[:, 512 * k:min(512 * (k + 1), dout)]
                        last = t.matmul(
                            ctx["psum"][:, 512 * (b0 + k):512 * (b0 + k) + br.shape[1]],
                            ctx["ones_row"], br, start=False, stop=True)
                return last
            reads = [xnbuf, wbuf] + list(extra_mm_reads)
            if bias_row is not None:
                reads += [biasrowb, ctx["ones_rowb"]]
            P.op("tensor", mmfn, reads=reads,
                 writes=[pbufs[b0 + k] for k in range(nbk)])
        for ti, tt in enumerate(tts):
            b0 = ti * nbk
            consume(tt, ctx["psum"][:, 512 * b0:512 * b0 + dout],
                    [pbufs[b0 + k] for k in range(nbk)])


def emit_inline_ln_scratch(ctx):
    A = ctx["arena"]
    st = {}
    st["xn"] = [A.bf16(D) for _ in range(2)]
    st["xnb"] = [Buf(f"iln{i}") for i in range(2)]
    st["cols"] = A.f32(16)
    st["colb"] = [Buf(f"ilc{i}") for i in range(2)]
    st["dump"] = A.bf16(D)
    st["dumpb"] = Buf("ildump")
    return st


def emit_inline_ln(ctx, st, tt, x1_ap, x1buf):
    P = ctx["P"]
    sl = tt % 2
    cb = st["colb"][sl]
    cols = st["cols"]
    s_c = cols[:, 5 * sl + 0:5 * sl + 1]
    ss_c = cols[:, 5 * sl + 1:5 * sl + 2]
    P.op("vector", lambda v, o=s_c, x=x1_ap: v.reduce_sum(out=o, in_=x, axis=AX.X),
         reads=[x1buf], writes=[cb])
    P.op("scalar", lambda s, o=st["dump"], a=ss_c, x=x1_ap:
         s.activation(out=o, in_=x, func=AF.Square, accum_out=a),
         reads=[x1buf], writes=[st["dumpb"], cb])
    rstd_c, nmr_c = emit_ln_cols(ctx, cols, sl, cb)
    P.op("vector", lambda v, o=st["xn"][sl], x=x1_ap, s1=rstd_c, s2=nmr_c:
         v.tensor_scalar(out=o, in0=x, scalar1=s1, scalar2=s2,
                         op0=ALU.mult, op1=ALU.add),
         reads=[x1buf], scalar_reads=[cb], writes=[st["xnb"][sl]])
    P.dma("sync", ctx["dram"]["xn"][128 * tt:128 * (tt + 1), :], st["xn"][sl],
          reads=[st["xnb"][sl]])


def emit_outer_ln_residual_consume(ctx, x_src_dram, x_dst_dram, next_ln=False):
    P, A = ctx["P"], ctx["arena"]
    xt = [A.f32(D) for _ in range(2)]
    xb = [Buf(f"orx{i}") for i in range(2)]
    x1 = [A.f32(D) for _ in range(2)]
    x1b = [Buf(f"orx1{i}") for i in range(2)]
    dump = A.bf16(D)
    dumpb = Buf("ordump")
    cols = A.f32(16)
    colb = [Buf(f"orc{i}") for i in range(2)]
    ilst = emit_inline_ln_scratch(ctx) if next_ln else None

    def consume(tt, psum_ap, bankbufs):
        sl = tt % 2
        cb = colb[sl]
        x_ap, xbuf = xt[sl], xb[sl]
        x1_ap, x1buf = x1[sl], x1b[sl]
        P.dma("sync", x_ap, x_src_dram[128 * tt:128 * (tt + 1), :], writes=[xbuf])
        s_c = cols[:, 5 * sl + 0:5 * sl + 1]
        ss_c = cols[:, 5 * sl + 1:5 * sl + 2]
        P.op("vector", lambda v, o=s_c, p=psum_ap: v.reduce_sum(out=o, in_=p, axis=AX.X),
             reads=bankbufs, writes=[cb])
        P.op("scalar", lambda s, o=dump, a=ss_c, p=psum_ap:
             s.activation(out=o, in_=p, func=AF.Square, accum_out=a),
             reads=bankbufs, writes=[dumpb, cb])
        rstd_c, nmr_c = emit_ln_cols(ctx, cols, sl, cb)
        P.op("vector", lambda v, x=x_ap, s1=nmr_c:
             v.tensor_scalar_add(out=x, in0=x, scalar1=s1),
             reads=[xbuf], scalar_reads=[cb], writes=[xbuf])
        P.op("vector", lambda v, o=x1_ap, s1=rstd_c, x=x_ap, p=psum_ap:
             v.scalar_tensor_tensor(out=o, in0=p, scalar=s1, in1=x,
                                    op0=ALU.mult, op1=ALU.add),
             reads=list(bankbufs) + [xbuf], scalar_reads=[cb], writes=[x1buf])
        P.dma("sync", x_dst_dram[128 * tt:128 * (tt + 1), :], x1_ap, reads=[x1buf])
        if next_ln:
            emit_inline_ln(ctx, ilst, tt, x1_ap, x1buf)

    return consume


def emit_attention_core(ctx, H, HD, scale, causal, q_dram, qdramb, k_dram, kdramb,
                        v_tm, v_tmb, negB_all, negBhb, o_dram, odramb,
                        den_dram, dendramb, negb_col):
    P, A = ctx["P"], ctx["arena"]
    VW = HD + 1
    A.mark()
    qp = [A.bf16(T, parts=HD) for _ in range(2)]
    kp = [A.bf16(T, parts=HD) for _ in range(2)]
    qpb = [Buf(f"aqp{i}") for i in range(2)]
    kpb = [Buf(f"akp{i}") for i in range(2)]
    e_sb = [A.bf16(1024) for _ in range(3)]
    e_bf = [Buf(f"ae{i}") for i in range(3)]
    o_stage = A.bf16(T)
    o_stageb = Buf("aostage")
    den = A.f32(512, parts=1)
    denb = Buf("aden")
    pbufs = ctx["psum_bufs"]
    hpc = 128 // HD

    for h in range(H):
        sl = h % 2
        negB = negb_col(h)
        P.dma("sync", qp[sl], q_dram[HD * h:HD * (h + 1), :],
              reads=[qdramb], writes=[qpb[sl]])
        P.dma("sync", kp[sl], k_dram[HD * h:HD * (h + 1), :],
              reads=[kdramb], writes=[kpb[sl]])
        for pi in range(4):
            qbs = (2 * pi, 2 * pi + 1)
            units = []
            for qb in qbs:
                ktm = min(NT, 4 * (qb + 1)) if causal else NT
                for kt in range(0, ktm, 2):
                    units.append((qb, kt, kt + 2 >= ktm))
            N = len(units)

            def emit_pv(n):
                qb, kt, last = units[n]
                ps = n % 3
                es = n % 3
                ob = qb - qbs[0]
                first = (kt == 0)
                for j in range(2):
                    vsl = v_tm[:, (kt + j) * H * VW + VW * h:(kt + j) * H * VW + VW * (h + 1)]
                    P.op("tensor", lambda t, vs=vsl, es=es, j=j, ob=ob,
                         st=(first and j == 0), sp=(last and j == 1):
                         t.matmul(ctx["psum"][0:VW, 512 * ob:512 * ob + 512],
                                  vs, e_sb[es][:, 512 * j:512 * (j + 1)],
                                  start=st, stop=sp),
                         reads=[e_bf[es], v_tmb], writes=[pbufs[ob]])
                if last:
                    P.op("scalar", lambda s, ob=ob, o=den:
                         s.activation(out=o, in_=ctx["psum"][HD:HD + 1, 512 * ob:512 * ob + 512],
                                      func=AF.Copy),
                         reads=[pbufs[ob]], writes=[denb])
                    P.dma("sync", den_dram[h:h + 1, 512 * qb:512 * (qb + 1)], den,
                          reads=[denb], writes=[dendramb], tag=f"denst_{h}_{qb}")
                    P.op("vector", lambda v, ob=ob, qb=qb,
                         o=o_stage[HD * (h % hpc):HD * (h % hpc + 1), 512 * qb:512 * (qb + 1)]:
                         v.tensor_copy(out=o, in_=ctx["psum"][0:HD, 512 * ob:512 * ob + 512]),
                         reads=[pbufs[ob]], writes=[o_stageb])

            for n in range(N):
                qb, kt, last = units[n]
                ps = n % 3
                def smm(t, kt=kt, qb=qb, ps=ps, sl=sl):
                    i1 = t.matmul(_bank(ctx, 2 + 2 * ps),
                                  kp[sl][:, 128 * kt:128 * (kt + 1)],
                                  qp[sl][:, 512 * qb:512 * (qb + 1)],
                                  start=True, stop=True)
                    i2 = t.matmul(_bank(ctx, 3 + 2 * ps),
                                  kp[sl][:, 128 * (kt + 1):128 * (kt + 2)],
                                  qp[sl][:, 512 * qb:512 * (qb + 1)],
                                  start=True, stop=True)
                    return i2
                P.op("tensor", smm, reads=[kpb[sl], qpb[sl]],
                     writes=[pbufs[2 + 2 * ps], pbufs[3 + 2 * ps]])
                es = n % 3
                P.op("scalar", lambda s, ps=ps, es=es, nb_=negB:
                     s.activation(out=e_sb[es], in_=_bank(ctx, 2 + 2 * ps, 2),
                                  func=AF.Exp, scale=scale, bias=nb_),
                     reads=[pbufs[2 + 2 * ps], pbufs[3 + 2 * ps]],
                     scalar_reads=[negBhb], writes=[e_bf[es]])
                if causal:
                    for j in range(2):
                        mi = (kt + j) - 4 * qb
                        if 0 <= mi <= 3:
                            P.op("vector", lambda v, es=es, j=j, m=ctx["masks"][mi]:
                                 v.tensor_mul(out=e_sb[es][:, 512 * j:512 * (j + 1)],
                                              in0=e_sb[es][:, 512 * j:512 * (j + 1)], in1=m),
                                 reads=[e_bf[es], ctx["masksb"]], writes=[e_bf[es]])
                if n >= 2:
                    emit_pv(n - 2)
            for n in (N - 2, N - 1):
                if n >= 0:
                    emit_pv(n)
        if h % hpc == hpc - 1:
            P.dma("sync", o_dram[128 * (h // hpc):128 * (h // hpc) + 128, :], o_stage,
                  reads=[o_stageb], writes=[odramb])
    A.release()


def emit_bulk_normalize(ctx, H, HD, o_chunks, obuf, den_dram, dendramb, rall_dram):
    P, A = ctx["P"], ctx["arena"]
    A.mark()
    rallb = Buf("ralld")
    den_all = A.f32(T, parts=H)
    den_allb = Buf("denall")
    den_tmp = A.f32(T, parts=H)
    den_tmpb = Buf("dentmp")
    P.dma("sync", den_all, den_dram, reads=[dendramb], writes=[den_allb], tag="denload")
    if DBG2:
        P.dma("sync", ctx["dram"]["dsnap"], den_all, reads=[den_allb])
    P.op("scalar", lambda s: s.activation(out=den_tmp, in_=den_all, func=AF.Ln),
         reads=[den_allb], writes=[den_tmpb])
    P.op("scalar", lambda s: s.activation(out=den_all, in_=den_tmp, func=AF.Exp,
                                          scale=-1.0),
         reads=[den_tmpb], writes=[den_allb])
    P.dma("sync", rall_dram[0:H, :], den_all, reads=[den_allb], writes=[rallb])
    P.barrier()
    A.release()
    A.mark()
    rbf = A.f32(T)
    rbfb = Buf("rbalf")
    rb = A.bf16(T)
    rbb = Buf("rball")
    hpc = 128 // HD
    for h in range(H):
        c, j = h // hpc, h % hpc
        P.dma("sync", rbf[HD * j:HD * (j + 1), :],
              bass.AP(rall_dram.tensor, rall_dram.offset + T * h, [[0, HD], [1, T]]),
              reads=[rallb], writes=[rbfb])
        P.op("vector", lambda v, j=j, o=rb[HD * j:HD * (j + 1), :],
             i_=rbf[HD * j:HD * (j + 1), :]:
             v.tensor_copy(out=o, in_=i_),
             reads=[rbfb], writes=[rbb])
        P.op("vector", lambda v, c=c, j=j,
             o=o_chunks[c][HD * j:HD * (j + 1), :], r=rb[HD * j:HD * (j + 1), :]:
             v.tensor_tensor(out=o, in0=o, in1=r, op=ALU.mult),
             reads=[obuf, rbb], writes=[obuf])
    P.barrier()
    A.release()


def emit_block_A(ctx, x_src_dram, x_dst_dram, iw, next_ln=False):
    P, A, nc = ctx["P"], ctx["arena"], ctx["nc"]
    H, HD = 16, 64
    VW = HD + 1
    d = ctx["dram"]
    pbufs = ctx["psum_bufs"]
    ctx["odramb"] = Buf("odramA")
    dendramb = Buf("dendramA")

    emit_ln_tm(ctx, x_src_dram, d["xn"])
    P.barrier()

    A.mark()
    v_tm = A.bf16(NT * H * VW)
    v_tmb = Buf("v_tm")
    mall = A.f32(2 * ND, parts=2)
    msq = mall[0:2, 0:ND]
    msk = mall[0:2, ND:2 * ND]
    mtmp = A.f32(2, parts=2)
    msqB = {"ap": msq, "buf": Buf("msq"), "hs": ctx["headsum2"], "hsb": ctx["headsum2b"],
            "nh": 2, "tmpcol": mtmp[:, 0:1], "tmpbuf": Buf("mt1"), "first": True}
    mskB = {"ap": msk, "buf": Buf("msk"), "hs": ctx["headsum2"], "hsb": ctx["headsum2b"],
            "nh": 2, "tmpcol": mtmp[:, 1:2], "tmpbuf": Buf("mt2"), "first": True}
    ones_view = bass.AP(v_tm.tensor, v_tm.offset + HD,
                        [[v_tm.ap[0][0], 128], [VW, NT * H], [1, 1]])
    P.op("vector", lambda v, o=ones_view: v.memset(o, 1.0), writes=[v_tmb])

    A.mark()
    xn_chunks, xnbuf = load_xn_fm(ctx, d["xn"])
    A.mark()
    wq, wqb = load_weight_fm(ctx, iw["A_wq"], ND, D)
    bq_col = A.f32(ND)
    bqb = Buf("bq")
    P.dma("sync", bq_col, iw["A_bq_col"], writes=[bqb])
    qdramb = Buf("qdram")
    emit_proj_formB(ctx, wq, wqb, xn_chunks, xnbuf, d["q"], qdramb, ND, ND,
                    bias_col=bq_col, biasbuf=bqb)
    A.release()
    P.barrier()
    A.mark()
    wk, wkb = load_weight_fm(ctx, iw["A_wk"], ND, D)
    bk_col = A.f32(ND)
    bkb = Buf("bk")
    P.dma("sync", bk_col, iw["A_bk_col"], writes=[bkb])
    kdramb = Buf("kdram")
    emit_proj_formB(ctx, wk, wkb, xn_chunks, xnbuf, d["k"], kdramb, ND, ND,
                    bias_col=bk_col, biasbuf=bkb)
    A.release()
    P.barrier()
    A.mark()
    wv, wvb = load_weight_fm(ctx, iw["A_wv"], ND, D)
    bv_row = A.bf16(D, parts=1)
    bvb = Buf("bv")
    P.dma("gpsimd", bv_row, iw["A_bv"].unsqueeze(0), writes=[bvb])

    def v_consume(tt, psum_ap, bankbufs):
        out = bass.AP(v_tm.tensor, v_tm.offset + tt * H * VW,
                      [[v_tm.ap[0][0], 128], [VW, H], [1, HD]])
        P.op("scalar", lambda s, o=out, i_=psum_ap.rearrange("p (h e) -> p h e", h=H):
             s.activation(out=o, in_=i_, func=AF.Copy),
             reads=bankbufs, writes=[v_tmb])

    emit_proj_formA(ctx, xn_chunks, xnbuf, wv, wvb, ND, D,
                    bias_row=bv_row, biasrowb=bvb, consume=v_consume, group=4)
    A.release()
    A.release()
    P.barrier()

    if DBG:
        P.dma("sync", d["q0"], d["q"], reads=[qdramb])
    emit_rope_phase(ctx, d["q"], qdramb, msqB, iw)
    emit_rope_phase(ctx, d["k"], kdramb, mskB, iw)
    P.barrier()

    A.mark()
    sqh = A.f32(16)
    sqhb = Buf("sqh")
    negm = A.f32(ND, parts=2)
    negmb = Buf("negm")
    negB_all = A.f32(2 * ND)
    negBhb = Buf("negBall")
    P.op("scalar", lambda s: s.activation(out=sqh[0:2, :], in_=mall[0:2, 0:16], func=AF.Sqrt),
         reads=[msqB["buf"], mskB["buf"]], writes=[sqhb])
    P.op("vector", lambda v: v.tensor_tensor(out=negm[0:2, :], in0=sqh[0:2, 0:ND],
                                             in1=sqh[0:2, ND:2 * ND], op=ALU.mult),
         reads=[sqhb], writes=[negmb])
    P.op("vector", lambda v: v.tensor_scalar_mul(out=negm[0:2, :], in0=negm[0:2, :],
                                                 scalar1=-0.125),
         reads=[negmb], writes=[negmb])
    P.dma("sync", d["negd"], negm[0:2, :], reads=[negmb], writes=[ctx["negdb"]])
    P.dma("sync", negB_all,
          bass.AP(d["negd"].tensor, d["negd"].offset, [[0, 128], [1, 2 * ND]]),
          reads=[ctx["negdb"]], writes=[negBhb])
    emit_attention_core(ctx, H, HD, 0.125, True, d["q"], qdramb, d["k"], kdramb,
                        v_tm, v_tmb, negB_all, negBhb, d["o"], ctx["odramb"],
                        d["denh"], dendramb,
                        lambda h: negB_all[:, ND * (h % 2) + h // 2:ND * (h % 2) + h // 2 + 1])
    A.release()
    P.barrier()

    A.mark()
    o_chunks = []
    obuf = Buf("ofm")
    for c in range(ND):
        ap = A.bf16(T)
        o_chunks.append(ap)
        P.dma("sync", ap, d["o"][128 * c:128 * (c + 1), :],
              reads=[ctx["odramb"]], writes=[obuf])
    emit_bulk_normalize(ctx, H, HD, o_chunks, obuf, d["denh"], dendramb, d["ralla"])
    if DBG2:
        for c in range(ND):
            P.dma("sync", d["q"][128 * c:128 * (c + 1), :], o_chunks[c], reads=[obuf])
    wo, wob = load_weight_fm(ctx, iw["A_wo"], ND, D)
    bo_row = A.bf16(D, parts=1)
    bob = Buf("bo")
    P.dma("gpsimd", bo_row, iw["A_bo"].unsqueeze(0), writes=[bob])
    consume = emit_outer_ln_residual_consume(ctx, x_src_dram, x_dst_dram,
                                             next_ln=next_ln)
    emit_proj_formA(ctx, o_chunks, obuf, wo, wob, ND, D,
                    bias_row=bo_row, biasrowb=bob, consume=consume, group=4)
    A.release()
    A.release()
    P.barrier()


def emit_block_F(ctx, x_src_dram, x_dst_dram, iw, skip_ln=False, next_ln=False):
    P, A, nc = ctx["P"], ctx["arena"], ctx["nc"]
    H, HD = 16, 32
    VW = HD + 1
    SCL = 1.0 / math.sqrt(HD)
    d = ctx["dram"]
    pbufs = ctx["psum_bufs"]
    aodramb = Buf("aodramF")
    dendramb = Buf("dendramF")
    qdramb = Buf("qdramF")
    kdramb = Buf("kdramF")
    mladramb = Buf("mladramF")
    moedramb = Buf("moedramF")

    if not skip_ln:
        emit_ln_tm(ctx, x_src_dram, d["xn"])
        P.barrier()

    A.mark()
    mall = A.f32(8, parts=4)
    msq = mall[0:4, 0:4]
    msk = mall[0:4, 4:8]
    mtmp = A.f32(2, parts=4)
    msqB = {"ap": msq, "buf": Buf("msqF"), "hs": ctx["headsum4"], "hsb": ctx["headsum4b"],
            "nh": 4, "tmpcol": mtmp[:, 0:1], "tmpbuf": Buf("mt1F")}
    mskB = {"ap": msk, "buf": Buf("mskF"), "hs": ctx["headsum4"], "hsb": ctx["headsum4b"],
            "nh": 4, "tmpcol": mtmp[:, 1:2], "tmpbuf": Buf("mt2F")}
    cT_sb = A.bf16(T, parts=8)
    cTb = Buf("cT")
    A.mark()
    v_tm = A.bf16(NT * H * VW)
    v_tmb = Buf("v_tmF")
    ones_view = bass.AP(v_tm.tensor, v_tm.offset + HD,
                        [[v_tm.ap[0][0], 128], [VW, NT * H], [1, 1]])
    P.op("vector", lambda v, o=ones_view: v.memset(o, 1.0), writes=[v_tmb])

    A.mark()
    xn_chunks, xnbuf = load_xn_fm(ctx, d["xn"])
    A.mark()
    wq, wqb = load_weight_fm(ctx, iw["F_qw"], ND, 512)
    bq_col = A.f32(4)
    bqb = Buf("bqF")
    P.dma("sync", bq_col, iw["F_qb_col"], writes=[bqb])
    emit_proj_formB(ctx, wq, wqb, xn_chunks, xnbuf, d["q"][0:512, :], qdramb, ND, 4,
                    bias_col=bq_col, biasbuf=bqb, maxsq=msqB)
    A.release()
    P.barrier()
    A.mark()
    wk, wkb = load_weight_fm(ctx, iw["F_kw"], ND, 512)
    bk_col = A.f32(4)
    bkb = Buf("bkF")
    P.dma("sync", bk_col, iw["F_kb_col"], writes=[bkb])
    emit_proj_formB(ctx, wk, wkb, xn_chunks, xnbuf, d["k"][0:512, :], kdramb, ND, 4,
                    bias_col=bk_col, biasbuf=bkb, maxsq=mskB)
    A.release()
    P.barrier()
    A.mark()
    wv, wvb = load_weight_fm(ctx, iw["F_vw"], ND, 512)
    bv_row = A.bf16(512, parts=1)
    bvb = Buf("bvF")
    P.dma("gpsimd", bv_row, iw["F_vb"].unsqueeze(0), writes=[bvb])

    def v_consume(tt, psum_ap, bankbufs):
        out = bass.AP(v_tm.tensor, v_tm.offset + tt * H * VW,
                      [[v_tm.ap[0][0], 128], [VW, H], [1, HD]])
        P.op("scalar", lambda s, o=out, i_=psum_ap.rearrange("p (h e) -> p h e", h=H):
             s.activation(out=o, in_=i_, func=AF.Copy),
             reads=bankbufs, writes=[v_tmb])

    emit_proj_formA(ctx, xn_chunks, xnbuf, wv, wvb, ND, 512,
                    bias_row=bv_row, biasrowb=bvb, consume=v_consume, group=4)
    A.release()
    A.release()
    P.barrier()

    A.mark()
    sqh = A.f32(8, parts=4)
    sqhb = Buf("sqhF")
    negm = A.f32(4, parts=4)
    negmb = Buf("negmF")
    negB_all = A.f32(16)
    negBhb = Buf("negBallF")
    P.op("scalar", lambda s: s.activation(out=sqh, in_=mall[0:4, 0:8], func=AF.Sqrt),
         reads=[msqB["buf"], mskB["buf"]], writes=[sqhb])
    P.op("vector", lambda v: v.tensor_tensor(out=negm[0:4, :], in0=sqh[0:4, 0:4],
                                             in1=sqh[0:4, 4:8], op=ALU.mult),
         reads=[sqhb], writes=[negmb])
    P.op("vector", lambda v: v.tensor_scalar_mul(out=negm[0:4, :], in0=negm[0:4, :],
                                                 scalar1=-SCL),
         reads=[negmb], writes=[negmb])
    P.dma("sync", d["negd2"], negm[0:4, :], reads=[ctx["negd2b"]] and [negmb], writes=[ctx["negd2b"]])
    P.dma("sync", negB_all,
          bass.AP(d["negd2"].tensor, d["negd2"].offset, [[0, 128], [1, 16]]),
          reads=[ctx["negd2b"]], writes=[negBhb])
    emit_attention_core(ctx, H, HD, SCL, False, d["q"][0:512, :], qdramb,
                        d["k"][0:512, :], kdramb, v_tm, v_tmb, negB_all, negBhb,
                        d["ao"], aodramb, d["denh"], dendramb,
                        lambda h: negB_all[:, 4 * (h % 4) + h // 4:4 * (h % 4) + h // 4 + 1])
    A.release()
    A.release()
    P.barrier()

    A.mark()
    ao_chunks = []
    aobuf = Buf("aofm")
    for c in range(4):
        ap = A.bf16(T)
        ao_chunks.append(ap)
        P.dma("sync", ap, d["ao"][128 * c:128 * (c + 1), :],
              reads=[aodramb], writes=[aobuf])
    emit_bulk_normalize(ctx, 16, 32, ao_chunks, aobuf, d["denh"], dendramb, d["ralla"])
    wo, wob = load_weight_fm(ctx, iw["F_ow"], 4, D)
    bo_col = A.f32(ND)
    bob = Buf("boF")
    P.dma("sync", bo_col, iw["F_ob_col"], writes=[bob])
    emit_proj_formB(ctx, wo, wob, ao_chunks, aobuf, d["mla"], mladramb, 4, ND,
                    bias_col=bo_col, biasbuf=bob)
    A.release()
    P.barrier()

    A.mark()
    mla_chunks = []
    mlabuf = Buf("mlafm")
    for c in range(ND):
        ap = A.bf16(T)
        mla_chunks.append(ap)
        P.dma("sync", ap, d["mla"][128 * c:128 * (c + 1), :],
              reads=[mladramb], writes=[mlabuf])
    rw, rwb = load_weight_fm(ctx, iw["F_rw"], ND, 8)
    rb_row = A.bf16(8, parts=1)
    rbb = Buf("rbrow")
    P.dma("gpsimd", rb_row, iw["F_rb"].unsqueeze(0), writes=[rbb])
    rcolsL = [A.f32(8) for _ in range(2)]
    rcolsbL = [Buf(f"rcols{i}") for i in range(2)]
    rlg_dbg = A.f32(8)
    rlgdbgb = Buf("rlgdbg")
    elL = [A.f32(8) for _ in range(2)]
    elbL = [Buf(f"el{i}") for i in range(2)]
    probsL = [A.f32(8) for _ in range(2)]
    probsbL = [Buf(f"probs{i}") for i in range(2)]
    mask1L = [A.f32(8) for _ in range(2)]
    mask1bL = [Buf(f"mask1{i}") for i in range(2)]
    combL = [A.bf16(8) for _ in range(2)]
    combbL = [Buf(f"comb{i}") for i in range(2)]

    def r_consume(tt, psum_ap, bankbufs):
        sl = tt % 2
        rcols, rcolsb = rcolsL[sl], rcolsbL[sl]
        el, elb = elL[sl], elbL[sl]
        probs, probsb = probsL[sl], probsbL[sl]
        mask1, mask1b = mask1L[sl], mask1bL[sl]
        comb, combb = combL[sl], combbL[sl]
        if DBG2:
            P.op("vector", lambda v, p=psum_ap, o=rlg_dbg:
                 v.tensor_copy(out=o, in_=p), reads=bankbufs, writes=[rlgdbgb])
            P.dma("sync", d["rldbg"][128 * tt:128 * (tt + 1), :], rlg_dbg,
                  reads=[rlgdbgb])
        m_c, nm_c, s_c, m2_c, d2_c = (rcols[:, i:i + 1] for i in range(5))
        P.op("vector", lambda v, p=psum_ap, o=m_c: v.reduce_max(out=o, in_=p, axis=AX.X),
             reads=bankbufs, writes=[rcolsb])
        P.op("vector", lambda v, o=nm_c, i_=m_c:
             v.tensor_scalar_mul(out=o, in0=i_, scalar1=-1.0),
             reads=[rcolsb], scalar_reads=[rcolsb], writes=[rcolsb])
        P.op("scalar", lambda s, p=psum_ap, o=el, b=nm_c:
             s.activation(out=o, in_=p, func=AF.Exp, bias=b, accum_out=s_c),
             reads=bankbufs, scalar_reads=[rcolsb], writes=[elb, rcolsb])
        P.op("scalar", lambda s, o=s_c, i_=s_c: s.activation(out=o, in_=i_, func=AF.Ln),
             reads=[rcolsb], writes=[rcolsb])
        P.op("scalar", lambda s, o=s_c, i_=s_c: s.activation(out=o, in_=i_, func=AF.Exp,
                                                             scale=-1.0),
             reads=[rcolsb], writes=[rcolsb])
        P.op("vector", lambda v, o=probs, i_=el, s1=s_c:
             v.tensor_scalar_mul(out=o, in0=i_, scalar1=s1),
             reads=[elb], scalar_reads=[rcolsb], writes=[probsb])
        P.op("vector", lambda v, o=m_c, i_=probs: v.reduce_max(out=o, in_=i_, axis=AX.X),
             reads=[probsb], writes=[rcolsb])
        P.op("vector", lambda v, o=mask1, i_=probs, s1=m_c:
             v.tensor_single_scalar(out=o, in_=i_, scalar=s1, op=ALU.is_ge),
             reads=[probsb], scalar_reads=[rcolsb], writes=[mask1b])
        P.op("vector", lambda v, o=el:
             v.scalar_tensor_tensor(out=o, in0=mask1, scalar=-1e30, in1=probs,
                                    op0=ALU.mult, op1=ALU.add),
             reads=[mask1b, probsb], writes=[elb])
        P.op("vector", lambda v, o=m2_c, i_=el: v.reduce_max(out=o, in_=i_, axis=AX.X),
             reads=[elb], writes=[rcolsb])
        P.op("vector", lambda v, o=mask1, i_=probs, s1=m2_c:
             v.tensor_single_scalar(out=o, in_=i_, scalar=s1, op=ALU.is_ge),
             reads=[probsb], scalar_reads=[rcolsb], writes=[mask1b])
        P.op("vector", lambda v, o=d2_c, i_=m_c, s1=m2_c:
             v.tensor_scalar(out=o, in0=i_, scalar1=s1, scalar2=1e-8,
                             op0=ALU.add, op1=ALU.add),
             reads=[rcolsb], scalar_reads=[rcolsb], writes=[rcolsb])
        P.op("scalar", lambda s, o=d2_c, i_=d2_c: s.activation(out=o, in_=i_, func=AF.Ln),
             reads=[rcolsb], writes=[rcolsb])
        P.op("scalar", lambda s, o=d2_c, i_=d2_c: s.activation(out=o, in_=i_, func=AF.Exp,
                                                               scale=-1.0),
             reads=[rcolsb], writes=[rcolsb])
        P.op("vector", lambda v, o=comb, s1=d2_c:
             v.scalar_tensor_tensor(out=o, in0=probs, scalar=s1, in1=mask1,
                                    op0=ALU.mult, op1=ALU.mult),
             reads=[probsb, mask1b], scalar_reads=[rcolsb], writes=[combb])
        if DBG2:
            P.dma("sync", d["rp_dbg"][128 * tt:128 * (tt + 1), 0:8], probs,
                  reads=[probsb])
            P.dma("sync", d["rp_dbg"][128 * tt:128 * (tt + 1), 8:16], mask1,
                  reads=[mask1b])
            P.dma("sync", d["rp_dbg"][128 * tt:128 * (tt + 1), 16:24], rcols[:, 0:8],
                  reads=[rcolsb])
        P.op("tensor", lambda t, tt=tt, sl=sl, comb=comb, combb=combb:
             t.matmul(ctx["psum"][0:8, 512 * (4 + sl):512 * (4 + sl) + 128],
                      comb, ctx["ident"], start=True, stop=True),
             reads=[combb, ctx["identb"]], writes=[pbufs[4 + sl]])
        P.op("vector", lambda v, tt=tt, sl=sl, o=cT_sb[0:8, 128 * tt:128 * (tt + 1)]:
             v.tensor_copy(out=o, in_=ctx["psum"][0:8, 512 * (4 + sl):512 * (4 + sl) + 128]),
             reads=[pbufs[4 + sl]], writes=[cTb])

    emit_proj_formA(ctx, mla_chunks, mlabuf, rw, rwb, ND, 8,
                    bias_row=rb_row, biasrowb=rbb, consume=r_consume, group=4)
    if DBG2:
        P.dma("sync", d["ctdbg"], cT_sb[0:8, :], reads=[cTb])
    A.release()
    P.barrier()

    A.mark()
    mla_win = [A.bf16(1024) for _ in range(ND)]
    mlawb = Buf("mlawin")
    w1sb = [A.bf16(2048) for _ in range(ND)]
    w1b = Buf("w1e")
    w2sb = [A.bf16(1024) for _ in range(16)]
    w2b = Buf("w2e")
    b1col = A.f32(16)
    b1b = Buf("b1e")
    b2all = A.bf16(1024, parts=8)
    b2b = Buf("b2all")
    P.dma("gpsimd", b2all, iw["F_b2"], writes=[b2b])
    cb_sb = A.bf16(T)
    cbb = Buf("cb")
    h1g = A.bf16(16 * 1024)
    h1gb = Buf("h1g")
    mstage = [A.f32(1024) for _ in range(2)]
    mstageb = [Buf(f"mst{i}") for i in range(2)]

    moet = [A.f32(D) for _ in range(2)]
    moetb = [Buf(f"moet{i}") for i in range(2)]
    xnt = [A.bf16(D) for _ in range(2)]
    xntb = [Buf(f"xnt{i}") for i in range(2)]
    bo_t = [A.f32(D) for _ in range(2)]
    bo_tb = [Buf(f"bot{i}") for i in range(2)]
    fxt = [A.f32(D) for _ in range(2)]
    fxb = [Buf(f"fx{i}") for i in range(2)]
    fx1 = [A.f32(D) for _ in range(2)]
    fx1b = [Buf(f"fx1{i}") for i in range(2)]
    fdump = A.bf16(D)
    fdumpb = Buf("fdump")
    fcols = A.f32(16)
    fcolb = [Buf(f"fc{i}") for i in range(2)]
    ilst = emit_inline_ln_scratch(ctx) if next_ln else None

    def f_final(tt):
        sl = tt % 2
        cb = fcolb[sl]
        P.dma("sync", moet[sl], d["moe"][128 * tt:128 * (tt + 1), :],
              reads=[moedramb], writes=[moetb[sl]])
        P.dma("sync", xnt[sl], d["xn"][128 * tt:128 * (tt + 1), :], writes=[xntb[sl]])
        P.dma("sync", fxt[sl], x_src_dram[128 * tt:128 * (tt + 1), :], writes=[fxb[sl]])
        P.op("vector", lambda v, sl=sl, o=bo_t[sl]:
             v.scalar_tensor_tensor(out=o, in0=moet[sl], scalar=0.3, in1=xnt[sl],
                                    op0=ALU.mult, op1=ALU.add),
             reads=[moetb[sl], xntb[sl]], writes=[bo_tb[sl]])
        s_c = fcols[:, 5 * sl + 0:5 * sl + 1]
        ss_c = fcols[:, 5 * sl + 1:5 * sl + 2]
        P.op("vector", lambda v, sl=sl, o=s_c: v.reduce_sum(out=o, in_=bo_t[sl], axis=AX.X),
             reads=[bo_tb[sl]], writes=[cb])
        P.op("scalar", lambda s, sl=sl, o=fdump, a=ss_c:
             s.activation(out=o, in_=bo_t[sl], func=AF.Square, accum_out=a),
             reads=[bo_tb[sl]], writes=[fdumpb, cb])
        rstd_c, nmr_c = emit_ln_cols(ctx, fcols, sl, cb)
        P.op("vector", lambda v, sl=sl, s1=nmr_c:
             v.tensor_scalar_add(out=fxt[sl], in0=fxt[sl], scalar1=s1),
             reads=[fxb[sl]], scalar_reads=[cb], writes=[fxb[sl]])
        P.op("vector", lambda v, sl=sl, o=fx1[sl], s1=rstd_c:
             v.scalar_tensor_tensor(out=o, in0=bo_t[sl], scalar=s1, in1=fxt[sl],
                                    op0=ALU.mult, op1=ALU.add),
             reads=[bo_tb[sl], fxb[sl]], scalar_reads=[cb], writes=[fx1b[sl]])
        P.dma("sync", x_dst_dram[128 * tt:128 * (tt + 1), :], fx1[sl],
              reads=[fx1b[sl]])
        if next_ln:
            emit_inline_ln(ctx, ilst, tt, fx1[sl], fx1b[sl])

    for e in range(8):
        for c in range(ND):
            P.dma("gpsimd", w1sb[c], iw["F_w1"][e, 128 * c:128 * (c + 1), :], writes=[w1b])
        for f in range(16):
            P.dma("gpsimd", w2sb[f], iw["F_w2"][e, 128 * f:128 * (f + 1), :], writes=[w2b])
        P.dma("sync", b1col, iw["F_b1_col"][e], writes=[b1b])
        for blk in range(NB):
            P.op("tensor", lambda t, e=e, blk=blk:
                 t.matmul(ctx["psum"][:, 512 * 4:512 * 5],
                          ctx["e8"][0:8, 128 * e:128 * (e + 1)],
                          cT_sb[0:8, 512 * blk:512 * (blk + 1)],
                          start=True, stop=True),
                 reads=[cTb, ctx["e8b"]], writes=[pbufs[4]])
            P.op("vector", lambda v, blk=blk, o=cb_sb[:, 512 * blk:512 * (blk + 1)]:
                 v.tensor_copy(out=o, in_=ctx["psum"][:, 512 * 4:512 * 5]),
                 reads=[pbufs[4]], writes=[cbb])
        for ps in range(4):
            w0 = 1024 * ps
            for c in range(ND):
                P.dma("sync", mla_win[c], d["mla"][128 * c:128 * (c + 1), w0:w0 + 1024],
                      reads=[mladramb], writes=[mlawb])
            for f in range(16):
                pb = 4 + 2 * (f % 2)
                def h1fn(t, f=f, pb=pb):
                    last = None
                    for nb in range(2):
                        for c in range(ND):
                            last = t.matmul(_bank(ctx, pb + nb),
                                            w1sb[c][:, 128 * f:128 * (f + 1)],
                                            mla_win[c][:, 512 * nb:512 * (nb + 1)],
                                            start=(c == 0), stop=(c == ND - 1))
                    return last
                P.op("tensor", h1fn, reads=[w1b, mlawb],
                     writes=[pbufs[pb], pbufs[pb + 1]])
                P.op("scalar", lambda s, f=f, pb=pb, o=h1g[:, 1024 * f:1024 * (f + 1)],
                     b=b1col[:, f:f + 1]:
                     s.activation(out=o, in_=_bank(ctx, pb, 2), func=AF.Gelu, bias=b),
                     reads=[pbufs[pb], pbufs[pb + 1]], scalar_reads=[b1b], writes=[h1gb])
                P.op("vector", lambda v, f=f, ps=ps, o=h1g[:, 1024 * f:1024 * (f + 1)],
                     cbw=cb_sb[:, w0:w0 + 1024]:
                     v.tensor_mul(out=o, in0=o, in1=cbw),
                     reads=[h1gb, cbb], writes=[h1gb])
            for tti in range(8):
                pb = 2 * (tti % 2)
                def h2fn(t, tti=tti, pb=pb, e=e, ps=ps):
                    last = None
                    for nb in range(2):
                        for f in range(16):
                            last = t.matmul(_bank(ctx, pb + nb),
                                            h1g[:, 1024 * f + 128 * tti:1024 * f + 128 * (tti + 1)],
                                            w2sb[f][:, 512 * nb:512 * (nb + 1)],
                                            start=(f == 0), stop=(f == 15 and e != 0))
                        if e == 0:
                            tt = ps * 8 + tti
                            last = t.matmul(_bank(ctx, pb + nb),
                                            cT_sb[0:8, 128 * tt:128 * (tt + 1)],
                                            b2all[0:8, 512 * nb:512 * (nb + 1)],
                                            start=False, stop=True)
                    return last
                reads = [h1gb, w2b] + ([cTb, b2b] if e == 0 else [])
                P.op("tensor", h2fn, reads=reads, writes=[pbufs[pb], pbufs[pb + 1]])
                msl = tti % 2
                P.op("vector", lambda v, pb=pb, o=mstage[msl]:
                     v.tensor_copy(out=o, in_=_bank(ctx, pb, 2)),
                     reads=[pbufs[pb], pbufs[pb + 1]], writes=[mstageb[msl]])
                tt = ps * 8 + tti
                if e == 0:
                    P.dma("gpsimd", d["moe"][128 * tt:128 * (tt + 1), :], mstage[msl],
                          reads=[mstageb[msl]], writes=[moedramb])
                else:
                    P.op("gpsimd", lambda g, tt=tt, msl=msl:
                         g.dma_start(out=d["moe"][128 * tt:128 * (tt + 1), :],
                                     in_=mstage[msl], accum_op=ALU.add),
                         reads=[mstageb[msl]], writes=[moedramb], dma=True)
                    if e == 7:
                        f_final(tt)
    A.release()
    A.release()
    P.barrier()


KSOFT_C = 40.0

def emit_block_B(ctx, x_src_dram, x_dst_dram, iw, skip_ln=False, next_ln=False):
    P, A, nc = ctx["P"], ctx["arena"], ctx["nc"]
    H, HD = 16, 64
    d = ctx["dram"]
    pbufs = ctx["psum_bufs"]
    qhb, khb, vhb, zb = Buf("qhB"), Buf("khB"), Buf("vhB"), Buf("zB")
    rowsb = Buf("rowsB")

    if not skip_ln:
        emit_ln_tm(ctx, x_src_dram, d["xn"])
        P.barrier()

    A.mark()
    rk_row = A.f32(D, parts=1)
    rk_rowb = Buf("rkrow")
    rcol = [A.f32(1) for _ in range(ND)]
    rcolb = Buf("rcolB")

    A.mark()
    xn_chunks, xnbuf = load_xn_fm(ctx, d["xn"])

    A.mark()
    wq, wqb = load_weight_fm(ctx, iw["B_wq"], ND, D)
    bq_row = A.bf16(D, parts=1)
    bqb = Buf("bqB")
    P.dma("gpsimd", bq_row, iw["B_bq"].unsqueeze(0), writes=[bqb])
    qtL = [A.f32(D) for _ in range(2)]
    qtbL = [Buf(f"qtB{i}") for i in range(2)]
    qeL = [A.bf16(D) for _ in range(2)]
    qebL = [Buf(f"qeB{i}") for i in range(2)]
    segL = [A.f32(16) for _ in range(2)]
    segbL = [Buf(f"segB{i}") for i in range(2)]

    def q_consume(tt, psum_ap, bankbufs):
        sl = tt % 2
        qt, qtb = qtL[sl], qtbL[sl]
        qe, qeb = qeL[sl], qebL[sl]
        seg, segb = segL[sl], segbL[sl]
        pv = psum_ap.rearrange("p (s i) -> p s i", s=16)
        P.op("vector", lambda v, o=seg[:, 0:16], pv=pv:
             v.reduce_max(out=o, in_=pv, axis=AX.X),
             reads=bankbufs, writes=[segb])
        P.op("vector", lambda v, pv=pv, o=qt[:].rearrange("p (s i) -> p s i", s=16),
             b=seg[:, 0:16].unsqueeze(2).broadcast_to([128, 16, 64]):
             v.tensor_tensor(out=o, in0=pv, in1=b, op=ALU.subtract),
             reads=bankbufs, scalar_reads=[segb], writes=[qtb])
        P.op("scalar", lambda s: s.activation(out=qe, in_=qt, func=AF.Exp),
             reads=[qtb], writes=[qeb])
        P.op("vector", lambda v, o=seg[:, 0:16], i_=qe[:].rearrange("p (s i) -> p s i", s=16):
             v.reduce_sum(out=o, in_=i_, axis=AX.X),
             reads=[qeb], writes=[segb])
        P.op("scalar", lambda s, o=seg[:, 0:16], i_=seg[:, 0:16]:
             s.activation(out=o, in_=i_, func=AF.Ln),
             reads=[segb], writes=[segb])
        P.op("scalar", lambda s, o=seg[:, 0:16], i_=seg[:, 0:16]:
             s.activation(out=o, in_=i_, func=AF.Exp, scale=-1.0),
             reads=[segb], writes=[segb])
        P.op("vector", lambda v, o=qe[:].rearrange("p (s i) -> p s i", s=16),
             i_=qe[:].rearrange("p (s i) -> p s i", s=16),
             b=seg[:, 0:16].unsqueeze(2).broadcast_to([128, 16, 64]):
             v.tensor_tensor(out=o, in0=i_, in1=b, op=ALU.mult),
             reads=[qeb], scalar_reads=[segb], writes=[qeb])
        P.dma("sync", d["qh"][128 * tt:128 * (tt + 1), :], qe,
              reads=[qeb], writes=[qhb])

    emit_proj_formA(ctx, xn_chunks, xnbuf, wq, wqb, ND, D,
                    bias_row=bq_row, biasrowb=bqb, consume=q_consume, group=4)
    A.release()
    P.barrier()

    A.mark()
    wk, wkb = load_weight_fm(ctx, iw["B_wk"], ND, D)
    bk_row = A.bf16(D, parts=1)
    bkb = Buf("bkB")
    P.dma("gpsimd", bk_row, iw["B_bk"].unsqueeze(0), writes=[bkb])
    keL = [A.bf16(D) for _ in range(2)]
    kebL = [Buf(f"keB{i}") for i in range(2)]
    kssum_started = [False]

    def k_consume(tt, psum_ap, bankbufs):
        ke, keb = keL[tt % 2], kebL[tt % 2]
        P.op("scalar", lambda s, p=psum_ap, ng=ctx["negc"]:
             s.activation(out=ke, in_=p, func=AF.Exp, bias=ng),
             reads=bankbufs, scalar_reads=[ctx["negcb"]], writes=[keb])
        st = not kssum_started[0]
        kssum_started[0] = True
        def sumfn(t, st=st, tt=tt):
            last = None
            for half in range(2):
                last = t.matmul(ctx["psum"][0:1, 512 * (6 + half):512 * (6 + half) + 512],
                                ctx["ones_colb"], ke[:, 512 * half:512 * (half + 1)],
                                start=st, stop=(tt == NT - 1))
            return last
        P.op("tensor", sumfn, reads=[keb, ctx["ones_colbb"]],
             writes=[pbufs[6], pbufs[7]])
        P.dma("sync", d["kh"][128 * tt:128 * (tt + 1), :], ke,
              reads=[keb], writes=[khb])

    emit_proj_formA(ctx, xn_chunks, xnbuf, wk, wkb, ND, D,
                    bias_row=bk_row, biasrowb=bkb, consume=k_consume, group=2)
    P.op("scalar", lambda s: s.activation(out=rk_row[:, 0:512],
                                          in_=ctx["psum"][0:1, 512 * 6:512 * 6 + 512],
                                          func=AF.Ln),
         reads=[pbufs[6]], writes=[rk_rowb])
    P.op("scalar", lambda s: s.activation(out=rk_row[:, 512:1024],
                                          in_=ctx["psum"][0:1, 512 * 7:512 * 7 + 512],
                                          func=AF.Ln),
         reads=[pbufs[7]], writes=[rk_rowb])
    P.op("scalar", lambda s: s.activation(out=rk_row, in_=rk_row, func=AF.Exp, scale=-1.0),
         reads=[rk_rowb], writes=[rk_rowb])
    P.dma("sync", d["row2"][0:1, :], rk_row, reads=[rk_rowb], writes=[rowsb])
    for c in range(ND):
        P.dma("sync", rcol[c],
              bass.AP(d["row2"].tensor, d["row2"].offset + 128 * c, [[1, 128], [1, 1]]),
              reads=[rowsb], writes=[rcolb])
    A.release()
    P.barrier()

    A.mark()
    wv, wvb = load_weight_fm(ctx, iw["B_wv"], ND, D)
    bv_row = A.bf16(D, parts=1)
    bvb = Buf("bvB")
    P.dma("gpsimd", bv_row, iw["B_bv"].unsqueeze(0), writes=[bvb])
    vstageL = [A.bf16(D) for _ in range(2)]
    vstagebL = [Buf(f"vstB{i}") for i in range(2)]

    def v_consume(tt, psum_ap, bankbufs):
        vstage, vstageb = vstageL[tt % 2], vstagebL[tt % 2]
        P.op("scalar", lambda s, p=psum_ap: s.activation(out=vstage, in_=p, func=AF.Copy),
             reads=bankbufs, writes=[vstageb])
        P.dma("sync", d["vh"][128 * tt:128 * (tt + 1), :], vstage,
              reads=[vstageb], writes=[vhb])

    emit_proj_formA(ctx, xn_chunks, xnbuf, wv, wvb, ND, D,
                    bias_row=bv_row, biasrowb=bvb, consume=v_consume, group=4)
    A.release()
    A.release()
    P.barrier()

    A.mark()
    kp2 = [A.bf16(NT * 128) for _ in range(2)]
    vp2 = [A.bf16(NT * 128) for _ in range(2)]
    qf2 = [A.bf16(T) for _ in range(2)]
    qpan = [A.bf16(T, parts=64) for _ in range(2)]
    kp2b = [Buf(f"kp2{i}") for i in range(2)]
    vp2b = [Buf(f"vp2{i}") for i in range(2)]
    qf2b = [Buf(f"qf2{i}") for i in range(2)]
    qpanb = [Buf(f"qpan{i}") for i in range(2)]
    state_sb = A.f32(128)
    state_bf = A.bf16(128)
    stateb = Buf("state")
    z_stage = A.bf16(T)
    z_stageb = Buf("zstage")

    for c in range(ND):
        sl = c % 2
        for part in range(4):
            src_k = bass.AP(d["kh"].tensor, d["kh"].offset + 1024 * part * D + 128 * c,
                            [[D, 128], [128 * D, 8], [1, 128]])
            src_v = bass.AP(d["vh"].tensor, d["vh"].offset + 1024 * part * D + 128 * c,
                            [[D, 128], [128 * D, 8], [1, 128]])
            P.dma("sync", kp2[sl][:, 1024 * part:1024 * (part + 1)],
                  src_k, reads=[khb], writes=[kp2b[sl]])
            P.dma("sync", vp2[sl][:, 1024 * part:1024 * (part + 1)],
                  src_v, reads=[vhb], writes=[vp2b[sl]])
        for slab in range(T // 512):
            P.dma("sync", qf2[sl][:, 512 * slab:512 * (slab + 1)],
                  d["qh"][512 * slab:512 * (slab + 1), 128 * c:128 * (c + 1)],
                  reads=[qhb], writes=[qf2b[sl]], transpose=True)
        P.dma("gpsimd", qpan[sl], qf2[sl][64:128, :], reads=[qf2b[sl]], writes=[qpanb[sl]])
        for half in range(2):
            h = 2 * c + half
            sbank = 4 + half
            def statefn(t, sl=sl, half=half, sbank=sbank):
                last = None
                for tt in range(NT):
                    last = t.matmul(ctx["psum"][0:64, 512 * sbank:512 * sbank + 64],
                                    kp2[sl][:, 128 * tt + 64 * half:128 * tt + 64 * (half + 1)],
                                    vp2[sl][:, 128 * tt + 64 * half:128 * tt + 64 * (half + 1)],
                                    start=(tt == 0), stop=(tt == NT - 1))
                return last
            P.op("tensor", statefn, reads=[kp2b[sl], vp2b[sl]],
                 writes=[pbufs[sbank]])
            P.op("vector", lambda v, half=half, c=c, sbank=sbank,
                 o=state_sb[0:64, 64 * half:64 * (half + 1)],
                 s1=rcol[c][64 * half:64 * (half + 1), 0:1]:
                 v.tensor_scalar_mul(out=o, in0=ctx["psum"][0:64, 512 * sbank:512 * sbank + 64],
                                     scalar1=s1),
                 reads=[pbufs[sbank]], scalar_reads=[rcolb], writes=[stateb])
            P.op("vector", lambda v, half=half,
                 o=state_bf[0:64, 64 * half:64 * (half + 1)],
                 i_=state_sb[0:64, 64 * half:64 * (half + 1)]:
                 v.tensor_copy(out=o, in_=i_),
                 reads=[stateb], writes=[stateb])
            rhs_src = qf2[sl] if half == 0 else qpan[sl]
            rhs_b = qf2b[sl] if half == 0 else qpanb[sl]
            for blk in range(NB):
                ob = blk % 4
                P.op("tensor", lambda t, half=half, blk=blk, ob=ob, rs=rhs_src:
                     t.matmul(ctx["psum"][0:64, 512 * ob:512 * ob + 512],
                              state_bf[0:64, 64 * half:64 * (half + 1)],
                              rs[0:64, 512 * blk:512 * (blk + 1)],
                              start=True, stop=True),
                     reads=[stateb, rhs_b], writes=[pbufs[ob]])
                P.op("vector", lambda v, half=half, blk=blk, ob=ob,
                     o=z_stage[64 * half:64 * (half + 1), 512 * blk:512 * (blk + 1)]:
                     v.tensor_copy(out=o, in_=ctx["psum"][0:64, 512 * ob:512 * ob + 512]),
                     reads=[pbufs[ob]], writes=[z_stageb])
        P.dma("sync", d["z"][128 * c:128 * (c + 1), :], z_stage,
              reads=[z_stageb], writes=[zb])
    A.release()
    P.barrier()

    A.mark()
    z_chunks = []
    zfmb = Buf("zfm")
    for c in range(ND):
        ap = A.bf16(T)
        z_chunks.append(ap)
        P.dma("sync", ap, d["z"][128 * c:128 * (c + 1), :], reads=[zb], writes=[zfmb])
    zsq = A.bf16(512)
    zsqb = Buf("zsqB")
    stats_sb = A.f32(512, parts=2)
    stats_sbb = Buf("statsB")
    stats_tm = A.f32(2 * NT)
    stats_tmb = Buf("statstmB")
    for blk in range(NB):
        for c in range(ND):
            P.op("scalar", lambda s, c=c, blk=blk, o=zsq:
                 s.activation(out=o, in_=z_chunks[c][:, 512 * blk:512 * (blk + 1)],
                              func=AF.Square),
                 reads=[zfmb], writes=[zsqb])
            def stfn(t, c=c, blk=blk):
                i1 = t.matmul(ctx["psum"][0:2, 512 * 7:512 * 7 + 512],
                              ctx["statpair"][:, 0:2],
                              z_chunks[c][:, 512 * blk:512 * (blk + 1)],
                              start=(c == 0), stop=False)
                i2 = t.matmul(ctx["psum"][0:2, 512 * 7:512 * 7 + 512],
                              ctx["statpair"][:, 2:4], zsq,
                              start=False, stop=(c == ND - 1))
                return i2
            P.op("tensor", stfn, reads=[zfmb, zsqb, ctx["statpairb"]],
                 writes=[pbufs[7]])
        P.op("vector", lambda v, blk=blk: v.tensor_copy(
             out=stats_sb[0:2, :], in_=ctx["psum"][0:2, 512 * 7:512 * 7 + 512]),
             reads=[pbufs[7]], writes=[stats_sbb])
        for q in range(4):
            tt = 4 * blk + q
            P.op("tensor", lambda t, q=q: t.transpose(
                 ctx["psum"][0:128, 512 * 6:512 * 6 + 2],
                 stats_sb[0:2, 128 * q:128 * (q + 1)], ctx["identf"][0:2, 0:2]),
                 reads=[stats_sbb, ctx["identfb"]], writes=[pbufs[6]])
            P.op("vector", lambda v, tt=tt, o=stats_tm[:, 2 * tt:2 * (tt + 1)]:
                 v.tensor_copy(out=o, in_=ctx["psum"][0:128, 512 * 6:512 * 6 + 2]),
                 reads=[pbufs[6]], writes=[stats_tmb])
    P.barrier()
    wo, wob = load_weight_fm(ctx, iw["B_wo"], ND, D)
    g_col = A.f32(ND)
    gb = Buf("gcolB")
    P.dma("sync", g_col, iw["B_g_col"], writes=[gb])
    be_col = A.f32(ND)
    beb = Buf("becolB")
    P.dma("sync", be_col, iw["B_b_col"], writes=[beb])
    gbf = A.bf16(ND)
    bbf = A.bf16(ND)
    gbfb = Buf("gbf")
    P.op("vector", lambda v: v.tensor_copy(out=gbf, in_=g_col), reads=[gb], writes=[gbfb])
    P.op("vector", lambda v: v.tensor_copy(out=bbf, in_=be_col), reads=[beb], writes=[gbfb])
    bo_row_f = A.f32(D, parts=1)
    bo_rowb = Buf("borowB")
    P.dma("sync", bo_row_f, iw["B_bo"].unsqueeze(0), writes=[bo_rowb])
    rowtmp = A.f32(D, parts=1)
    rowtmpb = Buf("rowtmp")
    def w1sumfn(t):
        last = None
        for c in range(ND):
            for half in range(2):
                last = t.matmul(ctx["psum"][0:1, 512 * (6 + half):512 * (6 + half) + 512],
                                gbf[:, c:c + 1], wo[c][:, 512 * half:512 * (half + 1)],
                                start=(c == 0), stop=(c == ND - 1))
        return last
    P.op("tensor", w1sumfn, reads=[wob, gbfb], writes=[pbufs[6], pbufs[7]])
    P.op("vector", lambda v: v.tensor_copy(out=rowtmp[:, 0:512],
                                           in_=ctx["psum"][0:1, 512 * 6:512 * 6 + 512]),
         reads=[pbufs[6]], writes=[rowtmpb])
    P.op("vector", lambda v: v.tensor_copy(out=rowtmp[:, 512:1024],
                                           in_=ctx["psum"][0:1, 512 * 7:512 * 7 + 512]),
         reads=[pbufs[7]], writes=[rowtmpb])
    P.dma("sync", d["row2"][0:1, :], rowtmp, reads=[rowtmpb], writes=[rowsb])
    def befn(t):
        last = None
        for c in range(ND):
            for half in range(2):
                last = t.matmul(ctx["psum"][0:1, 512 * (6 + half):512 * (6 + half) + 512],
                                bbf[:, c:c + 1], wo[c][:, 512 * half:512 * (half + 1)],
                                start=(c == 0), stop=(c == ND - 1))
        return last
    P.op("tensor", befn, reads=[wob, gbfb], writes=[pbufs[6], pbufs[7]])
    for half in range(2):
        P.op("vector", lambda v, half=half:
             v.tensor_add(out=rowtmp[:, 512 * half:512 * (half + 1)],
                          in0=ctx["psum"][0:1, 512 * (6 + half):512 * (6 + half) + 512],
                          in1=bo_row_f[:, 512 * half:512 * (half + 1)]),
             reads=[pbufs[6 + half], bo_rowb], writes=[rowtmpb])
    P.dma("sync", d["row2"][1:2, :], rowtmp, reads=[rowtmpb], writes=[rowsb])
    w1s_bc = A.f32(D)
    beff_bc = A.f32(D)
    bcb = Buf("bcB")
    P.dma("sync", w1s_bc, bass.AP(d["row2"].tensor, d["row2"].offset, [[0, 128], [1, D]]),
          reads=[rowsb], writes=[bcb])
    P.dma("sync", beff_bc, bass.AP(d["row2"].tensor, d["row2"].offset + D, [[0, 128], [1, D]]),
          reads=[rowsb], writes=[bcb])
    for c in range(ND):
        P.op("vector", lambda v, c=c, o=wo[c], s1=g_col[:, c:c + 1]:
             v.tensor_scalar_mul(out=o, in0=o, scalar1=s1),
             reads=[wob], scalar_reads=[gb], writes=[wob])
    xt = [A.f32(D) for _ in range(2)]
    xb_ = [Buf(f"bx{i}") for i in range(2)]
    x1 = [A.f32(D) for _ in range(2)]
    x1b = [Buf(f"bx1{i}") for i in range(2)]
    t1 = [A.f32(D) for _ in range(2)]
    t1b = [Buf(f"bt1{i}") for i in range(2)]
    dump = A.bf16(D)
    dumpb = Buf("bdump")
    cols = A.f32(16)
    colb = [Buf(f"bc{i}") for i in range(2)]
    lcols = A.f32(8)
    lcolb = [Buf(f"blc{i}") for i in range(2)]
    ilst = emit_inline_ln_scratch(ctx) if next_ln else None

    def final_consume(tt, psum_ap, bankbufs):
        sl = tt % 2
        cb = colb[sl]
        lcb = lcolb[sl]
        mu_c = stats_tm[:, 2 * tt:2 * tt + 1]
        msq_c = stats_tm[:, 2 * tt + 1:2 * tt + 2]
        v_c = lcols[:, 4 * sl + 0:4 * sl + 1]
        rst_c = lcols[:, 4 * sl + 1:4 * sl + 2]
        nrm_c = lcols[:, 4 * sl + 2:4 * sl + 3]
        P.op("scalar", lambda s: s.activation(out=v_c, in_=mu_c, func=AF.Square),
             reads=[stats_tmb], writes=[lcb])
        P.op("vector", lambda v: v.tensor_scalar(out=v_c, in0=msq_c, scalar1=v_c,
                                                 scalar2=None, op0=ALU.subtract),
             reads=[stats_tmb], scalar_reads=[lcb], writes=[lcb])
        P.op("scalar", lambda s, e=ctx["epscol"]:
             s.activation(out=rst_c, in_=v_c, func=AF.Ln, bias=e),
             reads=[lcb], scalar_reads=[ctx["epscolb"]], writes=[lcb])
        P.op("scalar", lambda s: s.activation(out=rst_c, in_=rst_c, func=AF.Exp, scale=-0.5),
             reads=[lcb], writes=[lcb])
        P.op("vector", lambda v: v.tensor_scalar(out=nrm_c, in0=mu_c, scalar1=rst_c,
                                                 scalar2=-1.0, op0=ALU.mult, op1=ALU.mult),
             reads=[stats_tmb], scalar_reads=[lcb], writes=[lcb])
        P.op("vector", lambda v, sl=sl, p=psum_ap, o=t1[sl], s1=rst_c:
             v.tensor_scalar_mul(out=o, in0=p, scalar1=s1),
             reads=bankbufs, scalar_reads=[lcb], writes=[t1b[sl]])
        P.op("vector", lambda v, sl=sl, o=t1[sl], s1=nrm_c:
             v.scalar_tensor_tensor(out=o, in0=w1s_bc, scalar=s1, in1=o,
                                    op0=ALU.mult, op1=ALU.add),
             reads=[bcb, t1b[sl]], scalar_reads=[lcb], writes=[t1b[sl]])
        P.op("vector", lambda v, sl=sl, o=t1[sl]:
             v.tensor_add(out=o, in0=o, in1=beff_bc),
             reads=[t1b[sl], bcb], writes=[t1b[sl]])
        P.dma("sync", xt[sl], x_src_dram[128 * tt:128 * (tt + 1), :], writes=[xb_[sl]])
        s_c = cols[:, 5 * sl + 0:5 * sl + 1]
        ss_c = cols[:, 5 * sl + 1:5 * sl + 2]
        P.op("vector", lambda v, sl=sl, o=s_c: v.reduce_sum(out=o, in_=t1[sl], axis=AX.X),
             reads=[t1b[sl]], writes=[cb])
        P.op("scalar", lambda s, sl=sl, o=dump, a=ss_c:
             s.activation(out=o, in_=t1[sl], func=AF.Square, accum_out=a),
             reads=[t1b[sl]], writes=[dumpb, cb])
        rstd_c, nmr_c = emit_ln_cols(ctx, cols, sl, cb)
        P.op("vector", lambda v, sl=sl, s1=nmr_c:
             v.tensor_scalar_add(out=xt[sl], in0=xt[sl], scalar1=s1),
             reads=[xb_[sl]], scalar_reads=[cb], writes=[xb_[sl]])
        P.op("vector", lambda v, sl=sl, o=x1[sl], s1=rstd_c:
             v.scalar_tensor_tensor(out=o, in0=t1[sl], scalar=s1, in1=xt[sl],
                                    op0=ALU.mult, op1=ALU.add),
             reads=[t1b[sl], xb_[sl]], scalar_reads=[cb], writes=[x1b[sl]])
        P.dma("sync", x_dst_dram[128 * tt:128 * (tt + 1), :], x1[sl], reads=[x1b[sl]])
        if next_ln:
            emit_inline_ln(ctx, ilst, tt, x1[sl], x1b[sl])

    emit_proj_formA(ctx, z_chunks, zfmb, wo, wob, ND, D,
                    bias_row=None, biasrowb=None, consume=final_consume, group=4)
    A.release()
    A.release()
    P.barrier()


def emit_block_G(ctx, x_src_dram, x_dst_dram, iw, skip_ln=False):
    P, A, nc = ctx["P"], ctx["arena"], ctx["nc"]
    NE, PS = 64, 32
    NP = NE * PS
    d = ctx["dram"]
    pbufs = ctx["psum_bufs"]
    ddb, ccb2, ccinb, ccoutb = Buf("ddG"), Buf("ccG"), Buf("ccinG"), Buf("ccoutG")

    if not skip_ln:
        emit_ln_tm(ctx, x_src_dram, d["xn"])
        P.barrier()

    A.mark()
    A.mark()
    xn_chunks, xnbuf = load_xn_fm(ctx, d["xn"])
    phi, phib = load_weight_fm(ctx, iw["G_phi"], ND, NP)
    pb_row = A.bf16(NP, parts=1)
    pbb = Buf("pbG")
    P.dma("gpsimd", pb_row, iw["G_phib"].unsqueeze(0), writes=[pbb])
    eEL = [A.bf16(NP) for _ in range(2)]
    eEbL = [Buf(f"eEG{i}") for i in range(2)]
    dd_tL = [A.bf16(NP) for _ in range(2)]
    dd_tbL = [Buf(f"ddt{i}") for i in range(2)]
    cc_tL = [A.bf16(NP) for _ in range(2)]
    cc_tbL = [Buf(f"cct{i}") for i in range(2)]
    segDL = [A.f32(NE) for _ in range(2)]
    segDbL = [Buf(f"segD{i}") for i in range(2)]
    segCL = [A.f32(PS) for _ in range(2)]
    segCbL = [Buf(f"segC{i}") for i in range(2)]
    gcolsL = [A.f32(2) for _ in range(2)]
    gcolsbL = [Buf(f"gcols{i}") for i in range(2)]

    def lg_consume(tt, psum_ap, bankbufs):
        sl = tt % 2
        eE, eEb = eEL[sl], eEbL[sl]
        dd_t, dd_tb = dd_tL[sl], dd_tbL[sl]
        cc_t, cc_tb = cc_tL[sl], cc_tbL[sl]
        segD, segDb = segDL[sl], segDbL[sl]
        segC, segCb = segCL[sl], segCbL[sl]
        gcols, gcolsb = gcolsL[sl], gcolsbL[sl]
        m_c = gcols[:, 0:1]
        nm_c = gcols[:, 1:2]
        P.op("vector", lambda v, p=psum_ap, o=m_c: v.reduce_max(out=o, in_=p, axis=AX.X),
             reads=bankbufs, writes=[gcolsb])
        P.op("vector", lambda v: v.tensor_scalar_mul(out=nm_c, in0=m_c, scalar1=-1.0),
             reads=[gcolsb], writes=[gcolsb])
        P.op("scalar", lambda s, p=psum_ap: s.activation(out=eE, in_=p, func=AF.Exp,
                                                         bias=nm_c),
             reads=bankbufs, scalar_reads=[gcolsb], writes=[eEb])
        ev_d = eE[:].rearrange("p (n q) -> p n q", n=NE)
        P.op("vector", lambda v, o=segD[:, 0:NE], i_=ev_d:
             v.reduce_sum(out=o, in_=i_, axis=AX.X),
             reads=[eEb], writes=[segDb])
        P.op("scalar", lambda s: s.activation(out=segD[:, 0:NE], in_=segD[:, 0:NE],
                                              func=AF.Ln),
             reads=[segDb], writes=[segDb])
        P.op("scalar", lambda s: s.activation(out=segD[:, 0:NE], in_=segD[:, 0:NE],
                                              func=AF.Exp, scale=-1.0),
             reads=[segDb], writes=[segDb])
        P.op("vector", lambda v, o=dd_t[:].rearrange("p (n q) -> p n q", n=NE), i_=ev_d,
             b=segD[:, 0:NE].unsqueeze(2).broadcast_to([128, NE, PS]):
             v.tensor_tensor(out=o, in0=i_, in1=b, op=ALU.mult),
             reads=[eEb], scalar_reads=[segDb], writes=[dd_tb])
        ev_c = eE[:].rearrange("p (n q) -> p q n", n=NE)
        P.op("vector", lambda v, o=segC[:, 0:PS], i_=ev_c:
             v.reduce_sum(out=o, in_=i_, axis=AX.X),
             reads=[eEb], writes=[segCb])
        P.op("scalar", lambda s: s.activation(out=segC[:, 0:PS], in_=segC[:, 0:PS],
                                              func=AF.Ln),
             reads=[segCb], writes=[segCb])
        P.op("scalar", lambda s: s.activation(out=segC[:, 0:PS], in_=segC[:, 0:PS],
                                              func=AF.Exp, scale=-1.0),
             reads=[segCb], writes=[segCb])
        P.op("vector", lambda v, o=cc_t[:].rearrange("p (n q) -> p q n", n=NE), i_=ev_c,
             b=segC[:, 0:PS].unsqueeze(2).broadcast_to([128, PS, NE]):
             v.tensor_tensor(out=o, in0=i_, in1=b, op=ALU.mult),
             reads=[eEb], scalar_reads=[segCb], writes=[cc_tb])
        P.dma("sync", d["dd"][128 * tt:128 * (tt + 1), :], dd_t,
              reads=[dd_tb], writes=[ddb])
        P.dma("sync", d["cc"][128 * tt:128 * (tt + 1), :], cc_t,
              reads=[cc_tb], writes=[ccb2])

    emit_proj_formA(ctx, xn_chunks, xnbuf, phi, phib, ND, NP,
                    bias_row=pb_row, biasrowb=pbb, consume=lg_consume, group=2)
    A.release()
    P.barrier()

    A.mark()
    xn_tm = A.bf16(NT * D)
    xn_tmb = Buf("xntmG")
    for tt in range(NT):
        P.dma("sync", xn_tm[:, D * tt:D * (tt + 1)], d["xn"][128 * tt:128 * (tt + 1), :],
              writes=[xn_tmb])
    X_sb = A.f32(16 * D // 2)
    X_sb = A.f32(16 * D)
    X_sbb = Buf("XsbG")
    dslice = [A.bf16(512) for _ in range(2)]
    dsliceb = [Buf(f"dsl{i}") for i in range(2)]
    for ps in range(4):
        for tt in range(NT):
            sl = tt % 2
            P.dma("sync", dslice[sl], d["dd"][128 * tt:128 * (tt + 1), 512 * ps:512 * (ps + 1)],
                  reads=[ddb], writes=[dsliceb[sl]])
            def dfn(t, tt=tt, sl=sl):
                last = None
                for npc in range(4):
                    for half in range(2):
                        last = t.matmul(ctx["psum"][:, 512 * (2 * npc + half):512 * (2 * npc + half) + 512],
                                        dslice[sl][:, 128 * npc:128 * (npc + 1)],
                                        xn_tm[:, D * tt + 512 * half:D * tt + 512 * (half + 1)],
                                        start=(tt == 0), stop=(tt == NT - 1))
                return last
            P.op("tensor", dfn, reads=[dsliceb[sl], xn_tmb], writes=list(pbufs))
        for npc in range(4):
            npi = 4 * ps + npc
            P.op("vector", lambda v, npc=npc, npi=npi,
                 o=X_sb[:, D * npi:D * (npi + 1)]:
                 v.tensor_copy(out=o, in_=ctx["psum"][:, 1024 * npc:1024 * (npc + 1)]),
                 reads=[pbufs[2 * npc], pbufs[2 * npc + 1]], writes=[X_sbb])
    for npi in range(16):
        P.dma("sync", d["ccin"][128 * npi:128 * (npi + 1), :],
              X_sb[:, D * npi:D * (npi + 1)], reads=[X_sbb], writes=[ccinb])
    A.release()
    P.op("gpsimd", lambda g: g.collective_compute(
         "ReduceScatter", ALU.add, ins=[d["ccin"].opt()], outs=[d["ccout"].opt()],
         replica_groups=[[0, 1, 2, 3, 4, 5, 6, 7]]),
         reads=[ccinb], writes=[ccoutb])
    P.barrier()

    XtT = A.bf16(ND * 256)
    XtTb = Buf("XtT")
    Yt = A.bf16(16 * D)
    Ytb = Buf("Yt")
    cfm = [A.bf16(16 * 512) for _ in range(4)]
    cfmb = [Buf(f"cfm{i}") for i in range(4)]

    def cfm_load(tg):
        sl = tg % 4
        for npi in range(16):
            P.dma("sync", cfm[sl][:, 512 * npi:512 * (npi + 1)],
                  d["cc"][512 * tg:512 * (tg + 1), 128 * npi:128 * (npi + 1)],
                  reads=[ccb2], writes=[cfmb[sl]], transpose=True)
    A.mark()
    Xt = A.bf16(2 * D)
    Xtb = Buf("Xt")
    xrow = A.f32(D)
    xrowb = Buf("xrowG")
    cols = A.f32(16)
    colb = [Buf(f"gc{i}") for i in range(2)]
    dump = A.bf16(D)
    dumpb = Buf("gdump")
    for npi in range(2):
        sl = npi % 2
        cb = colb[sl]
        P.dma("sync", xrow, d["ccout"][128 * npi:128 * (npi + 1), :],
              reads=[ccoutb], writes=[xrowb])
        P.op("vector", lambda v: v.tensor_scalar_min(out=xrow, in0=xrow, scalar1=65000.0),
             reads=[xrowb], writes=[xrowb])
        P.op("vector", lambda v: v.tensor_scalar_max(out=xrow, in0=xrow, scalar1=-33000.0),
             reads=[xrowb], writes=[xrowb])
        s_c = cols[:, 5 * sl + 0:5 * sl + 1]
        ss_c = cols[:, 5 * sl + 1:5 * sl + 2]
        P.op("vector", lambda v, o=s_c: v.reduce_sum(out=o, in_=xrow, axis=AX.X),
             reads=[xrowb], writes=[cb])
        P.op("scalar", lambda s, o=dump, a=ss_c:
             s.activation(out=o, in_=xrow, func=AF.Square, accum_out=a),
             reads=[xrowb], writes=[dumpb, cb])
        rstd_c, nmr_c = emit_ln_cols(ctx, cols, sl, cb)
        P.op("vector", lambda v, s1=rstd_c, s2=nmr_c:
             v.tensor_scalar(out=xrow, in0=xrow, scalar1=s1, scalar2=s2,
                             op0=ALU.mult, op1=ALU.add),
             reads=[xrowb], scalar_reads=[cb], writes=[xrowb])
        P.op("scalar", lambda s, npi=npi, o=Xt[:, D * npi:D * (npi + 1)]:
             s.activation(out=o, in_=xrow, func=AF.Tanh),
             reads=[xrowb], writes=[Xtb])
    psbf = ctx["psum"].bitcast(BF16)
    for npi in range(2):
        for j in range(ND):
            pb = 5 + (npi * ND + j) % 2
            P.op("tensor", lambda t, npi=npi, j=j, pb=pb:
                 t.transpose(psbf[:, 1024 * pb:1024 * pb + 128],
                             Xt[:, D * npi + 128 * j:D * npi + 128 * (j + 1)],
                             ctx["ident"]),
                 reads=[Xtb, ctx["identb"]], writes=[pbufs[pb]])
            P.op("vector", lambda v, npi=npi, j=j, pb=pb,
                 o=XtT[:, 256 * j + 128 * npi:256 * j + 128 * (npi + 1)]:
                 v.tensor_copy(out=o, in_=psbf[:, 1024 * pb:1024 * pb + 128]),
                 reads=[pbufs[pb]], writes=[XtTb])
    A.release()
    P.barrier()

    A.mark()
    ewc = [A.bf16(D) for _ in range(4)]
    ewcb = [Buf(f"ewc{i}") for i in range(4)]
    ebrow = A.bf16(D, parts=1)
    ebrowb = Buf("ebrow")
    Ytl = A.bf16(2 * D)
    Ytlb = Buf("Ytl")
    ytinb = Buf("ytin")
    ytoutb = Buf("ytout")
    for e in range(8):
        for j in range(ND):
            sl = (e * ND + j) % 4
            P.dma("gpsimd", ewc[sl],
                  iw["G_ew"][e, 128 * j:128 * (j + 1), :], writes=[ewcb[sl]])
            def efn(t, e=e, j=j, sl=sl):
                last = None
                for half in range(2):
                    last = t.matmul(ctx["psum"][0:PS, 512 * (6 + half):512 * (6 + half) + 512],
                                    XtT[:, 256 * j + 32 * e:256 * j + 32 * (e + 1)],
                                    ewc[sl][:, 512 * half:512 * (half + 1)],
                                    start=(j == 0), stop=False)
                return last
            P.op("tensor", efn, reads=[XtTb, ewcb[sl]], writes=[pbufs[6], pbufs[7]])
        P.dma("gpsimd", ebrow, iw["G_eb"][e:e + 1, :], writes=[ebrowb])
        def ebfn(t, e=e):
            last = None
            for half in range(2):
                last = t.matmul(ctx["psum"][0:PS, 512 * (6 + half):512 * (6 + half) + 512],
                                ctx["ones_row"][:, 0:PS],
                                ebrow[:, 512 * half:512 * (half + 1)],
                                start=False, stop=True)
            return last
        P.op("tensor", ebfn, reads=[ebrowb, ctx["ones_rowb"]],
             writes=[pbufs[6], pbufs[7]])
        P.op("vector", lambda v, e=e,
             o=Ytl[32 * (e % 4):32 * (e % 4 + 1), D * (e // 4):D * (e // 4 + 1)]:
             v.tensor_copy(out=o, in_=ctx["psum"][0:PS, 512 * 6:512 * 6 + 1024]),
             reads=[pbufs[6], pbufs[7]], writes=[Ytlb])
    cfm_load(0)
    cfm_load(1)
    for i in range(2):
        P.dma("gpsimd", d["ytin"][128 * i:128 * (i + 1), :], Ytl[:, D * i:D * (i + 1)],
              reads=[Ytlb], writes=[ytinb])
    P.op("gpsimd", lambda g: g.collective_compute(
         "AllGather", ALU.bypass, ins=[d["ytin"].opt()], outs=[d["ytout"].opt()],
         replica_groups=[[0, 1, 2, 3, 4, 5, 6, 7]]),
         reads=[ytinb], writes=[ytoutb])
    for npi in range(16):
        P.dma("scalar", Yt[:, D * npi:D * (npi + 1)],
              d["ytout"][128 * npi:128 * (npi + 1), :],
              reads=[ytoutb], writes=[Ytb])
    A.release()
    P.barrier()

    A.mark()
    consume = emit_outer_ln_residual_consume(ctx, x_src_dram, x_dst_dram)
    for tg in range(8):
        sl = tg % 4
        if tg + 2 < 8:
            cfm_load(tg + 2)
        for tti in range(4):
            tt = 4 * tg + tti
            b0 = tti * 2
            def cfn(t, tti=tti, sl=sl, b0=b0):
                last = None
                for npi in range(16):
                    for half in range(2):
                        last = t.matmul(
                            ctx["psum"][:, 512 * (b0 + half):512 * (b0 + half) + 512],
                            cfm[sl][:, 512 * npi + 128 * tti:512 * npi + 128 * (tti + 1)],
                            Yt[:, D * npi + 512 * half:D * npi + 512 * (half + 1)],
                            start=(npi == 0), stop=(npi == 15))
                return last
            P.op("tensor", cfn, reads=[cfmb[sl], Ytb],
                 writes=[pbufs[b0], pbufs[b0 + 1]])
            consume(tt, ctx["psum"][:, 512 * b0:512 * b0 + D],
                    [pbufs[b0], pbufs[b0 + 1]])
    A.release()
    A.release()
    P.barrier()


def build_program(blocks=("A", "F", "B", "G"), debug_taps=()):
    import contextlib
    nc = bass.Bass("TRN2", target_bir_lowering=False, debug=False)
    stack = contextlib.ExitStack()

    iw = {}

    def ein(name, shape, dtype=F32):
        iw[name] = nc.dram_tensor(name, list(shape), dtype, kind="ExternalInput").ap()
        return iw[name]

    ein("x_in", (T, D))
    if "A" in blocks:
        ein("A_wq", (D, D)); ein("A_wk", (D, D)); ein("A_wv", (D, D)); ein("A_wo", (D, D))
        ein("A_bq_col", (128, ND)); ein("A_bk_col", (128, ND))
        ein("A_bv", (D,)); ein("A_bo", (D,))
    if "B" in blocks:
        ein("B_wq", (D, D)); ein("B_bq", (D,))
        ein("B_wk", (D, D)); ein("B_bk", (D,))
        ein("B_wv", (D, D)); ein("B_bv", (D,))
        ein("B_wo", (D, D)); ein("B_bo", (D,))
        ein("B_g_col", (128, ND)); ein("B_b_col", (128, ND))
    if "G" in blocks:
        ein("G_phi", (D, 2048)); ein("G_phib", (2048,))
        ein("G_ew", (8, D, D)); ein("G_eb", (8, D))
    if "F" in blocks:
        ein("F_qw", (D, 512)); ein("F_qb_col", (128, 4))
        ein("F_kw", (D, 512)); ein("F_kb_col", (128, 4))
        ein("F_vw", (D, 512)); ein("F_vb", (512,))
        ein("F_ow", (512, D)); ein("F_ob_col", (128, ND))
        ein("F_rw", (D, 8)); ein("F_rb", (8,))
        ein("F_w1", (8, D, 2048)); ein("F_b1_col", (8, 128, 16))
        ein("F_w2", (8, 2048, D)); ein("F_b2", (8, D))
    ein("c_ident", (128, 128))
    ein("c_ones_row", (1, 128))
    ein("c_onesf_row", (1, 128))
    ein("c_masks", (128, 2048))
    ein("c_ropecc", (128, T))
    ein("c_ropess", (128, T))
    ein("c_headsum2", (128, 2))
    ein("c_headsum4", (128, 4))
    ein("c_sel2", (2, 256))
    ein("c_epscol", (128, 1))
    ein("c_negc", (128, 1))
    ein("c_statpair", (128, 4))
    ein("c_identf", (128, 128))
    ein("c_onescol", (128, 1))
    ein("c_e8", (8, 1024))

    out = nc.dram_tensor("x_out", [T, D], F32, kind="ExternalOutput").ap()

    dram = {}
    for nm, shape, dt in [
        ("xn", (T, D), BF16), ("q", (D, T), BF16), ("k", (D, T), BF16),
        ("q0", (D, T), BF16),
        ("edbg", (4 * 128, 512), BF16), ("ddbg", (4, 512), F32),
        ("bdbg", (128, 16), F32), ("mdbg", (2, 16), F32),
        ("negd", (2, 8), F32), ("rrow", (1, 512), F32),
        ("negd2", (4, 4), F32), ("ao", (512, T), BF16),
        ("mla", (D, T), BF16), ("moe", (T, D), F32),
        ("qh", (T, D), BF16), ("kh", (T, D), BF16), ("vh", (T, D), BF16),
        ("z", (D, T), BF16), ("row2", (2, D), F32),
        ("dd", (T, 2048), BF16), ("cc", (T, 2048), BF16),
        ("denh", (16, T), F32), ("ralla", (16, T), F32), ("dsnap", (16, T), F32),
        ("ctdbg", (8, T), BF16), ("rldbg", (T, 8), F32), ("rp_dbg", (T, 24), F32),
        ("v", (T, 16 * 65), BF16), ("o", (D, T), BF16),
        ("x1", (T, D), F32), ("x2", (T, D), F32), ("x3", (T, D), F32),
    ]:
        dram[nm] = nc.dram_tensor("scr_" + nm, list(shape), dt, kind="Internal").ap()

    dram["ccin"] = nc.dram_tensor("ccin", [2048, D], F32, kind="Internal").ap()
    dram["ccout"] = nc.dram_tensor("ccout", [256, D], F32, kind="Internal").ap()
    dram["ytin"] = nc.dram_tensor("ytin", [256, D], BF16, kind="Internal").ap()
    dram["ytout"] = nc.dram_tensor("ytout", [2048, D], BF16, kind="Internal").ap()
    P = Prog(nc)
    ctx = {"nc": nc, "P": P, "dram": dram,
           "psum_bufs": [Buf(f"psum{i}") for i in range(8)],
           "negd2b": Buf("negd2"), "rrowb": Buf("rrow"), "negdb": Buf("negd")}

    with stack:
        arena = Arena(nc, stack)
        ctx["arena"] = arena
        psum = stack.enter_context(nc.psum_tensor("psum", [128, 4096], F32))
        ctx["psum"] = psum[:]

        def const_sb(name, cols, parts=128, f32=False):
            ap = arena.f32(cols, parts=parts) if f32 else arena.bf16(cols, parts=parts)
            b = Buf(name)
            q = "gpsimd" if not f32 else "sync"
            P.dma(q, ap, iw[name], writes=[b])
            b.const = True
            return ap, b

        ctx["ident"], ctx["identb"] = const_sb("c_ident", 128)
        ctx["ones_row"], ctx["ones_rowb"] = const_sb("c_ones_row", 128, parts=1)
        ctx["ones_col_f"], ctx["ones_col_fb"] = const_sb("c_onesf_row", 128, parts=1, f32=True)
        mm_ap = arena.bf16(4 * 512)
        mb = Buf("masks")
        P.dma("gpsimd", mm_ap, iw["c_masks"], writes=[mb])
        mb.const = True
        ctx["masks"] = [mm_ap[:, 512 * i:512 * (i + 1)] for i in range(4)]
        ctx["masksb"] = mb
        ctx["headsum2"], ctx["headsum2b"] = const_sb("c_headsum2", 2)
        ctx["headsum4"], ctx["headsum4b"] = const_sb("c_headsum4", 4)
        ctx["sel2"], ctx["sel2b"] = const_sb("c_sel2", 256, parts=2, f32=True)
        ctx["epscol"], ctx["epscolb"] = const_sb("c_epscol", 1, f32=True)
        ctx["negc"], ctx["negcb"] = const_sb("c_negc", 1, f32=True)
        ctx["statpair"], ctx["statpairb"] = const_sb("c_statpair", 4)
        ctx["identf"], ctx["identfb"] = const_sb("c_identf", 128, f32=True)
        ctx["ones_colb"], ctx["ones_colbb"] = const_sb("c_onescol", 1)
        ctx["e8"], ctx["e8b"] = const_sb("c_e8", 1024, parts=8)

        cur = iw["x_in"]
        scratch_x = [dram["x1"], dram["x2"], dram["x3"]]
        for bi, b in enumerate(blocks):
            dst = out if bi == len(blocks) - 1 else scratch_x[bi]
            nln = bi + 1 < len(blocks)
            sln = bi > 0
            if b == "A":
                emit_block_A(ctx, cur, dst, iw, next_ln=nln)
            elif b == "F":
                emit_block_F(ctx, cur, dst, iw, skip_ln=sln, next_ln=nln)
            elif b == "B":
                emit_block_B(ctx, cur, dst, iw, skip_ln=sln, next_ln=nln)
            elif b == "G":
                emit_block_G(ctx, cur, dst, iw, skip_ln=sln)
            else:
                raise NotImplementedError(b)
            cur = dst
        P.barrier()
        for nm in debug_taps:
            src = dram[nm]
            tap = nc.dram_tensor("tap_" + nm, list(src.shape), src.dtype,
                                 kind="ExternalOutput").ap()
            P.dma("sync", tap, src)
        P.barrier()
        P.finalize_and_emit(stack)
    return nc


def _bias_col(b):
    return np.ascontiguousarray(np.asarray(b, dtype=np.float32).reshape(-1, 128).T)


def _rope_perm_np(H=16, HD=64):
    p = []
    for h in range(H):
        p += [h * HD + 2 * i for i in range(HD // 2)]
        p += [h * HD + 2 * i + 1 for i in range(HD // 2)]
    return np.array(p, dtype=np.int64)


def make_consts():
    c = {}
    c["c_ident"] = np.eye(128, dtype=np.float32)
    c["c_ones_row"] = np.ones((1, 128), dtype=np.float32)
    c["c_onesf_row"] = np.ones((1, 128), dtype=np.float32)
    masks = []
    for off in (0, 128, 256, 384):
        kk = np.arange(128)[:, None]
        qq = np.arange(512)[None, :]
        masks.append(((kk + off) <= qq).astype(np.float32))
    c["c_masks"] = np.concatenate(masks, 1)
    theta = 10000.0 ** (-np.arange(0, 64, 2, dtype=np.float64) / 64)
    fr = np.outer(theta, np.arange(T, dtype=np.float64))
    sin, cos = np.sin(fr), np.cos(fr)
    cc_blk = np.concatenate([cos, cos], 0)
    ss_blk = np.concatenate([-sin, sin], 0)
    c["c_ropecc"] = np.tile(cc_blk, (2, 1)).astype(np.float32)
    c["c_ropess"] = np.tile(ss_blk, (2, 1)).astype(np.float32)
    hs2 = np.zeros((128, 2), dtype=np.float32)
    hs2[:64, 0] = 1.0
    hs2[64:, 1] = 1.0
    c["c_headsum2"] = hs2
    hs4 = np.zeros((128, 4), dtype=np.float32)
    for j in range(4):
        hs4[32 * j:32 * (j + 1), j] = 1.0
    c["c_headsum4"] = hs4
    sel = np.zeros((2, 256), dtype=np.float32)
    sel[0, :128] = 1.0
    sel[1, 128:] = 1.0
    c["c_sel2"] = sel
    c["c_epscol"] = np.full((128, 1), 1e-5, dtype=np.float32)
    c["c_negc"] = np.full((128, 1), -KSOFT_C, dtype=np.float32)
    sp = np.zeros((128, 4), dtype=np.float32)
    sp[:, 0] = 1.0 / D
    sp[:, 3] = 1.0 / D
    c["c_statpair"] = sp
    c["c_identf"] = np.eye(128, dtype=np.float32)
    c["c_onescol"] = np.ones((128, 1), dtype=np.float32)
    e8 = np.zeros((8, 1024), dtype=np.float32)
    for e in range(8):
        e8[e, 128 * e:128 * (e + 1)] = 1.0
    c["c_e8"] = e8
    return c


def prep_inputs_A(inputs):
    perm = _rope_perm_np()
    m = {}
    m["A_wq"] = np.ascontiguousarray(np.asarray(inputs["L0_attn_Wq"], np.float32).T[:, perm])
    m["A_wk"] = np.ascontiguousarray(np.asarray(inputs["L0_attn_Wk"], np.float32).T[:, perm])
    m["A_wv"] = np.ascontiguousarray(np.asarray(inputs["L0_attn_Wv"], np.float32).T)
    m["A_wo"] = np.ascontiguousarray(np.asarray(inputs["L0_attn_Wo"], np.float32).T)
    m["A_bq_col"] = _bias_col(np.asarray(inputs["L0_attn_bq"])[perm])
    m["A_bk_col"] = _bias_col(np.asarray(inputs["L0_attn_bk"])[perm])
    m["A_bv"] = np.ascontiguousarray(np.asarray(inputs["L0_attn_bv"], np.float32))
    m["A_bo"] = np.ascontiguousarray(np.asarray(inputs["L0_attn_bo"], np.float32))
    return m


def prep_inputs_F(inputs):
    m = {}
    qw = np.asarray(inputs["L0_ffn_q_w"], np.float32)
    used = np.concatenate([np.arange(64 * h, 64 * h + 32) for h in range(16)])
    m["F_qw"] = np.ascontiguousarray(qw[:, used])
    qb = np.asarray(inputs["L0_ffn_q_b"], np.float32)[used]
    m["F_qb_col"] = _bias_col(qb)
    m["F_kw"] = np.ascontiguousarray(np.asarray(inputs["L0_ffn_k_w"], np.float32))
    m["F_kb_col"] = _bias_col(np.asarray(inputs["L0_ffn_k_b"], np.float32))
    m["F_vw"] = np.ascontiguousarray(np.asarray(inputs["L0_ffn_v_w"], np.float32))
    m["F_vb"] = np.ascontiguousarray(np.asarray(inputs["L0_ffn_v_b"], np.float32))
    m["F_ow"] = np.ascontiguousarray(np.asarray(inputs["L0_ffn_o_w"], np.float32)[:512])
    m["F_ob_col"] = _bias_col(np.asarray(inputs["L0_ffn_o_b"], np.float32))
    m["F_rw"] = np.ascontiguousarray(np.asarray(inputs["L0_ffn_router_w"], np.float32))
    m["F_rb"] = np.ascontiguousarray(np.asarray(inputs["L0_ffn_router_b"], np.float32))
    m["F_w1"] = np.ascontiguousarray(np.asarray(inputs["L0_ffn_w1"], np.float32))
    b1 = np.asarray(inputs["L0_ffn_b1"], np.float32)
    m["F_b1_col"] = np.ascontiguousarray(b1.reshape(8, 16, 128).transpose(0, 2, 1))
    m["F_w2"] = np.ascontiguousarray(np.asarray(inputs["L0_ffn_w2"], np.float32))
    m["F_b2"] = np.ascontiguousarray(np.asarray(inputs["L0_ffn_b2"], np.float32))
    return m


def prep_inputs_B(inputs):
    m = {}
    for nm, w in (("wq", "Wq"), ("wk", "Wk"), ("wv", "Wv"), ("wo", "Wo")):
        m["B_" + nm] = np.ascontiguousarray(
            np.asarray(inputs["L1_attn_" + w], np.float32).T)
    for nm, b in (("bq", "bq"), ("bk", "bk"), ("bv", "bv"), ("bo", "bo")):
        m["B_" + nm] = np.ascontiguousarray(np.asarray(inputs["L1_attn_" + b], np.float32))
    m["B_g_col"] = _bias_col(np.asarray(inputs["L1_attn_ln_g"], np.float32))
    m["B_b_col"] = _bias_col(np.asarray(inputs["L1_attn_ln_b"], np.float32))
    return m


def prep_inputs_G(inputs):
    m = {}
    m["G_phi"] = np.ascontiguousarray(np.asarray(inputs["L1_ffn_phi_w"], np.float32).T)
    m["G_phib"] = np.ascontiguousarray(np.asarray(inputs["L1_ffn_phi_b"], np.float32))
    return m


def run_blocks(inputs, blocks=("A",), x_override=None, debug_taps=(), raw=False):
    nc = build_program(blocks=blocks, debug_taps=debug_taps)
    shared = make_consts()
    shared.update(prep_inputs_A(inputs))
    shared.update(prep_inputs_F(inputs))
    shared.update(prep_inputs_B(inputs))
    shared.update(prep_inputs_G(inputs))
    x = np.asarray(x_override if x_override is not None else inputs["L0_attn_x"],
                   dtype=np.float32)
    g_ew = np.ascontiguousarray(np.asarray(inputs["L1_ffn_expert_w"], np.float32))
    g_eb = np.ascontiguousarray(np.asarray(inputs["L1_ffn_expert_b"], np.float32))
    in_maps = []
    for b in range(8):
        im = dict(shared)
        im["x_in"] = np.ascontiguousarray(x[b])
        im["G_ew"] = g_ew[8 * b:8 * (b + 1)]
        im["G_eb"] = g_eb[8 * b:8 * (b + 1)]
        in_maps.append(im)
    res = run_bass_kernel_spmd(nc, in_maps, core_ids=list(range(8)))
    if raw:
        return res
    return np.stack([r["x_out"] for r in res.results], 0)


def kernel(**inputs):
    return run_blocks(inputs, blocks=("A", "F", "B", "G"))


if __name__ == "__main__":
    pass
